# version 04
# baseline (speedup 1.0000x reference)
import contextlib
import numpy as np
import concourse.bass as bass
import concourse.mybir as mybir
from concourse.bass_utils import run_bass_kernel_spmd

F32 = mybir.dt.float32
BF16 = mybir.dt.bfloat16
I32 = mybir.dt.int32
AF = mybir.ActivationFunctionType
ALU = mybir.AluOpType
AX = mybir.AxisListType

N, D, INNER, M, H, HD = 4096, 1024, 4096, 256, 16, 64
TILES, DC, NIB = 32, 8, 8

_SC = ["sum", "ssq", "mean", "m2", "nm2", "var", "lnv", "rstd", "mr", "nmr",
       "amax", "amc", "rc", "qs", "aqrec", "colg", "colu", "hamax", "hqs",
       "colt", "c1024inv", "cneg1", "cneghalf", "eps", "srt"]
SCOL = {n: i for i, n in enumerate(_SC)}


def build_program(dbg=(), phases=("p0", "p1a", "p1b", "p2a", "p3a", "p3b"),
                  n_tiles=None):
    n_t = n_tiles or TILES
    n_rows = n_t * 128
    nc = bass.Bass("TRN2", target_bir_lowering=False, dynamic_dma_scratch_size=512)

    x = nc.dram_tensor("x", [n_rows, D], F32, kind="ExternalInput").ap()
    wq = nc.dram_tensor("wq", [D, D], F32, kind="ExternalInput").ap()
    wk = nc.dram_tensor("wk", [D, D], F32, kind="ExternalInput").ap()
    wv = nc.dram_tensor("wv", [D, D], F32, kind="ExternalInput").ap()
    omt = nc.dram_tensor("omt", [HD, M], F32, kind="ExternalInput").ap()
    wgt = nc.dram_tensor("wgt", [D, INNER], F32, kind="ExternalInput").ap()
    wut = nc.dram_tensor("wut", [D, INNER], F32, kind="ExternalInput").ap()
    wdt = nc.dram_tensor("wdt", [INNER, D], F32, kind="ExternalInput").ap()
    y = nc.dram_tensor("y", [n_rows, D], F32, kind="ExternalOutput").ap()

    def scr(name, shape, dt):
        kind = "ExternalOutput" if name in dbg else "Internal"
        return nc.dram_tensor(name, shape, dt, kind=kind).ap()

    hq_scr = scr("hq_scr", [n_rows, INNER], BF16)
    y1_scr = scr("y1_scr", [n_rows, D], F32)
    y2_scr = scr("y2_scr", [n_rows, D], F32)
    qb_scr = scr("qb_scr", [n_rows, D], BF16)
    q2_scr = scr("q2_scr", [n_rows, D], BF16)
    gt_scr = scr("gt_scr", [D, INNER], BF16)
    ut_scr = scr("ut_scr", [D, INNER], BF16)
    dt_scr = scr("dt_scr", [INNER, D], BF16)

    sems, ctr = {}, {}

    def new_sem(nm):
        sems[nm] = nc.ctx.enter_context(nc.semaphore(nm))
        ctr[nm] = 0
        return nm

    for nm in ("sA", "sV", "sP", "sT"):
        new_sem(nm)
    WAITED = {e: {} for e in ("sync", "scalar", "vector", "gpsimd", "tensor")}
    store_sems = []

    def dma_pair(stream, store=False):
        pair = []
        for i in range(2):
            nm = new_sem(f"d{stream}{i}")
            pair.append(nm)
            if store:
                store_sems.append(nm)
        return pair

    def dma_one(stream, store=False):
        nm = new_sem(f"d{stream}")
        if store:
            store_sems.append(nm)
        return nm

    def inc(instr, sem, n=1):
        ctr[sem] += n
        instr.then_inc(sems[sem], n)
        return ctr[sem]

    def dma(eng, sem, out, in_):
        v = inc(eng.dma_start(out=out, in_=in_), sem, 16)
        return (sem, v)

    def wait(eng, ename, sem, val):
        if val and WAITED[ename].get(sem, 0) < val:
            eng.wait_ge(sems[sem], val)
            WAITED[ename][sem] = val

    def wait_t(eng, ename, tok):
        if tok is not None:
            wait(eng, ename, tok[0], tok[1])

    def wait_stores(eng, ename):
        for s in store_sems:
            wait(eng, ename, s, ctr[s])

    def dinc(eng, sem, n=1):
        return inc(eng.drain(), sem, n)

    cm = nc.ctx.enter_context
    ident_bf = cm(nc.sbuf_tensor("ident_bf", [128, 128], BF16))
    omdup_bf = cm(nc.sbuf_tensor("omdup_bf", [128, M], BF16))
    neghalf_bf = cm(nc.sbuf_tensor("neghalf_bf", [128, M], BF16))
    onesbf_col = cm(nc.sbuf_tensor("onesbf_col", [128, 1], BF16))
    onesf_col = cm(nc.sbuf_tensor("onesf_col", [128, 1], F32))
    onesrow = cm(nc.sbuf_tensor("onesrow", [1, 128], F32))
    scols = cm(nc.sbuf_tensor("scols", [128, 64], F32))
    wcols = cm(nc.sbuf_tensor("wcols", [128, 8], F32))
    sdh_col = cm(nc.sbuf_tensor("sdh_col", [128, 1], F32))
    hrecip_all = cm(nc.sbuf_tensor("hrecip_all", [128, TILES], F32))
    kvaug_bf = cm(nc.sbuf_tensor("kvaug_bf", [128, 2 * H * 65], BF16))
    sqjunk = cm(nc.sbuf_tensor("sqjunk", [128, D], BF16))
    sqjunk2 = cm(nc.sbuf_tensor("sqjunk2", [128, D], BF16))
    zcol = cm(nc.sbuf_tensor("zcol", [128, H], F32))
    pbk = [cm(nc.psum_tensor(f"pb{i}", [128, 512], F32)) for i in range(8)]

    def sc(nm, par=0):
        i = SCOL[nm] + 32 * (par % 2)
        return scols[:, i:i + 1]

    _uid = [0]

    def _nm(p):
        _uid[0] += 1
        return f"{p}{_uid[0]}"

    def finish():
        with nc.Block() as blk:
            @blk.gpsimd
            def _(g):
                wait_stores(g, "gpsimd")
        return nc

    p0 = contextlib.ExitStack()
    wtmp0 = [p0.enter_context(nc.sbuf_tensor(f"p0w{i}", [128, INNER], F32))
             for i in range(2)]
    accs = p0.enter_context(nc.sbuf_tensor("accs", [128, 28], F32))

    with nc.Block() as blk:
        @blk.vector
        def _(v):
            v.memset(scols[:], 0.0)
            v.memset(onesf_col[:], 1.0)
            v.memset(onesrow[:], 1.0)
            v.memset(neghalf_bf[:], -0.5)
            v.memset(kvaug_bf[:], 0.0)
            v.memset(zcol[:], 0.0)

        @blk.gpsimd
        def _(g):
            g.memset(onesbf_col[:], 1.0)
            g.memset(sqjunk[:], 0.0)
            g.memset(accs[:], 0.0)
            g.iota(ident_bf[:, :], [[1, 128]], base=0, channel_multiplier=-1,
                   allow_small_or_imprecise_dtypes=True)

    with nc.Block() as blk:
        @blk.vector
        def _(v):
            v.tensor_scalar(ident_bf[:, :], ident_bf[:, :], 0.0, None, ALU.is_equal)

        @blk.gpsimd
        def _(g):
            for par in range(2):
                g.memset(sc("c1024inv", par), 1.0 / 1024.0)
                g.memset(sc("cneg1", par), -1.0)
                g.memset(sc("cneghalf", par), -0.5)
                g.memset(sc("eps", par), 1e-5)

    chunks = []
    for mi, w in enumerate((wgt, wut)):
        for i in range(8):
            chunks.append((mi, i, w[i * 128:(i + 1) * 128, :]))
    for i in range(8):
        chunks.append((2, i, wdt[i * 512:(i + 1) * 512, :]
                       .rearrange("(p r) n -> p (r n)", p=128)))
    sp0 = dma_pair("p0w")
    d0, v0base = [], ctr["sV"]
    with nc.Block() as blk:
        @blk.sync
        def _(sy):
            for j, (mi, i, ap) in enumerate(chunks):
                if j >= 2:
                    wait(sy, "sync", "sV", v0base + j - 1)
                d0.append(dma(sy, sp0[j % 2], wtmp0[j % 2][:], ap))

        @blk.vector
        def _(v):
            for j, (mi, i, ap) in enumerate(chunks):
                wait_t(v, "vector", d0[j])
                inc(v.reduce_sum(accs[:, mi * 8 + i:mi * 8 + i + 1],
                                 wtmp0[j % 2][:], AX.X,
                                 apply_absolute_value=True), "sV")

    v0tok = [None]
    with nc.Block() as blk:
        @blk.vector
        def _(v):
            v.drain()
            for mi in range(3):
                v.reduce_sum(accs[:, 24 + mi:25 + mi], accs[:, mi * 8:(mi + 1) * 8],
                             AX.X)
            v0tok[0] = dinc(v, "sV")

        @blk.tensor
        def _(t):
            wait(t, "tensor", "sV", v0tok[0])
            last = None
            for mi in range(3):
                last = t.matmul(pbk[0][0:1, mi:mi + 1], accs[:, 24 + mi:25 + mi],
                                onesf_col[:, :], start=(mi == 0), stop=(mi == 2))
            inc(last, "sT")

    t0a = ctr["sT"]
    with nc.Block() as blk:
        @blk.vector
        def _(v):
            wait(v, "vector", "sT", t0a)
            v.tensor_copy(accs[0:1, 0:3], pbk[0][0:1, 0:3])
            v.drain()
            v.tensor_scalar(accs[0:1, 4:7], accs[0:1, 0:3], 1.0 / 4194304.0,
                            1e-5, ALU.mult, ALU.max)
            v.drain()
            v.reciprocal(accs[0:1, 8:11], accs[0:1, 4:7])
            v0tok[0] = dinc(v, "sV")

        @blk.tensor
        def _(t):
            wait(t, "tensor", "sV", v0tok[0])
            last = None
            for j in range(6):
                src = accs[0:1, 4 + j:5 + j] if j < 3 else \
                    accs[0:1, 8 + j - 3:9 + j - 3]
                last = t.matmul(pbk[1][:, j:j + 1], onesrow[0:1, :], src,
                                start=(j == 0), stop=(j == 5))
            inc(last, "sT")

    t0b = ctr["sT"]
    with nc.Block() as blk:
        @blk.vector
        def _(v):
            wait(v, "vector", "sT", t0b)
            v.tensor_copy(wcols[:, 0:6], pbk[1][:, 0:6])
            v.drain()
            v.tensor_scalar(sdh_col[:, :], wcols[:, 2:3], 0.5, None, ALU.mult)
            v.drain()
    p0.close()

    def emit_stats_A(s, src_f, lnf_out, v_var_tok, v_nmr_tok, par=0):
        s.activation(sqjunk[:, :], src_f, AF.Copy, accum_out=sc("sum", par))
        s.activation(sqjunk2[:, :], src_f, AF.Square, accum_out=sc("ssq", par))
        dinc(s, "sA")
        wait(s, "scalar", "sV", v_var_tok)
        s.activation(sc("srt", par), sc("var", par), AF.Sqrt, bias=sc("eps", par))
        dinc(s, "sA")
        wait(s, "scalar", "sV", v_nmr_tok)
        s.activation(lnf_out, src_f, AF.Identity, scale=sc("rstd", par),
                     bias=sc("nmr", par))
        return dinc(s, "sA")

    def emit_stats_V(v, a_acc_tok, a_srt_tok, par=0):
        wait(v, "vector", "sA", a_acc_tok)
        v.tensor_scalar(sc("mean", par), sc("sum", par), 1.0 / 1024.0, None,
                        ALU.mult)
        v.drain()
        v.tensor_tensor(sc("m2", par), sc("mean", par), sc("mean", par),
                        ALU.mult)
        v.drain()
        v.scalar_tensor_tensor(sc("var", par), sc("ssq", par), 1.0 / 1024.0,
                               sc("m2", par), ALU.mult, ALU.subtract)
        t1 = dinc(v, "sV")
        wait(v, "vector", "sA", a_srt_tok)
        v.reciprocal(sc("rstd", par), sc("srt", par))
        v.drain()
        v.tensor_scalar(sc("nmr", par), sc("mean", par), sc("rstd", par), -1.0,
                        ALU.mult, ALU.mult)
        t2 = dinc(v, "sV")
        return t1, t2

    def emit_quant_v(v, src_f, aqi_out, rec_ap, colgu, par=0):
        v.reduce_max(sc("amax", par), src_f, AX.X, apply_absolute_value=True)
        v.drain()
        v.tensor_scalar(sc("amc", par), sc("amax", par), sc("eps", par), None,
                        ALU.max)
        v.drain()
        v.reciprocal(sc("rc", par), sc("amc", par))
        v.drain()
        v.tensor_scalar(sc("qs", par), sc("rc", par), 127.0, None, ALU.mult)
        v.drain()
        v.reciprocal(rec_ap, sc("qs", par))
        v.drain()
        if colgu:
            v.tensor_scalar(colgu[0], rec_ap, wcols[:, 0:1], None, ALU.mult)
            v.tensor_scalar(colgu[1], rec_ap, wcols[:, 1:2], None, ALU.mult)
        v.tensor_scalar(aqi_out, src_f, sc("qs", par), None, ALU.mult)
        return dinc(v, "sV")

    def emit_transposes(t, src_b, banks, n):
        last = None
        for c in range(n):
            bank = banks[(c // 4) % len(banks)]
            last = t.matmul(bank[:, (c % 4) * 128:(c % 4) * 128 + 128],
                            src_b[:, c * 128:(c + 1) * 128], ident_bf[:, :],
                            start=(c % 4 == 0), stop=(c % 4 == 3))
        return last

    def emit_wquant(wsrc_list, dst_sb_list, scr_list, wsi_list, nchunk, ccols):
        est = contextlib.ExitStack()
        wtmp = [est.enter_context(nc.sbuf_tensor(_nm("wqw"), [128, ccols], F32))
                for _ in range(2)]
        i32t = [est.enter_context(nc.sbuf_tensor(_nm("wqi"), [128, ccols], I32))
                for _ in range(2)]
        jobs = []
        for (ws, db, sd, wsi) in zip(wsrc_list, dst_sb_list, scr_list, wsi_list):
            for i in range(nchunk):
                jobs.append((ws, db, sd, wsi, i))
        spw = dma_pair(_nm("wql"))
        sps = dma_pair(_nm("wqs"), store=True)
        dt_ = []
        vbase = ctr["sV"]
        with nc.Block() as blk:
            @blk.sync
            def _(sy):
                wait_stores(sy, "sync")
                for j, (ws, db, sd, wsi, i) in enumerate(jobs):
                    if j >= 2:
                        wait(sy, "sync", "sV", vbase + 2 * (j - 1))
                    dt_.append(dma(sy, spw[j % 2], wtmp[j % 2][:],
                                   ws[i * 128:(i + 1) * 128, :]))

            @blk.scalar
            def _(s):
                for j, (ws, db, sd, wsi, i) in enumerate(jobs):
                    wait(s, "scalar", "sV", vbase + 2 * (j + 1))
                    dma(s, sps[j % 2], sd[i * 128:(i + 1) * 128, :],
                        db[:, i * ccols:(i + 1) * ccols])

            @blk.vector
            def _(v):
                for j, (ws, db, sd, wsi, i) in enumerate(jobs):
                    wait_t(v, "vector", dt_[j])
                    v.tensor_scalar(i32t[j % 2][:], wtmp[j % 2][:],
                                    wcols[:, 3 + wsi:4 + wsi], None, ALU.mult)
                    dinc(v, "sV")
                    v.tensor_scalar(db[:, i * ccols:(i + 1) * ccols],
                                    i32t[j % 2][:], -1.0, 1.0, ALU.max, ALU.min)
                    dinc(v, "sV")
        est.close()

    if "p1a" not in phases:
        return finish()
    p1 = contextlib.ExitStack()
    wg_sb = p1.enter_context(nc.sbuf_tensor("wg_sb", [128, DC * INNER], BF16))
    wu_sb = p1.enter_context(nc.sbuf_tensor("wu_sb", [128, DC * INNER], BF16))
    emit_wquant([wgt, wut], [wg_sb, wu_sb], [gt_scr, ut_scr], [0, 1], 8, INNER)

    xa = [p1.enter_context(nc.sbuf_tensor(f"xa{i}", [128, D], F32))
          for i in range(2)]
    lnf = [p1.enter_context(nc.sbuf_tensor(f"lnf{i}", [128, D], F32))
           for i in range(2)]
    aqi = p1.enter_context(nc.sbuf_tensor("aqi", [128, D], I32))
    aqb = [p1.enter_context(nc.sbuf_tensor(f"aqb{i}", [128, D], BF16))
           for i in range(2)]
    aqT = [p1.enter_context(nc.sbuf_tensor(f"aqTs{i}", [128, D], BF16))
           for i in range(4)]
    qcols = p1.enter_context(nc.sbuf_tensor("qcols", [128, 6], F32))

    def qc(kind, slot):
        base = 0 if kind == "g" else 3
        return qcols[:, base + slot:base + slot + 1]
    silu_sb = [p1.enter_context(nc.sbuf_tensor(f"silu{i}", [128, 512], F32))
               for i in range(3)]
    h_sb = p1.enter_context(nc.sbuf_tensor("h_sb", [128, INNER], F32))
    hqi = p1.enter_context(nc.sbuf_tensor("hqi", [128, 2048], I32))
    hqb = [p1.enter_context(nc.sbuf_tensor(f"hqb{i}", [128, INNER], BF16))
           for i in range(2)]

    sxin = dma_pair("xin")
    sshq = dma_pair("sthq", store=True)
    tok_x = [None] * n_t
    stok_hq = [None] * n_t

    A0, V0, T0 = ctr["sA"], ctr["sV"], ctr["sT"]
    AT, VT, TT, GU = {}, {}, {}, {}
    _ca, _cv, _ct = [A0], [V0], [T0]

    def _fa(t):
        for j in (1, 2, 3):
            _ca[0] += 1
            AT[("s", t, j)] = _ca[0]

    def _ea(t):
        _ca[0] += 1
        AT[("e", t)] = _ca[0]

    def _fv(t):
        for j in (1, 2, 3, 4):
            _cv[0] += 1
            VT[("f", t, j)] = _cv[0]

    DP = 3
    for _k in range(DP):
        _fa(_k)
    for _k in range(DP):
        _ea(_k)
    for _t in range(n_t):
        for _ib in range(NIB):
            _ca[0] += 1
            AT[("m", _t, _ib)] = _ca[0]
        for _j in (1, 2):
            _ca[0] += 1
            AT[("h", _t, _j)] = _ca[0]
        if _t + DP < n_t:
            _fa(_t + DP)
            _ea(_t + DP)
    for _k in range(DP):
        _fv(_k)
    for _t in range(n_t):
        for _ib in range(NIB):
            _cv[0] += 1
            VT[("m", _t, _ib)] = _cv[0]
        for _j in (1, 2):
            _cv[0] += 1
            VT[("h", _t, _j)] = _cv[0]
        if _t + DP < n_t:
            _fv(_t + DP)
    for _t in range(DP):
        _ct[0] += 1
        TT[_t] = _ct[0]
    for _t in range(n_t):
        for _ib in range(NIB):
            _ct[0] += 1
            GU[(_t, _ib)] = _ct[0]
        if _t + DP < n_t:
            _ct[0] += 1
            TT[_t + DP] = _ct[0]
    GBANK = [(pbk[2], pbk[3]), (pbk[4], pbk[5]), (pbk[6], pbk[7])]

    with nc.Block() as blk:
        @blk.sync
        def _(sy):
            for t in range(n_t):
                if t >= 2:
                    wait(sy, "sync", "sA", AT[("s", t - 2, 3)])
                tok_x[t] = dma(sy, sxin[t % 2], xa[t % 2][:],
                               x[t * 128:(t + 1) * 128, :])

        @blk.scalar
        def _(s):
            def front_a(t):
                par = t % 2
                wait_t(s, "scalar", tok_x[t])
                if t >= 2:
                    wait(s, "scalar", "sV", VT[("f", t - 2, 3)])
                tok = emit_stats_A(s, xa[par][:], lnf[par][:],
                                   VT[("f", t, 1)], VT[("f", t, 2)], par)
                assert tok == AT[("s", t, 3)]

            def evac_a(t):
                wait(s, "scalar", "sT", TT[t])
                if t >= 4:
                    wait(s, "scalar", "sT", GU[(t - 4, 7)])
                s.copy(aqT[t % 4][:, 0:512], pbk[0][:, :])
                s.copy(aqT[t % 4][:, 512:1024], pbk[1][:, :])
                tok = dinc(s, "sA")
                assert tok == AT[("e", t)]

            for k in range(DP):
                front_a(k)
            for k in range(DP):
                evac_a(k)
            for t in range(n_t):
                par = t % 2
                wait(s, "scalar", "sV", VT[("f", t, 3)])
                for ib in range(NIB):
                    gg = 8 * t + ib
                    wait(s, "scalar", "sT", GU[(t, ib)])
                    if gg >= 3:
                        tt, ii = divmod(gg - 3, 8)
                        wait(s, "scalar", "sV", VT[("m", tt, ii)])
                    tok = inc(s.activation(silu_sb[gg % 3][:],
                                           GBANK[gg % 3][0][:, :], AF.Silu,
                                           scale=qc("g", t % 3)), "sA")
                    assert tok == AT[("m", t, ib)]
                wait(s, "scalar", "sV", VT[("h", t, 1)])
                if t >= 2:
                    wait_t(s, "scalar", stok_hq[t - 2])
                tok = inc(s.activation(hqb[par][:, 0:2048], hqi[:],
                                       AF.Identity), "sA")
                assert tok == AT[("h", t, 1)]
                wait(s, "scalar", "sV", VT[("h", t, 2)])
                s.activation(hqb[par][:, 2048:4096], hqi[:], AF.Identity)
                tok = dinc(s, "sA")
                assert tok == AT[("h", t, 2)]
                stok_hq[t] = dma(s, sshq[par],
                                 hq_scr[t * 128:(t + 1) * 128, :], hqb[par][:])
                if t + DP < n_t:
                    front_a(t + DP)
                    evac_a(t + DP)

        @blk.vector
        def _(v):
            def front_v(t):
                par = t % 2
                t1, t2 = emit_stats_V(v, AT[("s", t, 1)], AT[("s", t, 2)], par)
                assert (t1, t2) == (VT[("f", t, 1)], VT[("f", t, 2)])
                wait(v, "vector", "sA", AT[("s", t, 3)])
                tok = emit_quant_v(v, lnf[par][:], aqi[:], sc("aqrec", par),
                                   (qc("g", t % 3), qc("u", t % 3)), par)
                assert tok == VT[("f", t, 3)]
                if t >= 2:
                    wait(v, "vector", "sT", TT[t - 2])
                v.tensor_copy(aqb[par][:], aqi[:])
                tok = dinc(v, "sV")
                assert tok == VT[("f", t, 4)]

            for k in range(DP):
                front_v(k)
            for t in range(n_t):
                par = t % 2
                for ib in range(NIB):
                    gg = 8 * t + ib
                    wait(v, "vector", "sT", GU[(t, ib)])
                    wait(v, "vector", "sA", AT[("m", t, ib)])
                    tok = inc(v.scalar_tensor_tensor(
                        h_sb[:, ib * 512:(ib + 1) * 512],
                        GBANK[gg % 3][1][:, :], qc("u", t % 3),
                        silu_sb[gg % 3][:], ALU.mult, ALU.mult), "sV")
                    assert tok == VT[("m", t, ib)]
                v.drain()
                v.reduce_max(sc("hamax", par), h_sb[:], AX.X,
                             apply_absolute_value=True)
                v.drain()
                v.tensor_scalar(sc("amc", par), sc("hamax", par),
                                sc("eps", par), None, ALU.max)
                v.drain()
                v.reciprocal(sc("rc", par), sc("amc", par))
                v.drain()
                v.tensor_scalar(sc("hqs", par), sc("rc", par), 127.0, None,
                                ALU.mult)
                v.drain()
                v.reciprocal(hrecip_all[:, t:t + 1], sc("hqs", par))
                v.drain()
                if t >= 1:
                    wait(v, "vector", "sA", AT[("h", t - 1, 2)])
                v.tensor_scalar(hqi[:], h_sb[:, 0:2048], sc("hqs", par), None,
                                ALU.mult)
                tok = dinc(v, "sV")
                assert tok == VT[("h", t, 1)]
                wait(v, "vector", "sA", AT[("h", t, 1)])
                v.tensor_scalar(hqi[:], h_sb[:, 2048:4096], sc("hqs", par),
                                None, ALU.mult)
                tok = dinc(v, "sV")
                assert tok == VT[("h", t, 2)]
                if t + DP < n_t:
                    front_v(t + DP)

        @blk.tensor
        def _(te):
            def trans(t):
                wait(te, "tensor", "sV", VT[("f", t, 4)])
                if t >= 1:
                    wait(te, "tensor", "sA", AT[("e", t - 1)])
                tok = inc(emit_transposes(te, aqb[t % 2][:, :],
                                          [pbk[0], pbk[1]], 8), "sT")
                assert tok == TT[t]

            for k in range(DP):
                trans(k)
            for t in range(n_t):
                wait(te, "tensor", "sA", AT[("e", t)])
                for ib in range(NIB):
                    gg = 8 * t + ib
                    if gg >= 3:
                        tt, ii = divmod(gg - 3, 8)
                        wait(te, "tensor", "sV", VT[("m", tt, ii)])
                    gb, ub = GBANK[gg % 3]
                    for c in range(DC):
                        te.matmul(gb[:, :],
                                  aqT[t % 4][:, c * 128:(c + 1) * 128],
                                  wg_sb[:, c * INNER + ib * 512:
                                        c * INNER + ib * 512 + 512],
                                  start=(c == 0), stop=(c == DC - 1))
                    last = None
                    for c in range(DC):
                        last = te.matmul(ub[:, :],
                                         aqT[t % 4][:, c * 128:(c + 1) * 128],
                                         wu_sb[:, c * INNER + ib * 512:
                                               c * INNER + ib * 512 + 512],
                                         start=(c == 0), stop=(c == DC - 1))
                    tok = inc(last, "sT")
                    assert tok == GU[(t, ib)]
                if t + DP < n_t:
                    trans(t + DP)

    assert ctr["sA"] == A0 + 14 * n_t
    assert ctr["sV"] == V0 + 14 * n_t
    assert ctr["sT"] == T0 + 9 * n_t
    p1.close()

    def run_down_phase(preamble, hq_src, res_src, out_dst, final):
        pd = contextlib.ExitStack()
        wd_sb = pd.enter_context(nc.sbuf_tensor(_nm("wd_sb"),
                                                [128, 32 * 1024], BF16))
        preamble(wd_sb)
        hqa = [pd.enter_context(nc.sbuf_tensor(_nm("hqa"), [128, INNER], BF16))
               for _ in range(2)]
        resa = [pd.enter_context(nc.sbuf_tensor(_nm("resa"), [128, D], F32))
                for _ in range(2)]
        hqT = [pd.enter_context(nc.sbuf_tensor(_nm("hqT"), [128, INNER], BF16))
               for _ in range(2)]
        y3sb = [pd.enter_context(nc.sbuf_tensor(_nm("y3sb"), [128, D], F32))
                for _ in range(2)]
        sdlh = dma_pair(_nm("dnlh"))
        sdlr = dma_pair(_nm("dnlr"))
        sds = dma_pair(_nm("dns"), store=True)
        tok_hq = [None] * n_t
        tok_res = [None] * n_t
        stok = [None] * n_t

        T0d, A0d, V0d = ctr["sT"], ctr["sA"], ctr["sV"]
        _pe_off = [1, 2, 3, 4, 6, 8, 10, 12]
        _dg_off = [5, 7, 9, 11, 13, 14, 15, 16]

        def TG(t, g):
            return T0d + 16 * t + _pe_off[g]

        def DG(t, dg):
            return T0d + 16 * t + _dg_off[dg]

        def EV(t, g):
            return A0d + 8 * t + 1 + g

        def Y3(t):
            return V0d + t + 1

        with nc.Block() as blk:
            @blk.sync
            def _(sy):
                wait_stores(sy, "sync")
                for t in range(n_t):
                    if t >= 2:
                        wait(sy, "sync", "sT", TG(t - 2, 7))
                        wait(sy, "sync", "sV", Y3(t - 2))
                    tok_hq[t] = dma(sy, sdlh[t % 2], hqa[t % 2][:],
                                    hq_src[t * 128:(t + 1) * 128, :])
                    tok_res[t] = dma(sy, sdlr[t % 2], resa[t % 2][:],
                                     res_src[t * 128:(t + 1) * 128, :])

            @blk.scalar
            def _(s):
                for t in range(n_t):
                    for g in range(8):
                        wait(s, "scalar", "sT", TG(t, g))
                        if t >= 2:
                            wait(s, "scalar", "sT", DG(t - 2, g))
                        s.copy(hqT[t % 2][:, g * 512:(g + 1) * 512],
                               pbk[g % 4][:, :])
                        tok = dinc(s, "sA")
                        assert tok == EV(t, g)
                    wait(s, "scalar", "sV", Y3(t))
                    stok[t] = dma(s, sds[t % 2],
                                  out_dst[t * 128:(t + 1) * 128, :],
                                  y3sb[t % 2][:])

            @blk.vector
            def _(v):
                for t in range(n_t):
                    dnb = (pbk[4], pbk[5]) if t % 2 == 0 else (pbk[6], pbk[7])
                    wait(v, "vector", "sT", DG(t, 7))
                    wait_t(v, "vector", tok_res[t])
                    if t >= 2:
                        wait_t(v, "vector", stok[t - 2])
                    v.tensor_scalar(sc("colt"), hrecip_all[:, t:t + 1],
                                    sdh_col[:, :], None, ALU.mult)
                    v.drain()
                    v.scalar_tensor_tensor(y3sb[t % 2][:, 0:512], dnb[0][:, :],
                                           sc("colt"), resa[t % 2][:, 0:512],
                                           ALU.mult, ALU.add)
                    v.scalar_tensor_tensor(y3sb[t % 2][:, 512:1024],
                                           dnb[1][:, :], sc("colt"),
                                           resa[t % 2][:, 512:1024],
                                           ALU.mult, ALU.add)
                    tok = inc(v.drain(), "sV")
                    assert tok == Y3(t)

            @blk.tensor
            def _(te):
                for t in range(n_t):
                    dnb = (pbk[4], pbk[5]) if t % 2 == 0 else (pbk[6], pbk[7])

                    def tgrp(g, t=t):
                        if t >= 1:
                            wait(te, "tensor", "sA", EV(t - 1, 4 + (g % 4)))
                        last = None
                        for k in range(4):
                            c = g * 4 + k
                            last = te.matmul(
                                pbk[g % 4][:, k * 128:(k + 1) * 128],
                                hqa[t % 2][:, c * 128:(c + 1) * 128],
                                ident_bf[:, :], start=(k == 0), stop=(k == 3))
                        tok = inc(last, "sT")
                        assert tok == TG(t, g)

                    def dgrp(dg, t=t, dnb=dnb):
                        wait(te, "tensor", "sA", EV(t, dg))
                        if dg == 0 and t >= 2:
                            wait(te, "tensor", "sV", Y3(t - 2))
                        last = None
                        for k in range(4):
                            ci = dg * 4 + k
                            te.matmul(dnb[0][:, :],
                                      hqT[t % 2][:, ci * 128:(ci + 1) * 128],
                                      wd_sb[:, ci * 1024:ci * 1024 + 512],
                                      start=(ci == 0), stop=(ci == 31))
                            last = te.matmul(
                                dnb[1][:, :],
                                hqT[t % 2][:, ci * 128:(ci + 1) * 128],
                                wd_sb[:, ci * 1024 + 512:ci * 1024 + 1024],
                                start=(ci == 0), stop=(ci == 31))
                        tok = inc(last, "sT")
                        assert tok == DG(t, dg)

                    wait_t(te, "tensor", tok_hq[t])
                    for g in range(4):
                        tgrp(g)
                    for g in range(4):
                        dgrp(g)
                        tgrp(4 + g)
                    for dg in range(4, 8):
                        dgrp(dg)

            assert ctr["sT"] == T0d + 16 * n_t
            assert ctr["sA"] == A0d + 8 * n_t
            assert ctr["sV"] == V0d + n_t

        pd.close()

    def p1b_pre(wd_sb):
        emit_wquant([wdt], [wd_sb], [dt_scr], [2], 32, 1024)

    if "p1b" not in phases:
        return finish()
    run_down_phase(p1b_pre, hq_scr, x, y1_scr, final=False)

    if "p2a" not in phases:
        return finish()
    p2 = contextlib.ExitStack()
    wq_sb = p2.enter_context(nc.sbuf_tensor("wq_sb", [128, DC * D], BF16))
    wk_sb = p2.enter_context(nc.sbuf_tensor("wk_sb", [128, DC * D], BF16))
    wv_sb = p2.enter_context(nc.sbuf_tensor("wv_sb", [128, DC * D], BF16))

    pre = contextlib.ExitStack()
    wtmp2 = [pre.enter_context(nc.sbuf_tensor(_nm("w2"), [128, D], F32))
             for _ in range(2)]
    jobs2 = [(w, dst, i) for (w, dst) in ((wq, wq_sb), (wk, wk_sb), (wv, wv_sb))
             for i in range(8)]
    sp2w = dma_pair("p2w")
    d2, v2base = [], ctr["sV"]
    with nc.Block() as blk:
        @blk.sync
        def _(sy):
            wait_stores(sy, "sync")
            for j, (w, dst, i) in enumerate(jobs2):
                if j >= 2:
                    wait(sy, "sync", "sV", v2base + j - 1)
                d2.append(dma(sy, sp2w[j % 2], wtmp2[j % 2][:],
                              w[i * 128:(i + 1) * 128, :]))

        @blk.vector
        def _(v):
            for j, (w, dst, i) in enumerate(jobs2):
                wait_t(v, "vector", d2[j])
                v.tensor_copy(dst[:, i * D:(i + 1) * D], wtmp2[j % 2][:])
                dinc(v, "sV")

    sp2o = dma_one("p2o")
    with nc.Block() as blk:
        @blk.sync
        def _(sy):
            dma(sy, sp2o, wtmp2[0][0:64, 0:M], omt)
            dma(sy, sp2o, wtmp2[0][64:128, 0:M], omt)

        @blk.vector
        def _(v):
            wait(v, "vector", sp2o, ctr[sp2o])
            v.tensor_copy(omdup_bf[:, :], wtmp2[0][:, 0:M])
            v.drain()
    pre.close()

    ya = [p2.enter_context(nc.sbuf_tensor(f"ya{i}", [128, D], F32))
          for i in range(2)]
    lnb_sb = p2.enter_context(nc.sbuf_tensor("lnb_sb", [128, D], BF16))
    lnT_sb = p2.enter_context(nc.sbuf_tensor("lnT_sb", [128, D], BF16))
    k_f = p2.enter_context(nc.sbuf_tensor("k_f", [128, D], F32))
    q_f = p2.enter_context(nc.sbuf_tensor("q_f", [128, D], F32))
    k_b = p2.enter_context(nc.sbuf_tensor("k_b", [128, D], BF16))
    k2_b = p2.enter_context(nc.sbuf_tensor("k2_b", [128, D], BF16))
    v_b = p2.enter_context(nc.sbuf_tensor("v_b", [128, D], BF16))
    pk_b = p2.enter_context(nc.sbuf_tensor("pk_b", [128, H * M], BF16))
    qb_sb = [p2.enter_context(nc.sbuf_tensor(f"qb{i}", [128, D], BF16))
             for i in range(2)]
    q2_sb = [p2.enter_context(nc.sbuf_tensor(f"q2{i}", [128, D], BF16))
             for i in range(2)]

    sy1in = dma_pair("y1in")
    sstq = dma_pair("stq", store=True)
    sstq2 = dma_pair("stq2", store=True)
    tok_y1 = [None] * (n_t + 1)
    stok_q = [None] * n_t
    stok_q2 = [None] * n_t

    with nc.Block() as blk:
        @blk.sync
        def _(sy):
            wait_stores(sy, "sync")
            tok_y1[0] = dma(sy, sy1in[0], ya[0][:], y1_scr[0:128, :])

    for t in range(n_t):
        tb, ab, vb = ctr["sT"], ctr["sA"], ctr["sV"]
        with nc.Block() as blk:
            @blk.sync
            def _(sy, t=t):
                if t + 1 < n_t:
                    tok_y1[t + 1] = dma(sy, sy1in[(t + 1) % 2],
                                        ya[(t + 1) % 2][:],
                                        y1_scr[(t + 1) * 128:(t + 2) * 128, :])

            @blk.scalar
            def _(s, t=t, tb=tb, vb=vb):
                wait_t(s, "scalar", tok_y1[t])
                emit_stats_A(s, ya[t % 2][:], lnb_sb[:], vb + 1, vb + 2)
                wait(s, "scalar", "sT", tb + 1)
                s.copy(lnT_sb[:, 0:512], pbk[5][:, :])
                s.copy(lnT_sb[:, 512:1024], pbk[6][:, :])
                dinc(s, "sA")
                wait(s, "scalar", "sT", tb + 2)
                s.copy(k_f[:, 0:512], pbk[5][:, :])
                s.copy(k_f[:, 512:1024], pbk[6][:, :])
                dinc(s, "sA")
                s.activation(k2_b[:, :], k_f[:, :], AF.Square)
                dinc(s, "sA")
                wait(s, "scalar", "sT", tb + 4)
                s.copy(q_f[:, 0:512], pbk[5][:, :])
                s.copy(q_f[:, 512:1024], pbk[6][:, :])
                dinc(s, "sA")
                if t >= 2:
                    wait_t(s, "scalar", stok_q2[t - 2])
                s.activation(q2_sb[t % 2][:, :], q_f[:, :], AF.Square)
                dinc(s, "sA")
                for hh in range(H):
                    wait(s, "scalar", "sT", tb + 5 + hh)
                    s.activation(
                        pk_b[:, hh * M:(hh + 1) * M],
                        pbk[5 + (hh % 2)][:, 0:M], AF.Exp)
                    dinc(s, "sA")
                wait(s, "scalar", "sV", vb + 5)
                stok_q[t] = dma(s, sstq[t % 2],
                                qb_scr[t * 128:(t + 1) * 128, :], qb_sb[t % 2][:])
                stok_q2[t] = dma(s, sstq2[t % 2],
                                 q2_scr[t * 128:(t + 1) * 128, :],
                                 q2_sb[t % 2][:])

            @blk.vector
            def _(v, t=t, tb=tb, ab=ab):
                emit_stats_V(v, ab + 1, ab + 2)
                wait(v, "vector", "sA", ab + 5)
                v.tensor_copy(k_b[:, :], k_f[:, :])
                dinc(v, "sV")
                wait(v, "vector", "sT", tb + 3)
                v.tensor_copy(v_b[:, 0:512], pbk[5][:, :])
                v.tensor_copy(v_b[:, 512:1024], pbk[6][:, :])
                dinc(v, "sV")
                wait(v, "vector", "sA", ab + 7)
                if t >= 2:
                    wait_t(v, "vector", stok_q[t - 2])
                v.tensor_copy(qb_sb[t % 2][:, :], q_f[:, :])
                inc(v.drain(), "sV")

            @blk.tensor
            def _(te, t=t, ab=ab, vb=vb):
                wait(te, "tensor", "sA", ab + 3)
                inc(emit_transposes(te, lnb_sb[:, :], [pbk[5], pbk[6]], 8), "sT")
                wait(te, "tensor", "sA", ab + 4)
                last = None
                for co in range(8):
                    bank = pbk[5] if co < 4 else pbk[6]
                    sl = bank[:, (co % 4) * 128:(co % 4) * 128 + 128]
                    for cd in range(DC):
                        last = te.matmul(
                            sl, wk_sb[:, cd * D + co * 128:cd * D + co * 128 + 128],
                            lnT_sb[:, cd * 128:(cd + 1) * 128],
                            start=(co % 4 == 0 and cd == 0),
                            stop=(co % 4 == 3 and cd == DC - 1))
                inc(last, "sT")
                wait(te, "tensor", "sA", ab + 5)
                last = None
                for half in range(2):
                    bank = pbk[5] if half == 0 else pbk[6]
                    for cd in range(DC):
                        last = te.matmul(
                            bank[:, :], lnT_sb[:, cd * 128:(cd + 1) * 128],
                            wv_sb[:, cd * D + half * 512:cd * D + half * 512 + 512],
                            start=(cd == 0), stop=(cd == DC - 1))
                inc(last, "sT")
                wait(te, "tensor", "sV", vb + 4)
                last = None
                for co in range(8):
                    bank = pbk[5] if co < 4 else pbk[6]
                    sl = bank[:, (co % 4) * 128:(co % 4) * 128 + 128]
                    for cd in range(DC):
                        last = te.matmul(
                            sl, wq_sb[:, cd * D + co * 128:cd * D + co * 128 + 128],
                            lnT_sb[:, cd * 128:(cd + 1) * 128],
                            start=(co % 4 == 0 and cd == 0),
                            stop=(co % 4 == 3 and cd == DC - 1))
                inc(last, "sT")
                wait(te, "tensor", "sV", vb + 3)
                wait(te, "tensor", "sA", ab + 7)
                for hh in range(H):
                    if hh >= 2:
                        wait(te, "tensor", "sA", ab + 9 + (hh - 2))
                    par = hh % 2
                    sl = pbk[5 + par][:, 0:M]
                    ksl = k_b[par * 64:par * 64 + 64,
                              (hh // 2) * 128:(hh // 2) * 128 + 128]
                    k2sl = k2_b[par * 64:par * 64 + 64,
                                (hh // 2) * 128:(hh // 2) * 128 + 128]
                    te.matmul(sl, ksl, omdup_bf[par * 64:par * 64 + 64, :],
                              start=True, stop=False)
                    inc(te.matmul(sl, k2sl, neghalf_bf[par * 64:par * 64 + 64, :],
                                  start=False, stop=True), "sT")
                last = None
                for hh in range(H):
                    wait(te, "tensor", "sA", ab + 9 + hh)
                    hl = hh - 8 if hh >= 8 else hh
                    for mc in range(2):
                        kvb = pbk[mc * 2 + (1 if hh >= 8 else 0)]
                        lhs = pk_b[:, hh * M + mc * 128:hh * M + mc * 128 + 128]
                        te.matmul(kvb[:, hl * 64:hl * 64 + 64], lhs,
                                  v_b[:, hh * 64:hh * 64 + 64],
                                  start=(t == 0 and hl == 0),
                                  stop=(t == n_t - 1 and hl == 7))
                        last = te.matmul(pbk[4][:, mc * 16 + hh:mc * 16 + hh + 1],
                                         lhs, onesbf_col[:, :],
                                         start=(t == 0 and hh == 0 and mc == 0),
                                         stop=(t == n_t - 1 and hh == H - 1
                                               and mc == 1))
                inc(last, "sT")

        assert ctr["sT"] == tb + 21, (ctr["sT"], tb)
        assert ctr["sA"] == ab + 8 + H
        assert ctr["sV"] == vb + 5

    with nc.Block() as blk:
        @blk.vector
        def _(v):
            for mc in range(2):
                for hh in range(H):
                    kvb = pbk[mc * 2 + (1 if hh >= 8 else 0)]
                    hl = hh - 8 if hh >= 8 else hh
                    base = mc * (H * 65) + hh * 65
                    v.tensor_copy(kvaug_bf[:, base:base + 64],
                                  kvb[:, hl * 64:hl * 64 + 64])
                    v.tensor_copy(kvaug_bf[:, base + 64:base + 65],
                                  pbk[4][:, mc * 16 + hh:mc * 16 + hh + 1])
            v.drain()
    p2.close()

    if "p3a" not in phases:
        return finish()
    p3 = contextlib.ExitStack()
    wg3_sb = p3.enter_context(nc.sbuf_tensor("wg3_sb", [128, DC * INNER], BF16))
    wu3_sb = p3.enter_context(nc.sbuf_tensor("wu3_sb", [128, DC * INNER], BF16))
    stern = dma_one("tern")
    with nc.Block() as blk:
        @blk.sync
        def _(sy):
            wait_stores(sy, "sync")
            for i in range(8):
                dma(sy, stern, wg3_sb[:, i * INNER:(i + 1) * INNER],
                    gt_scr[i * 128:(i + 1) * 128, :])
            for i in range(8):
                dma(sy, stern, wu3_sb[:, i * INNER:(i + 1) * INNER],
                    ut_scr[i * 128:(i + 1) * 128, :])

    y1a = [p3.enter_context(nc.sbuf_tensor(f"y1a{i}", [128, D], F32))
           for i in range(2)]
    qba = [p3.enter_context(nc.sbuf_tensor(f"qba{i}", [128, D], BF16))
           for i in range(2)]
    q2a = [p3.enter_context(nc.sbuf_tensor(f"q2a{i}", [128, D], BF16))
           for i in range(2)]
    pq_b = p3.enter_context(nc.sbuf_tensor("pq_b", [128, 32 * 128], BF16))
    y2_sb = [p3.enter_context(nc.sbuf_tensor(f"y2sb{i}", [128, D], F32))
             for i in range(2)]
    lnf3 = p3.enter_context(nc.sbuf_tensor("lnf3", [128, D], F32))
    aqi3 = p3.enter_context(nc.sbuf_tensor("aqi3", [128, D], I32))
    aqb3 = p3.enter_context(nc.sbuf_tensor("aqb3", [128, D], BF16))
    aqT3 = p3.enter_context(nc.sbuf_tensor("aqT3", [128, D], BF16))
    silu3 = [p3.enter_context(nc.sbuf_tensor(f"silu3{i}", [128, 512], F32))
             for i in range(2)]
    h3_sb = p3.enter_context(nc.sbuf_tensor("h3_sb", [128, INNER], F32))
    hqi3 = p3.enter_context(nc.sbuf_tensor("hqi3", [128, 2048], I32))
    hqb3 = p3.enter_context(nc.sbuf_tensor("hqb3", [128, INNER], BF16))

    sp3in = dma_pair("p3in")
    ssty2 = dma_pair("sty2", store=True)
    ssth2 = dma_pair("sth2", store=True)
    tok3 = [None] * (n_t + 1)
    stok_h2 = [None] * n_t
    stok_y2 = [None] * n_t

    def p3_loads(sy, tt):
        dma(sy, sp3in[tt % 2], y1a[tt % 2][:],
            y1_scr[tt * 128:(tt + 1) * 128, :])
        dma(sy, sp3in[tt % 2], qba[tt % 2][:],
            qb_scr[tt * 128:(tt + 1) * 128, :])
        tok3[tt] = dma(sy, sp3in[tt % 2], q2a[tt % 2][:],
                       q2_scr[tt * 128:(tt + 1) * 128, :])

    with nc.Block() as blk:
        @blk.sync
        def _(sy):
            p3_loads(sy, 0)

    ATT_BANK = [4] * 7 + [5] * 7 + [6] * 2
    ATT_SLOT = list(range(7)) + list(range(7)) + list(range(2))

    for t in range(n_t):
        tb, ab, vb, pbs = ctr["sT"], ctr["sA"], ctr["sV"], ctr["sP"]
        with nc.Block() as blk:
            @blk.sync
            def _(sy, t=t):
                if t + 1 < n_t:
                    p3_loads(sy, t + 1)

            @blk.scalar
            def _(s, t=t, tb=tb, vb=vb, pbs=pbs):
                for g in range(4):
                    wait(s, "scalar", "sT", tb + 1 + g)
                    bA = pbk[0] if g % 2 == 0 else pbk[2]
                    bB = pbk[1] if g % 2 == 0 else pbk[3]
                    s.activation(pq_b[:, g * 1024:g * 1024 + 512], bA[:, :],
                                 AF.Exp)
                    s.activation(pq_b[:, g * 1024 + 512:(g + 1) * 1024],
                                 bB[:, :], AF.Exp)
                    dinc(s, "sA")
                wait(s, "scalar", "sV", vb + 1)
                emit_stats_A(s, y2_sb[t % 2][:], lnf3[:], vb + 2, vb + 3)
                wait(s, "scalar", "sT", tb + 6)
                s.copy(aqT3[:, 0:512], pbk[0][:, :])
                s.copy(aqT3[:, 512:1024], pbk[1][:, :])
                dinc(s, "sA")
                wait(s, "scalar", "sV", vb + 4)
                for ib in range(NIB):
                    wait(s, "scalar", "sT", tb + 7 + ib)
                    if ib >= 2:
                        wait(s, "scalar", "sV", vb + 6 + (ib - 2))
                    s.activation(silu3[ib % 2][:],
                                 pbk[4 + 2 * (ib % 2)][:, :], AF.Silu,
                                 scale=sc("colg"))
                    dinc(s, "sA")
                wait(s, "scalar", "sV", vb + 1)
                stok_y2[t] = dma(s, ssty2[t % 2],
                                 y2_scr[t * 128:(t + 1) * 128, :],
                                 y2_sb[t % 2][:])
                wait(s, "scalar", "sP", pbs + 2)
                stok_h2[t] = dma(s, ssth2[t % 2],
                                 hq_scr[t * 128:(t + 1) * 128, :], hqb3[:])

            @blk.vector
            def _(v, t=t, tb=tb, ab=ab, pbs=pbs):
                wait(v, "vector", "sT", tb + 5)
                wait_t(v, "vector", tok3[t])
                if t >= 2:
                    wait_t(v, "vector", stok_y2[t - 2])
                for hh in range(H):
                    v.tensor_copy(zcol[:, hh:hh + 1],
                                  pbk[ATT_BANK[hh]][:, ATT_SLOT[hh] * 65 + 64:
                                                    ATT_SLOT[hh] * 65 + 65])
                v.drain()
                v.tensor_scalar(zcol[:, :], zcol[:, :], 2.56e-4, None, ALU.add)
                v.drain()
                v.reciprocal(zcol[:, :], zcol[:, :])
                v.drain()
                for hh in range(H):
                    v.scalar_tensor_tensor(
                        y2_sb[t % 2][:, hh * 64:hh * 64 + 64],
                        pbk[ATT_BANK[hh]][:, ATT_SLOT[hh] * 65:
                                          ATT_SLOT[hh] * 65 + 64],
                        zcol[:, hh:hh + 1], y1a[t % 2][:, hh * 64:hh * 64 + 64],
                        ALU.mult, ALU.add)
                inc(v.drain(), "sV")
                emit_stats_V(v, ab + 5, ab + 6)
                wait(v, "vector", "sA", ab + 7)
                emit_quant_v(v, lnf3[:], aqi3[:], sc("aqrec"),
                             (sc("colg"), sc("colu")))
                v.tensor_copy(aqb3[:], aqi3[:])
                dinc(v, "sV")
                for ib in range(NIB):
                    wait(v, "vector", "sT", tb + 7 + ib)
                    wait(v, "vector", "sA", ab + 9 + ib)
                    inc(v.scalar_tensor_tensor(
                        h3_sb[:, ib * 512:(ib + 1) * 512],
                        pbk[5 + 2 * (ib % 2)][:, :], sc("colu"),
                        silu3[ib % 2][:], ALU.mult, ALU.mult), "sV")
                v.drain()
                v.reduce_max(sc("hamax"), h3_sb[:], AX.X,
                             apply_absolute_value=True)
                v.drain()
                v.tensor_scalar(sc("amc"), sc("hamax"), sc("eps"), None, ALU.max)
                v.drain()
                v.reciprocal(sc("rc"), sc("amc"))
                v.drain()
                v.tensor_scalar(sc("hqs"), sc("rc"), 127.0, None, ALU.mult)
                v.drain()
                v.reciprocal(hrecip_all[:, t:t + 1], sc("hqs"))
                v.drain()
                v.tensor_scalar(hqi3[:], h3_sb[:, 0:2048], sc("hqs"), None,
                                ALU.mult)
                dinc(v, "sV")
                wait(v, "vector", "sP", pbs + 1)
                v.tensor_scalar(hqi3[:], h3_sb[:, 2048:4096], sc("hqs"), None,
                                ALU.mult)
                dinc(v, "sV")

            @blk.gpsimd
            def _(g, t=t, vb=vb):
                wait(g, "gpsimd", "sV", vb + 14)
                if t >= 1:
                    wait_t(g, "gpsimd", stok_h2[t - 1])
                g.tensor_copy(hqb3[:, 0:2048], hqi3[:])
                dinc(g, "sP")
                wait(g, "gpsimd", "sV", vb + 15)
                g.tensor_copy(hqb3[:, 2048:4096], hqi3[:])
                dinc(g, "sP")

            @blk.tensor
            def _(te, t=t, ab=ab, vb=vb, pbs=pbs):
                wait_t(te, "tensor", tok3[t])
                if t == 0:
                    wait(te, "tensor", stern, ctr[stern])
                for g in range(4):
                    mc, par = g // 2, g % 2
                    bA = pbk[0] if g % 2 == 0 else pbk[2]
                    bB = pbk[1] if g % 2 == 0 else pbk[3]
                    if g >= 2:
                        wait(te, "tensor", "sA", ab + 1 + (g - 2))
                    oms = omdup_bf[par * 64:par * 64 + 64,
                                   mc * 128:mc * 128 + 128]
                    ngs = neghalf_bf[par * 64:par * 64 + 64, 0:128]
                    last = None
                    for c in range(8):
                        bank = bA if c < 4 else bB
                        sl = bank[:, (c % 4) * 128:(c % 4) * 128 + 128]
                        qsl = qba[t % 2][par * 64:par * 64 + 64,
                                         c * 128:(c + 1) * 128]
                        q2sl = q2a[t % 2][par * 64:par * 64 + 64,
                                          c * 128:(c + 1) * 128]
                        te.matmul(sl, oms, qsl, start=(c % 4 == 0), stop=False)
                        last = te.matmul(sl, ngs, q2sl, start=False,
                                         stop=(c % 4 == 3))
                    inc(last, "sT")
                wait(te, "tensor", "sA", ab + 4)
                last = None
                bank_started = set()
                for hh in range(H):
                    bk, slt = ATT_BANK[hh], ATT_SLOT[hh]
                    for mc in range(2):
                        g = mc * 2 + (hh % 2)
                        j = g * 8 + hh // 2
                        first = bk not in bank_started
                        bank_started.add(bk)
                        last = te.matmul(
                            pbk[bk][:, slt * 65:slt * 65 + 65],
                            pq_b[:, j * 128:(j + 1) * 128],
                            kvaug_bf[:, mc * (H * 65) + hh * 65:
                                     mc * (H * 65) + (hh + 1) * 65],
                            start=first,
                            stop=(mc == 1 and hh in (6, 13, 15)))
                inc(last, "sT")
                wait(te, "tensor", "sV", vb + 5)
                wait(te, "tensor", "sA", ab + 4)
                inc(emit_transposes(te, aqb3[:, :], [pbk[0], pbk[1]], 8), "sT")
                wait(te, "tensor", "sA", ab + 8)
                for ib in range(NIB):
                    if ib >= 2:
                        wait(te, "tensor", "sV", vb + 6 + (ib - 2))
                    else:
                        wait(te, "tensor", "sV", vb + 1)
                    gb = pbk[4 + 2 * (ib % 2)]
                    ub = pbk[5 + 2 * (ib % 2)]
                    for c in range(DC):
                        te.matmul(gb[:, :], aqT3[:, c * 128:(c + 1) * 128],
                                  wg3_sb[:, c * INNER + ib * 512:
                                         c * INNER + ib * 512 + 512],
                                  start=(c == 0), stop=(c == DC - 1))
                    last = None
                    for c in range(DC):
                        last = te.matmul(ub[:, :], aqT3[:, c * 128:(c + 1) * 128],
                                         wu3_sb[:, c * INNER + ib * 512:
                                                c * INNER + ib * 512 + 512],
                                         start=(c == 0), stop=(c == DC - 1))
                    inc(last, "sT")

        assert ctr["sT"] == tb + 6 + NIB, (ctr["sT"], tb)
        assert ctr["sA"] == ab + 8 + NIB
        assert ctr["sV"] == vb + 7 + NIB
        assert ctr["sP"] == pbs + 2
    p3.close()

    if "p3b" not in phases:
        return finish()
    swdld = dma_one("wdld")

    def p3b_pre(wd_sb):
        with nc.Block() as blk:
            @blk.sync
            def _(sy):
                wait_stores(sy, "sync")
                for ci in range(32):
                    dma(sy, swdld, wd_sb[:, ci * 1024:(ci + 1) * 1024],
                        dt_scr[ci * 128:(ci + 1) * 128, :])
                wait(sy, "sync", swdld, ctr[swdld])

    run_down_phase(p3b_pre, hq_scr, y2_scr, y, final=True)

    return finish()


_NC_CACHE = {}


def _get_program():
    if "nc" not in _NC_CACHE:
        _NC_CACHE["nc"] = build_program()
    return _NC_CACHE["nc"]


def kernel(L0_attn_x, L0_attn_Wq, L0_attn_Wk, L0_attn_Wv, L0_attn_omega,
           L0_ffn_w_gate, L0_ffn_w_up, L0_ffn_w_down):
    xs = np.ascontiguousarray(np.asarray(L0_attn_x, dtype=np.float32))
    wq_ = np.ascontiguousarray(np.asarray(L0_attn_Wq, dtype=np.float32))
    wk_ = np.ascontiguousarray(np.asarray(L0_attn_Wk, dtype=np.float32))
    wv_ = np.ascontiguousarray(np.asarray(L0_attn_Wv, dtype=np.float32))
    omt_ = np.ascontiguousarray(np.asarray(L0_attn_omega, dtype=np.float32).T)
    wgt_ = np.ascontiguousarray(np.asarray(L0_ffn_w_gate, dtype=np.float32).T)
    wut_ = np.ascontiguousarray(np.asarray(L0_ffn_w_up, dtype=np.float32).T)
    wdt_ = np.ascontiguousarray(np.asarray(L0_ffn_w_down, dtype=np.float32).T)

    nc = _get_program()
    in_maps = [{"x": np.ascontiguousarray(xs[c]), "wq": wq_, "wk": wk_,
                "wv": wv_, "omt": omt_, "wgt": wgt_, "wut": wut_, "wdt": wdt_}
               for c in range(8)]
    res = run_bass_kernel_spmd(nc, in_maps, core_ids=list(range(8)))
    return np.stack([np.asarray(res.results[c]["y"], dtype=np.float32)
                     for c in range(8)], axis=0)


if __name__ == "__main__":
    print("building...")
    nc = build_program()
    print("instructions:", nc.n_instructions())


# version 05
# speedup vs baseline: 1.0731x; 1.0731x over previous
import contextlib
import numpy as np
import concourse.bass as bass
import concourse.mybir as mybir
from concourse.bass_utils import run_bass_kernel_spmd

F32 = mybir.dt.float32
BF16 = mybir.dt.bfloat16
I32 = mybir.dt.int32
AF = mybir.ActivationFunctionType
ALU = mybir.AluOpType
AX = mybir.AxisListType

N, D, INNER, M, H, HD = 4096, 1024, 4096, 256, 16, 64
TILES, DC, NIB = 32, 8, 8

_SC = ["sum", "ssq", "mean", "m2", "nm2", "var", "lnv", "rstd", "mr", "nmr",
       "amax", "amc", "rc", "qs", "aqrec", "colg", "colu", "hamax", "hqs",
       "colt", "c1024inv", "cneg1", "cneghalf", "eps", "srt"]
SCOL = {n: i for i, n in enumerate(_SC)}


def build_program(dbg=(), phases=("p0", "p1a", "p1b", "p2a", "p3a", "p3b"),
                  n_tiles=None):
    n_t = n_tiles or TILES
    n_rows = n_t * 128
    nc = bass.Bass("TRN2", target_bir_lowering=False, dynamic_dma_scratch_size=512)

    x = nc.dram_tensor("x", [n_rows, D], F32, kind="ExternalInput").ap()
    wq = nc.dram_tensor("wq", [D, D], F32, kind="ExternalInput").ap()
    wk = nc.dram_tensor("wk", [D, D], F32, kind="ExternalInput").ap()
    wv = nc.dram_tensor("wv", [D, D], F32, kind="ExternalInput").ap()
    omt = nc.dram_tensor("omt", [HD, M], F32, kind="ExternalInput").ap()
    wgt = nc.dram_tensor("wgt", [D, INNER], F32, kind="ExternalInput").ap()
    wut = nc.dram_tensor("wut", [D, INNER], F32, kind="ExternalInput").ap()
    wdt = nc.dram_tensor("wdt", [INNER, D], F32, kind="ExternalInput").ap()
    y = nc.dram_tensor("y", [n_rows, D], F32, kind="ExternalOutput").ap()

    def scr(name, shape, dt):
        kind = "ExternalOutput" if name in dbg else "Internal"
        return nc.dram_tensor(name, shape, dt, kind=kind).ap()

    hq_scr = scr("hq_scr", [n_rows, INNER], BF16)
    y1_scr = scr("y1_scr", [n_rows, D], F32)
    y2_scr = scr("y2_scr", [n_rows, D], F32)
    qb_scr = scr("qb_scr", [n_rows, D], BF16)
    q2_scr = scr("q2_scr", [n_rows, D], BF16)
    gt_scr = scr("gt_scr", [D, INNER], BF16)
    ut_scr = scr("ut_scr", [D, INNER], BF16)
    dt_scr = scr("dt_scr", [INNER, D], BF16)

    sems, ctr = {}, {}

    def new_sem(nm):
        sems[nm] = nc.ctx.enter_context(nc.semaphore(nm))
        ctr[nm] = 0
        return nm

    for nm in ("sA", "sV", "sP", "sT"):
        new_sem(nm)
    WAITED = {e: {} for e in ("sync", "scalar", "vector", "gpsimd", "tensor")}
    store_sems = []

    def dma_pair(stream, store=False):
        pair = []
        for i in range(2):
            nm = new_sem(f"d{stream}{i}")
            pair.append(nm)
            if store:
                store_sems.append(nm)
        return pair

    def dma_one(stream, store=False):
        nm = new_sem(f"d{stream}")
        if store:
            store_sems.append(nm)
        return nm

    def inc(instr, sem, n=1):
        ctr[sem] += n
        instr.then_inc(sems[sem], n)
        return ctr[sem]

    def dma(eng, sem, out, in_):
        v = inc(eng.dma_start(out=out, in_=in_), sem, 16)
        return (sem, v)

    def wait(eng, ename, sem, val):
        if val and WAITED[ename].get(sem, 0) < val:
            eng.wait_ge(sems[sem], val)
            WAITED[ename][sem] = val

    def wait_t(eng, ename, tok):
        if tok is not None:
            wait(eng, ename, tok[0], tok[1])

    def wait_stores(eng, ename):
        for s in store_sems:
            wait(eng, ename, s, ctr[s])

    def dinc(eng, sem, n=1):
        return inc(eng.drain(), sem, n)

    cm = nc.ctx.enter_context
    ident_bf = cm(nc.sbuf_tensor("ident_bf", [128, 128], BF16))
    omdup_bf = cm(nc.sbuf_tensor("omdup_bf", [128, M], BF16))
    neghalf_bf = cm(nc.sbuf_tensor("neghalf_bf", [128, M], BF16))
    onesbf_col = cm(nc.sbuf_tensor("onesbf_col", [128, 1], BF16))
    onesf_col = cm(nc.sbuf_tensor("onesf_col", [128, 1], F32))
    onesrow = cm(nc.sbuf_tensor("onesrow", [1, 128], F32))
    scols = cm(nc.sbuf_tensor("scols", [128, 64], F32))
    wcols = cm(nc.sbuf_tensor("wcols", [128, 8], F32))
    sdh_col = cm(nc.sbuf_tensor("sdh_col", [128, 1], F32))
    hrecip_all = cm(nc.sbuf_tensor("hrecip_all", [128, TILES], F32))
    kvaug_bf = cm(nc.sbuf_tensor("kvaug_bf", [128, 2 * H * 65], BF16))
    sqjunk = cm(nc.sbuf_tensor("sqjunk", [128, D], BF16))
    sqjunk2 = cm(nc.sbuf_tensor("sqjunk2", [128, D], BF16))
    zcol = cm(nc.sbuf_tensor("zcol", [128, H], F32))
    pbk = [cm(nc.psum_tensor(f"pb{i}", [128, 512], F32)) for i in range(8)]

    def sc(nm, par=0):
        i = SCOL[nm] + 32 * (par % 2)
        return scols[:, i:i + 1]

    _uid = [0]

    def _nm(p):
        _uid[0] += 1
        return f"{p}{_uid[0]}"

    def finish():
        with nc.Block() as blk:
            @blk.gpsimd
            def _(g):
                wait_stores(g, "gpsimd")
        return nc

    p0 = contextlib.ExitStack()
    wtmp0 = [p0.enter_context(nc.sbuf_tensor(f"p0w{i}", [128, INNER], F32))
             for i in range(2)]
    accs = p0.enter_context(nc.sbuf_tensor("accs", [128, 28], F32))

    with nc.Block() as blk:
        @blk.vector
        def _(v):
            v.memset(scols[:], 0.0)
            v.memset(onesf_col[:], 1.0)
            v.memset(onesrow[:], 1.0)
            v.memset(neghalf_bf[:], -0.5)
            v.memset(kvaug_bf[:], 0.0)
            v.memset(zcol[:], 0.0)

        @blk.gpsimd
        def _(g):
            g.memset(onesbf_col[:], 1.0)
            g.memset(sqjunk[:], 0.0)
            g.memset(accs[:], 0.0)
            g.iota(ident_bf[:, :], [[1, 128]], base=0, channel_multiplier=-1,
                   allow_small_or_imprecise_dtypes=True)

    with nc.Block() as blk:
        @blk.vector
        def _(v):
            v.tensor_scalar(ident_bf[:, :], ident_bf[:, :], 0.0, None, ALU.is_equal)

        @blk.gpsimd
        def _(g):
            for par in range(2):
                g.memset(sc("c1024inv", par), 1.0 / 1024.0)
                g.memset(sc("cneg1", par), -1.0)
                g.memset(sc("cneghalf", par), -0.5)
                g.memset(sc("eps", par), 1e-5)

    chunks = []
    for mi, w in enumerate((wgt, wut)):
        for i in range(8):
            chunks.append((mi, i, w[i * 128:(i + 1) * 128, :]))
    for i in range(8):
        chunks.append((2, i, wdt[i * 512:(i + 1) * 512, :]
                       .rearrange("(p r) n -> p (r n)", p=128)))
    sp0 = dma_pair("p0w")
    d0, v0base = [], ctr["sV"]
    with nc.Block() as blk:
        @blk.sync
        def _(sy):
            for j, (mi, i, ap) in enumerate(chunks):
                if j >= 2:
                    wait(sy, "sync", "sV", v0base + j - 1)
                d0.append(dma(sy, sp0[j % 2], wtmp0[j % 2][:], ap))

        @blk.vector
        def _(v):
            for j, (mi, i, ap) in enumerate(chunks):
                wait_t(v, "vector", d0[j])
                inc(v.reduce_sum(accs[:, mi * 8 + i:mi * 8 + i + 1],
                                 wtmp0[j % 2][:], AX.X,
                                 apply_absolute_value=True), "sV")

    v0tok = [None]
    with nc.Block() as blk:
        @blk.vector
        def _(v):
            v.drain()
            for mi in range(3):
                v.reduce_sum(accs[:, 24 + mi:25 + mi], accs[:, mi * 8:(mi + 1) * 8],
                             AX.X)
            v0tok[0] = dinc(v, "sV")

        @blk.tensor
        def _(t):
            wait(t, "tensor", "sV", v0tok[0])
            last = None
            for mi in range(3):
                last = t.matmul(pbk[0][0:1, mi:mi + 1], accs[:, 24 + mi:25 + mi],
                                onesf_col[:, :], start=(mi == 0), stop=(mi == 2))
            inc(last, "sT")

    t0a = ctr["sT"]
    with nc.Block() as blk:
        @blk.vector
        def _(v):
            wait(v, "vector", "sT", t0a)
            v.tensor_copy(accs[0:1, 0:3], pbk[0][0:1, 0:3])
            v.drain()
            v.tensor_scalar(accs[0:1, 4:7], accs[0:1, 0:3], 1.0 / 4194304.0,
                            1e-5, ALU.mult, ALU.max)
            v.drain()
            v.reciprocal(accs[0:1, 8:11], accs[0:1, 4:7])
            v0tok[0] = dinc(v, "sV")

        @blk.tensor
        def _(t):
            wait(t, "tensor", "sV", v0tok[0])
            last = None
            for j in range(6):
                src = accs[0:1, 4 + j:5 + j] if j < 3 else \
                    accs[0:1, 8 + j - 3:9 + j - 3]
                last = t.matmul(pbk[1][:, j:j + 1], onesrow[0:1, :], src,
                                start=(j == 0), stop=(j == 5))
            inc(last, "sT")

    t0b = ctr["sT"]
    with nc.Block() as blk:
        @blk.vector
        def _(v):
            wait(v, "vector", "sT", t0b)
            v.tensor_copy(wcols[:, 0:6], pbk[1][:, 0:6])
            v.drain()
            v.tensor_scalar(sdh_col[:, :], wcols[:, 2:3], 0.5, None, ALU.mult)
            v.drain()
    p0.close()

    def emit_stats_A(s, src_f, lnf_out, v_var_tok, v_nmr_tok, par=0):
        s.activation(sqjunk[:, :], src_f, AF.Copy, accum_out=sc("sum", par))
        s.activation(sqjunk2[:, :], src_f, AF.Square, accum_out=sc("ssq", par))
        dinc(s, "sA")
        wait(s, "scalar", "sV", v_var_tok)
        s.activation(sc("srt", par), sc("var", par), AF.Sqrt, bias=sc("eps", par))
        dinc(s, "sA")
        wait(s, "scalar", "sV", v_nmr_tok)
        s.activation(lnf_out, src_f, AF.Identity, scale=sc("rstd", par),
                     bias=sc("nmr", par))
        return dinc(s, "sA")

    def emit_stats_V(v, a_acc_tok, a_srt_tok, par=0):
        wait(v, "vector", "sA", a_acc_tok)
        v.tensor_scalar(sc("mean", par), sc("sum", par), 1.0 / 1024.0, None,
                        ALU.mult)
        v.drain()
        v.tensor_tensor(sc("m2", par), sc("mean", par), sc("mean", par),
                        ALU.mult)
        v.drain()
        v.scalar_tensor_tensor(sc("var", par), sc("ssq", par), 1.0 / 1024.0,
                               sc("m2", par), ALU.mult, ALU.subtract)
        t1 = dinc(v, "sV")
        wait(v, "vector", "sA", a_srt_tok)
        v.reciprocal(sc("rstd", par), sc("srt", par))
        v.drain()
        v.tensor_scalar(sc("nmr", par), sc("mean", par), sc("rstd", par), -1.0,
                        ALU.mult, ALU.mult)
        t2 = dinc(v, "sV")
        return t1, t2

    def emit_quant_v(v, src_f, aqi_out, rec_ap, colgu, par=0):
        v.reduce_max(sc("amax", par), src_f, AX.X, apply_absolute_value=True)
        v.drain()
        v.tensor_scalar(sc("amc", par), sc("amax", par), sc("eps", par), None,
                        ALU.max)
        v.drain()
        v.reciprocal(sc("rc", par), sc("amc", par))
        v.drain()
        v.tensor_scalar(sc("qs", par), sc("rc", par), 127.0, None, ALU.mult)
        v.drain()
        v.reciprocal(rec_ap, sc("qs", par))
        v.drain()
        if colgu:
            v.tensor_scalar(colgu[0], rec_ap, wcols[:, 0:1], None, ALU.mult)
            v.tensor_scalar(colgu[1], rec_ap, wcols[:, 1:2], None, ALU.mult)
        v.tensor_scalar(aqi_out, src_f, sc("qs", par), None, ALU.mult)
        return dinc(v, "sV")

    def emit_transposes(t, src_b, banks, n):
        last = None
        for c in range(n):
            bank = banks[(c // 4) % len(banks)]
            last = t.matmul(bank[:, (c % 4) * 128:(c % 4) * 128 + 128],
                            src_b[:, c * 128:(c + 1) * 128], ident_bf[:, :],
                            start=(c % 4 == 0), stop=(c % 4 == 3))
        return last

    def emit_wquant(wsrc_list, dst_sb_list, scr_list, wsi_list, nchunk, ccols):
        est = contextlib.ExitStack()
        wtmp = [est.enter_context(nc.sbuf_tensor(_nm("wqw"), [128, ccols], F32))
                for _ in range(2)]
        i32t = [est.enter_context(nc.sbuf_tensor(_nm("wqi"), [128, ccols], I32))
                for _ in range(2)]
        jobs = []
        for (ws, db, sd, wsi) in zip(wsrc_list, dst_sb_list, scr_list, wsi_list):
            for i in range(nchunk):
                jobs.append((ws, db, sd, wsi, i))
        spw = dma_pair(_nm("wql"))
        sps = dma_pair(_nm("wqs"), store=True)
        dt_ = []
        vbase = ctr["sV"]
        with nc.Block() as blk:
            @blk.sync
            def _(sy):
                wait_stores(sy, "sync")
                for j, (ws, db, sd, wsi, i) in enumerate(jobs):
                    if j >= 2:
                        wait(sy, "sync", "sV", vbase + 2 * (j - 1))
                    dt_.append(dma(sy, spw[j % 2], wtmp[j % 2][:],
                                   ws[i * 128:(i + 1) * 128, :]))

            @blk.scalar
            def _(s):
                for j, (ws, db, sd, wsi, i) in enumerate(jobs):
                    wait(s, "scalar", "sV", vbase + 2 * (j + 1))
                    dma(s, sps[j % 2], sd[i * 128:(i + 1) * 128, :],
                        db[:, i * ccols:(i + 1) * ccols])

            @blk.vector
            def _(v):
                for j, (ws, db, sd, wsi, i) in enumerate(jobs):
                    wait_t(v, "vector", dt_[j])
                    v.tensor_scalar(i32t[j % 2][:], wtmp[j % 2][:],
                                    wcols[:, 3 + wsi:4 + wsi], None, ALU.mult)
                    dinc(v, "sV")
                    v.tensor_scalar(db[:, i * ccols:(i + 1) * ccols],
                                    i32t[j % 2][:], -1.0, 1.0, ALU.max, ALU.min)
                    dinc(v, "sV")
        est.close()

    if "p1a" not in phases:
        return finish()
    p1 = contextlib.ExitStack()
    wg_sb = p1.enter_context(nc.sbuf_tensor("wg_sb", [128, DC * INNER], BF16))
    wu_sb = p1.enter_context(nc.sbuf_tensor("wu_sb", [128, DC * INNER], BF16))
    emit_wquant([wgt, wut], [wg_sb, wu_sb], [gt_scr, ut_scr], [0, 1], 8, INNER)

    xa = [p1.enter_context(nc.sbuf_tensor(f"xa{i}", [128, D], F32))
          for i in range(2)]
    lnf = [p1.enter_context(nc.sbuf_tensor(f"lnf{i}", [128, D], F32))
           for i in range(2)]
    aqi = p1.enter_context(nc.sbuf_tensor("aqi", [128, D], I32))
    aqb = [p1.enter_context(nc.sbuf_tensor(f"aqb{i}", [128, D], BF16))
           for i in range(2)]
    aqT = [p1.enter_context(nc.sbuf_tensor(f"aqT{i}", [128, D], BF16))
           for i in range(3)]
    qcols = p1.enter_context(nc.sbuf_tensor("qcols", [128, 6], F32))

    def qc(kind, slot):
        base = 0 if kind == "g" else 3
        return qcols[:, base + slot:base + slot + 1]
    silu_sb = [p1.enter_context(nc.sbuf_tensor(f"silu{i}", [128, 512], F32))
               for i in range(3)]
    h_sb = p1.enter_context(nc.sbuf_tensor("h_sb", [128, INNER], F32))
    hqi = p1.enter_context(nc.sbuf_tensor("hqi", [128, 2048], I32))
    hqb = [p1.enter_context(nc.sbuf_tensor(f"hqb{i}", [128, INNER], BF16))
           for i in range(2)]

    sxin = dma_pair("xin")
    sshq = dma_pair("sthq", store=True)
    tok_x = [None] * n_t
    stok_hq = [None] * n_t

    A0, V0, T0 = ctr["sA"], ctr["sV"], ctr["sT"]
    AT, VT, TT, GU = {}, {}, {}, {}
    _ca, _cv, _ct = [A0], [V0], [T0]

    def _fa(t):
        for j in (1, 2, 3):
            _ca[0] += 1
            AT[("s", t, j)] = _ca[0]

    def _ea(t):
        _ca[0] += 1
        AT[("e", t)] = _ca[0]

    def _fv(t):
        for j in (1, 2, 3, 4):
            _cv[0] += 1
            VT[("f", t, j)] = _cv[0]

    _fa(0), _fa(1), _ea(0), _ea(1)
    for _t in range(n_t):
        if _t + 2 < n_t:
            _fa(_t + 2)
        for _ib in range(NIB):
            _ca[0] += 1
            AT[("m", _t, _ib)] = _ca[0]
        for _j in (1, 2):
            _ca[0] += 1
            AT[("h", _t, _j)] = _ca[0]
        if _t + 2 < n_t:
            _ea(_t + 2)
    _fv(0), _fv(1)
    for _t in range(n_t):
        if _t + 2 < n_t:
            _fv(_t + 2)
        for _ib in range(NIB):
            _cv[0] += 1
            VT[("m", _t, _ib)] = _cv[0]
        for _j in (1, 2):
            _cv[0] += 1
            VT[("h", _t, _j)] = _cv[0]
    for _t in (0, 1):
        _ct[0] += 1
        TT[_t] = _ct[0]
    for _t in range(n_t):
        for _ib in range(NIB):
            _ct[0] += 1
            GU[(_t, _ib)] = _ct[0]
        if _t + 2 < n_t:
            _ct[0] += 1
            TT[_t + 2] = _ct[0]
    GBANK = [(pbk[2], pbk[3]), (pbk[4], pbk[5]), (pbk[6], pbk[7])]

    with nc.Block() as blk:
        @blk.sync
        def _(sy):
            for t in range(n_t):
                if t >= 2:
                    wait(sy, "sync", "sA", AT[("s", t - 2, 3)])
                tok_x[t] = dma(sy, sxin[t % 2], xa[t % 2][:],
                               x[t * 128:(t + 1) * 128, :])

        @blk.scalar
        def _(s):
            def front_a(t):
                par = t % 2
                wait_t(s, "scalar", tok_x[t])
                if t >= 2:
                    wait(s, "scalar", "sV", VT[("f", t - 2, 3)])
                tok = emit_stats_A(s, xa[par][:], lnf[par][:],
                                   VT[("f", t, 1)], VT[("f", t, 2)], par)
                assert tok == AT[("s", t, 3)]

            def evac_a(t):
                wait(s, "scalar", "sT", TT[t])
                if t >= 3:
                    wait(s, "scalar", "sT", GU[(t - 3, 7)])
                s.copy(aqT[t % 3][:, 0:512], pbk[0][:, :])
                s.copy(aqT[t % 3][:, 512:1024], pbk[1][:, :])
                tok = dinc(s, "sA")
                assert tok == AT[("e", t)]

            front_a(0), front_a(1), evac_a(0), evac_a(1)
            for t in range(n_t):
                par = t % 2
                if t + 2 < n_t:
                    front_a(t + 2)
                wait(s, "scalar", "sV", VT[("f", t, 3)])
                for ib in range(NIB):
                    gg = 8 * t + ib
                    wait(s, "scalar", "sT", GU[(t, ib)])
                    if gg >= 3:
                        tt, ii = divmod(gg - 3, 8)
                        wait(s, "scalar", "sV", VT[("m", tt, ii)])
                    tok = inc(s.activation(silu_sb[gg % 3][:],
                                           GBANK[gg % 3][0][:, :], AF.Silu,
                                           scale=qc("g", t % 3)), "sA")
                    assert tok == AT[("m", t, ib)]
                wait(s, "scalar", "sV", VT[("h", t, 1)])
                if t >= 2:
                    wait_t(s, "scalar", stok_hq[t - 2])
                tok = inc(s.activation(hqb[par][:, 0:2048], hqi[:],
                                       AF.Identity), "sA")
                assert tok == AT[("h", t, 1)]
                wait(s, "scalar", "sV", VT[("h", t, 2)])
                s.activation(hqb[par][:, 2048:4096], hqi[:], AF.Identity)
                tok = dinc(s, "sA")
                assert tok == AT[("h", t, 2)]
                stok_hq[t] = dma(s, sshq[par],
                                 hq_scr[t * 128:(t + 1) * 128, :], hqb[par][:])
                if t + 2 < n_t:
                    evac_a(t + 2)

        @blk.vector
        def _(v):
            def front_v(t):
                par = t % 2
                t1, t2 = emit_stats_V(v, AT[("s", t, 1)], AT[("s", t, 2)], par)
                assert (t1, t2) == (VT[("f", t, 1)], VT[("f", t, 2)])
                wait(v, "vector", "sA", AT[("s", t, 3)])
                if t >= 3:
                    wait(v, "vector", "sA", AT[("m", t - 3, 7)])
                tok = emit_quant_v(v, lnf[par][:], aqi[:], sc("aqrec", par),
                                   (qc("g", t % 3), qc("u", t % 3)), par)
                assert tok == VT[("f", t, 3)]
                if t >= 2:
                    wait(v, "vector", "sT", TT[t - 2])
                v.tensor_copy(aqb[par][:], aqi[:])
                tok = dinc(v, "sV")
                assert tok == VT[("f", t, 4)]

            front_v(0), front_v(1)
            for t in range(n_t):
                par = t % 2
                if t + 2 < n_t:
                    front_v(t + 2)
                for ib in range(NIB):
                    gg = 8 * t + ib
                    wait(v, "vector", "sT", GU[(t, ib)])
                    wait(v, "vector", "sA", AT[("m", t, ib)])
                    tok = inc(v.scalar_tensor_tensor(
                        h_sb[:, ib * 512:(ib + 1) * 512],
                        GBANK[gg % 3][1][:, :], qc("u", t % 3),
                        silu_sb[gg % 3][:], ALU.mult, ALU.mult), "sV")
                    assert tok == VT[("m", t, ib)]
                v.drain()
                v.reduce_max(sc("hamax", par), h_sb[:], AX.X,
                             apply_absolute_value=True)
                v.drain()
                v.tensor_scalar(sc("amc", par), sc("hamax", par),
                                sc("eps", par), None, ALU.max)
                v.drain()
                v.reciprocal(sc("rc", par), sc("amc", par))
                v.drain()
                v.tensor_scalar(sc("hqs", par), sc("rc", par), 127.0, None,
                                ALU.mult)
                v.drain()
                v.reciprocal(hrecip_all[:, t:t + 1], sc("hqs", par))
                v.drain()
                if t >= 1:
                    wait(v, "vector", "sA", AT[("h", t - 1, 2)])
                v.tensor_scalar(hqi[:], h_sb[:, 0:2048], sc("hqs", par), None,
                                ALU.mult)
                tok = dinc(v, "sV")
                assert tok == VT[("h", t, 1)]
                wait(v, "vector", "sA", AT[("h", t, 1)])
                v.tensor_scalar(hqi[:], h_sb[:, 2048:4096], sc("hqs", par),
                                None, ALU.mult)
                tok = dinc(v, "sV")
                assert tok == VT[("h", t, 2)]

        @blk.tensor
        def _(te):
            def trans(t):
                wait(te, "tensor", "sV", VT[("f", t, 4)])
                if t >= 1:
                    wait(te, "tensor", "sA", AT[("e", t - 1)])
                tok = inc(emit_transposes(te, aqb[t % 2][:, :],
                                          [pbk[0], pbk[1]], 8), "sT")
                assert tok == TT[t]

            trans(0), trans(1)
            for t in range(n_t):
                wait(te, "tensor", "sA", AT[("e", t)])
                for ib in range(NIB):
                    gg = 8 * t + ib
                    if gg >= 3:
                        tt, ii = divmod(gg - 3, 8)
                        wait(te, "tensor", "sV", VT[("m", tt, ii)])
                    gb, ub = GBANK[gg % 3]
                    for c in range(DC):
                        te.matmul(gb[:, :],
                                  aqT[t % 3][:, c * 128:(c + 1) * 128],
                                  wg_sb[:, c * INNER + ib * 512:
                                        c * INNER + ib * 512 + 512],
                                  start=(c == 0), stop=(c == DC - 1))
                    last = None
                    for c in range(DC):
                        last = te.matmul(ub[:, :],
                                         aqT[t % 3][:, c * 128:(c + 1) * 128],
                                         wu_sb[:, c * INNER + ib * 512:
                                               c * INNER + ib * 512 + 512],
                                         start=(c == 0), stop=(c == DC - 1))
                    tok = inc(last, "sT")
                    assert tok == GU[(t, ib)]
                if t + 2 < n_t:
                    trans(t + 2)

    assert ctr["sA"] == A0 + 14 * n_t
    assert ctr["sV"] == V0 + 14 * n_t
    assert ctr["sT"] == T0 + 9 * n_t
    p1.close()

    def run_down_phase(preamble, hq_src, res_src, out_dst, final):
        pd = contextlib.ExitStack()
        wd_sb = pd.enter_context(nc.sbuf_tensor(_nm("wd_sb"),
                                                [128, 32 * 1024], BF16))
        preamble(wd_sb)
        hqa = [pd.enter_context(nc.sbuf_tensor(_nm("hqa"), [128, INNER], BF16))
               for _ in range(2)]
        resa = [pd.enter_context(nc.sbuf_tensor(_nm("resa"), [128, D], F32))
                for _ in range(2)]
        hqT = [pd.enter_context(nc.sbuf_tensor(_nm("hqT"), [128, INNER], BF16))
               for _ in range(2)]
        y3sb = [pd.enter_context(nc.sbuf_tensor(_nm("y3sb"), [128, D], F32))
                for _ in range(2)]
        sdlh = dma_pair(_nm("dnlh"))
        sdlr = dma_pair(_nm("dnlr"))
        sds = dma_pair(_nm("dns"), store=True)
        tok_hq = [None] * n_t
        tok_res = [None] * n_t
        stok = [None] * n_t

        T0d, A0d, V0d = ctr["sT"], ctr["sA"], ctr["sV"]
        _pe_off = [1, 2, 3, 4, 6, 8, 10, 12]
        _dg_off = [5, 7, 9, 11, 13, 14, 15, 16]

        def TG(t, g):
            return T0d + 16 * t + _pe_off[g]

        def DG(t, dg):
            return T0d + 16 * t + _dg_off[dg]

        def EV(t, g):
            return A0d + 8 * t + 1 + g

        def Y3(t):
            return V0d + t + 1

        with nc.Block() as blk:
            @blk.sync
            def _(sy):
                wait_stores(sy, "sync")
                for t in range(n_t):
                    if t >= 2:
                        wait(sy, "sync", "sT", TG(t - 2, 7))
                        wait(sy, "sync", "sV", Y3(t - 2))
                    tok_hq[t] = dma(sy, sdlh[t % 2], hqa[t % 2][:],
                                    hq_src[t * 128:(t + 1) * 128, :])
                    tok_res[t] = dma(sy, sdlr[t % 2], resa[t % 2][:],
                                     res_src[t * 128:(t + 1) * 128, :])

            @blk.scalar
            def _(s):
                for t in range(n_t):
                    for g in range(8):
                        wait(s, "scalar", "sT", TG(t, g))
                        if t >= 2:
                            wait(s, "scalar", "sT", DG(t - 2, g))
                        s.copy(hqT[t % 2][:, g * 512:(g + 1) * 512],
                               pbk[g % 4][:, :])
                        tok = dinc(s, "sA")
                        assert tok == EV(t, g)
                    wait(s, "scalar", "sV", Y3(t))
                    stok[t] = dma(s, sds[t % 2],
                                  out_dst[t * 128:(t + 1) * 128, :],
                                  y3sb[t % 2][:])

            @blk.vector
            def _(v):
                for t in range(n_t):
                    dnb = (pbk[4], pbk[5]) if t % 2 == 0 else (pbk[6], pbk[7])
                    wait(v, "vector", "sT", DG(t, 7))
                    wait_t(v, "vector", tok_res[t])
                    if t >= 2:
                        wait_t(v, "vector", stok[t - 2])
                    v.tensor_scalar(sc("colt"), hrecip_all[:, t:t + 1],
                                    sdh_col[:, :], None, ALU.mult)
                    v.drain()
                    v.scalar_tensor_tensor(y3sb[t % 2][:, 0:512], dnb[0][:, :],
                                           sc("colt"), resa[t % 2][:, 0:512],
                                           ALU.mult, ALU.add)
                    v.scalar_tensor_tensor(y3sb[t % 2][:, 512:1024],
                                           dnb[1][:, :], sc("colt"),
                                           resa[t % 2][:, 512:1024],
                                           ALU.mult, ALU.add)
                    tok = inc(v.drain(), "sV")
                    assert tok == Y3(t)

            @blk.tensor
            def _(te):
                for t in range(n_t):
                    dnb = (pbk[4], pbk[5]) if t % 2 == 0 else (pbk[6], pbk[7])

                    def tgrp(g, t=t):
                        if t >= 1:
                            wait(te, "tensor", "sA", EV(t - 1, 4 + (g % 4)))
                        last = None
                        for k in range(4):
                            c = g * 4 + k
                            last = te.matmul(
                                pbk[g % 4][:, k * 128:(k + 1) * 128],
                                hqa[t % 2][:, c * 128:(c + 1) * 128],
                                ident_bf[:, :], start=(k == 0), stop=(k == 3))
                        tok = inc(last, "sT")
                        assert tok == TG(t, g)

                    def dgrp(dg, t=t, dnb=dnb):
                        wait(te, "tensor", "sA", EV(t, dg))
                        if dg == 0 and t >= 2:
                            wait(te, "tensor", "sV", Y3(t - 2))
                        last = None
                        for k in range(4):
                            ci = dg * 4 + k
                            te.matmul(dnb[0][:, :],
                                      hqT[t % 2][:, ci * 128:(ci + 1) * 128],
                                      wd_sb[:, ci * 1024:ci * 1024 + 512],
                                      start=(ci == 0), stop=(ci == 31))
                            last = te.matmul(
                                dnb[1][:, :],
                                hqT[t % 2][:, ci * 128:(ci + 1) * 128],
                                wd_sb[:, ci * 1024 + 512:ci * 1024 + 1024],
                                start=(ci == 0), stop=(ci == 31))
                        tok = inc(last, "sT")
                        assert tok == DG(t, dg)

                    wait_t(te, "tensor", tok_hq[t])
                    for g in range(4):
                        tgrp(g)
                    for g in range(4):
                        dgrp(g)
                        tgrp(4 + g)
                    for dg in range(4, 8):
                        dgrp(dg)

            assert ctr["sT"] == T0d + 16 * n_t
            assert ctr["sA"] == A0d + 8 * n_t
            assert ctr["sV"] == V0d + n_t

        pd.close()

    def p1b_pre(wd_sb):
        emit_wquant([wdt], [wd_sb], [dt_scr], [2], 32, 1024)

    if "p1b" not in phases:
        return finish()
    run_down_phase(p1b_pre, hq_scr, x, y1_scr, final=False)

    if "p2a" not in phases:
        return finish()
    p2 = contextlib.ExitStack()
    wq_sb = p2.enter_context(nc.sbuf_tensor("wq_sb", [128, DC * D], BF16))
    wk_sb = p2.enter_context(nc.sbuf_tensor("wk_sb", [128, DC * D], BF16))
    wv_sb = p2.enter_context(nc.sbuf_tensor("wv_sb", [128, DC * D], BF16))

    pre = contextlib.ExitStack()
    wtmp2 = [pre.enter_context(nc.sbuf_tensor(_nm("w2"), [128, D], F32))
             for _ in range(2)]
    jobs2 = [(w, dst, i) for (w, dst) in ((wq, wq_sb), (wk, wk_sb), (wv, wv_sb))
             for i in range(8)]
    sp2w = dma_pair("p2w")
    d2, v2base = [], ctr["sV"]
    with nc.Block() as blk:
        @blk.sync
        def _(sy):
            wait_stores(sy, "sync")
            for j, (w, dst, i) in enumerate(jobs2):
                if j >= 2:
                    wait(sy, "sync", "sV", v2base + j - 1)
                d2.append(dma(sy, sp2w[j % 2], wtmp2[j % 2][:],
                              w[i * 128:(i + 1) * 128, :]))

        @blk.vector
        def _(v):
            for j, (w, dst, i) in enumerate(jobs2):
                wait_t(v, "vector", d2[j])
                v.tensor_copy(dst[:, i * D:(i + 1) * D], wtmp2[j % 2][:])
                dinc(v, "sV")

    sp2o = dma_one("p2o")
    with nc.Block() as blk:
        @blk.sync
        def _(sy):
            dma(sy, sp2o, wtmp2[0][0:64, 0:M], omt)
            dma(sy, sp2o, wtmp2[0][64:128, 0:M], omt)

        @blk.vector
        def _(v):
            wait(v, "vector", sp2o, ctr[sp2o])
            v.tensor_copy(omdup_bf[:, :], wtmp2[0][:, 0:M])
            v.drain()
    pre.close()

    ya = [p2.enter_context(nc.sbuf_tensor(f"ya{i}", [128, D], F32))
          for i in range(2)]
    lnb_sb = p2.enter_context(nc.sbuf_tensor("lnb_sb", [128, D], BF16))
    lnT_sb = p2.enter_context(nc.sbuf_tensor("lnT_sb", [128, D], BF16))
    k_f = p2.enter_context(nc.sbuf_tensor("k_f", [128, D], F32))
    q_f = p2.enter_context(nc.sbuf_tensor("q_f", [128, D], F32))
    k_b = p2.enter_context(nc.sbuf_tensor("k_b", [128, D], BF16))
    k2_b = p2.enter_context(nc.sbuf_tensor("k2_b", [128, D], BF16))
    v_b = p2.enter_context(nc.sbuf_tensor("v_b", [128, D], BF16))
    pk_b = p2.enter_context(nc.sbuf_tensor("pk_b", [128, H * M], BF16))
    qb_sb = [p2.enter_context(nc.sbuf_tensor(f"qb{i}", [128, D], BF16))
             for i in range(2)]
    q2_sb = [p2.enter_context(nc.sbuf_tensor(f"q2{i}", [128, D], BF16))
             for i in range(2)]

    sy1in = dma_pair("y1in")
    sstq = dma_pair("stq", store=True)
    sstq2 = dma_pair("stq2", store=True)
    tok_y1 = [None] * (n_t + 1)
    stok_q = [None] * n_t
    stok_q2 = [None] * n_t

    with nc.Block() as blk:
        @blk.sync
        def _(sy):
            wait_stores(sy, "sync")
            tok_y1[0] = dma(sy, sy1in[0], ya[0][:], y1_scr[0:128, :])

    for t in range(n_t):
        tb, ab, vb = ctr["sT"], ctr["sA"], ctr["sV"]
        with nc.Block() as blk:
            @blk.sync
            def _(sy, t=t):
                if t + 1 < n_t:
                    tok_y1[t + 1] = dma(sy, sy1in[(t + 1) % 2],
                                        ya[(t + 1) % 2][:],
                                        y1_scr[(t + 1) * 128:(t + 2) * 128, :])

            @blk.scalar
            def _(s, t=t, tb=tb, vb=vb):
                wait_t(s, "scalar", tok_y1[t])
                emit_stats_A(s, ya[t % 2][:], lnb_sb[:], vb + 1, vb + 2)
                wait(s, "scalar", "sT", tb + 1)
                s.copy(lnT_sb[:, 0:512], pbk[5][:, :])
                s.copy(lnT_sb[:, 512:1024], pbk[6][:, :])
                dinc(s, "sA")
                wait(s, "scalar", "sT", tb + 2)
                s.copy(k_f[:, 0:512], pbk[5][:, :])
                s.copy(k_f[:, 512:1024], pbk[6][:, :])
                dinc(s, "sA")
                s.activation(k2_b[:, :], k_f[:, :], AF.Square)
                dinc(s, "sA")
                wait(s, "scalar", "sT", tb + 4)
                s.copy(q_f[:, 0:512], pbk[5][:, :])
                s.copy(q_f[:, 512:1024], pbk[6][:, :])
                dinc(s, "sA")
                if t >= 2:
                    wait_t(s, "scalar", stok_q2[t - 2])
                s.activation(q2_sb[t % 2][:, :], q_f[:, :], AF.Square)
                dinc(s, "sA")
                for hh in range(H):
                    wait(s, "scalar", "sT", tb + 5 + hh)
                    s.activation(
                        pk_b[:, hh * M:(hh + 1) * M],
                        pbk[5 + (hh % 2)][:, 0:M], AF.Exp)
                    dinc(s, "sA")
                wait(s, "scalar", "sV", vb + 5)
                stok_q[t] = dma(s, sstq[t % 2],
                                qb_scr[t * 128:(t + 1) * 128, :], qb_sb[t % 2][:])
                stok_q2[t] = dma(s, sstq2[t % 2],
                                 q2_scr[t * 128:(t + 1) * 128, :],
                                 q2_sb[t % 2][:])

            @blk.vector
            def _(v, t=t, tb=tb, ab=ab):
                emit_stats_V(v, ab + 1, ab + 2)
                wait(v, "vector", "sA", ab + 5)
                v.tensor_copy(k_b[:, :], k_f[:, :])
                dinc(v, "sV")
                wait(v, "vector", "sT", tb + 3)
                v.tensor_copy(v_b[:, 0:512], pbk[5][:, :])
                v.tensor_copy(v_b[:, 512:1024], pbk[6][:, :])
                dinc(v, "sV")
                wait(v, "vector", "sA", ab + 7)
                if t >= 2:
                    wait_t(v, "vector", stok_q[t - 2])
                v.tensor_copy(qb_sb[t % 2][:, :], q_f[:, :])
                inc(v.drain(), "sV")

            @blk.tensor
            def _(te, t=t, ab=ab, vb=vb):
                wait(te, "tensor", "sA", ab + 3)
                inc(emit_transposes(te, lnb_sb[:, :], [pbk[5], pbk[6]], 8), "sT")
                wait(te, "tensor", "sA", ab + 4)
                last = None
                for co in range(8):
                    bank = pbk[5] if co < 4 else pbk[6]
                    sl = bank[:, (co % 4) * 128:(co % 4) * 128 + 128]
                    for cd in range(DC):
                        last = te.matmul(
                            sl, wk_sb[:, cd * D + co * 128:cd * D + co * 128 + 128],
                            lnT_sb[:, cd * 128:(cd + 1) * 128],
                            start=(co % 4 == 0 and cd == 0),
                            stop=(co % 4 == 3 and cd == DC - 1))
                inc(last, "sT")
                wait(te, "tensor", "sA", ab + 5)
                last = None
                for half in range(2):
                    bank = pbk[5] if half == 0 else pbk[6]
                    for cd in range(DC):
                        last = te.matmul(
                            bank[:, :], lnT_sb[:, cd * 128:(cd + 1) * 128],
                            wv_sb[:, cd * D + half * 512:cd * D + half * 512 + 512],
                            start=(cd == 0), stop=(cd == DC - 1))
                inc(last, "sT")
                wait(te, "tensor", "sV", vb + 4)
                last = None
                for co in range(8):
                    bank = pbk[5] if co < 4 else pbk[6]
                    sl = bank[:, (co % 4) * 128:(co % 4) * 128 + 128]
                    for cd in range(DC):
                        last = te.matmul(
                            sl, wq_sb[:, cd * D + co * 128:cd * D + co * 128 + 128],
                            lnT_sb[:, cd * 128:(cd + 1) * 128],
                            start=(co % 4 == 0 and cd == 0),
                            stop=(co % 4 == 3 and cd == DC - 1))
                inc(last, "sT")
                wait(te, "tensor", "sV", vb + 3)
                wait(te, "tensor", "sA", ab + 7)
                for hh in range(H):
                    if hh >= 2:
                        wait(te, "tensor", "sA", ab + 9 + (hh - 2))
                    par = hh % 2
                    sl = pbk[5 + par][:, 0:M]
                    ksl = k_b[par * 64:par * 64 + 64,
                              (hh // 2) * 128:(hh // 2) * 128 + 128]
                    k2sl = k2_b[par * 64:par * 64 + 64,
                                (hh // 2) * 128:(hh // 2) * 128 + 128]
                    te.matmul(sl, ksl, omdup_bf[par * 64:par * 64 + 64, :],
                              start=True, stop=False)
                    inc(te.matmul(sl, k2sl, neghalf_bf[par * 64:par * 64 + 64, :],
                                  start=False, stop=True), "sT")
                last = None
                for hh in range(H):
                    wait(te, "tensor", "sA", ab + 9 + hh)
                    hl = hh - 8 if hh >= 8 else hh
                    for mc in range(2):
                        kvb = pbk[mc * 2 + (1 if hh >= 8 else 0)]
                        lhs = pk_b[:, hh * M + mc * 128:hh * M + mc * 128 + 128]
                        te.matmul(kvb[:, hl * 64:hl * 64 + 64], lhs,
                                  v_b[:, hh * 64:hh * 64 + 64],
                                  start=(t == 0 and hl == 0),
                                  stop=(t == n_t - 1 and hl == 7))
                        last = te.matmul(pbk[4][:, mc * 16 + hh:mc * 16 + hh + 1],
                                         lhs, onesbf_col[:, :],
                                         start=(t == 0 and hh == 0 and mc == 0),
                                         stop=(t == n_t - 1 and hh == H - 1
                                               and mc == 1))
                inc(last, "sT")

        assert ctr["sT"] == tb + 21, (ctr["sT"], tb)
        assert ctr["sA"] == ab + 8 + H
        assert ctr["sV"] == vb + 5

    with nc.Block() as blk:
        @blk.vector
        def _(v):
            for mc in range(2):
                for hh in range(H):
                    kvb = pbk[mc * 2 + (1 if hh >= 8 else 0)]
                    hl = hh - 8 if hh >= 8 else hh
                    base = mc * (H * 65) + hh * 65
                    v.tensor_copy(kvaug_bf[:, base:base + 64],
                                  kvb[:, hl * 64:hl * 64 + 64])
                    v.tensor_copy(kvaug_bf[:, base + 64:base + 65],
                                  pbk[4][:, mc * 16 + hh:mc * 16 + hh + 1])
            v.drain()
    p2.close()

    if "p3a" not in phases:
        return finish()
    p3 = contextlib.ExitStack()
    wg3_sb = p3.enter_context(nc.sbuf_tensor("wg3_sb", [128, DC * INNER], BF16))
    wu3_sb = p3.enter_context(nc.sbuf_tensor("wu3_sb", [128, DC * INNER], BF16))
    stern = dma_one("tern")
    with nc.Block() as blk:
        @blk.sync
        def _(sy):
            wait_stores(sy, "sync")
            for i in range(8):
                dma(sy, stern, wg3_sb[:, i * INNER:(i + 1) * INNER],
                    gt_scr[i * 128:(i + 1) * 128, :])
            for i in range(8):
                dma(sy, stern, wu3_sb[:, i * INNER:(i + 1) * INNER],
                    ut_scr[i * 128:(i + 1) * 128, :])

    y1a = [p3.enter_context(nc.sbuf_tensor(f"y1a{i}", [128, D], F32))
           for i in range(2)]
    qba = [p3.enter_context(nc.sbuf_tensor(f"qba{i}", [128, D], BF16))
           for i in range(2)]
    q2a = [p3.enter_context(nc.sbuf_tensor(f"q2a{i}", [128, D], BF16))
           for i in range(2)]
    pq_b = p3.enter_context(nc.sbuf_tensor("pq_b", [128, 32 * 128], BF16))
    y2_sb = [p3.enter_context(nc.sbuf_tensor(f"y2sb{i}", [128, D], F32))
             for i in range(2)]
    lnf3 = p3.enter_context(nc.sbuf_tensor("lnf3", [128, D], F32))
    aqi3 = p3.enter_context(nc.sbuf_tensor("aqi3", [128, D], I32))
    aqb3 = p3.enter_context(nc.sbuf_tensor("aqb3", [128, D], BF16))
    aqT3 = p3.enter_context(nc.sbuf_tensor("aqT3", [128, D], BF16))
    silu3 = [p3.enter_context(nc.sbuf_tensor(f"silu3{i}", [128, 512], F32))
             for i in range(2)]
    h3_sb = p3.enter_context(nc.sbuf_tensor("h3_sb", [128, INNER], F32))
    hqi3 = p3.enter_context(nc.sbuf_tensor("hqi3", [128, 2048], I32))
    hqb3 = p3.enter_context(nc.sbuf_tensor("hqb3", [128, INNER], BF16))

    sp3in = dma_pair("p3in")
    ssty2 = dma_pair("sty2", store=True)
    ssth2 = dma_pair("sth2", store=True)
    tok3 = [None] * (n_t + 1)
    stok_h2 = [None] * n_t
    stok_y2 = [None] * n_t

    def p3_loads(sy, tt):
        dma(sy, sp3in[tt % 2], y1a[tt % 2][:],
            y1_scr[tt * 128:(tt + 1) * 128, :])
        dma(sy, sp3in[tt % 2], qba[tt % 2][:],
            qb_scr[tt * 128:(tt + 1) * 128, :])
        tok3[tt] = dma(sy, sp3in[tt % 2], q2a[tt % 2][:],
                       q2_scr[tt * 128:(tt + 1) * 128, :])

    with nc.Block() as blk:
        @blk.sync
        def _(sy):
            p3_loads(sy, 0)

    ATT_BANK = [4] * 7 + [5] * 7 + [6] * 2
    ATT_SLOT = list(range(7)) + list(range(7)) + list(range(2))

    for t in range(n_t):
        tb, ab, vb, pbs = ctr["sT"], ctr["sA"], ctr["sV"], ctr["sP"]
        with nc.Block() as blk:
            @blk.sync
            def _(sy, t=t):
                if t + 1 < n_t:
                    p3_loads(sy, t + 1)

            @blk.scalar
            def _(s, t=t, tb=tb, vb=vb, pbs=pbs):
                for g in range(4):
                    wait(s, "scalar", "sT", tb + 1 + g)
                    bA = pbk[0] if g % 2 == 0 else pbk[2]
                    bB = pbk[1] if g % 2 == 0 else pbk[3]
                    s.activation(pq_b[:, g * 1024:g * 1024 + 512], bA[:, :],
                                 AF.Exp)
                    s.activation(pq_b[:, g * 1024 + 512:(g + 1) * 1024],
                                 bB[:, :], AF.Exp)
                    dinc(s, "sA")
                wait(s, "scalar", "sV", vb + 1)
                emit_stats_A(s, y2_sb[t % 2][:], lnf3[:], vb + 2, vb + 3)
                wait(s, "scalar", "sT", tb + 6)
                s.copy(aqT3[:, 0:512], pbk[0][:, :])
                s.copy(aqT3[:, 512:1024], pbk[1][:, :])
                dinc(s, "sA")
                wait(s, "scalar", "sV", vb + 4)
                for ib in range(NIB):
                    wait(s, "scalar", "sT", tb + 7 + ib)
                    if ib >= 2:
                        wait(s, "scalar", "sV", vb + 6 + (ib - 2))
                    s.activation(silu3[ib % 2][:],
                                 pbk[4 + 2 * (ib % 2)][:, :], AF.Silu,
                                 scale=sc("colg"))
                    dinc(s, "sA")
                wait(s, "scalar", "sV", vb + 1)
                stok_y2[t] = dma(s, ssty2[t % 2],
                                 y2_scr[t * 128:(t + 1) * 128, :],
                                 y2_sb[t % 2][:])
                wait(s, "scalar", "sV", vb + 14)
                if t >= 1:
                    wait_t(s, "scalar", stok_h2[t - 1])
                inc(s.activation(hqb3[:, 0:2048], hqi3[:], AF.Identity),
                    "sA")
                wait(s, "scalar", "sV", vb + 15)
                s.activation(hqb3[:, 2048:4096], hqi3[:], AF.Identity)
                dinc(s, "sA")
                stok_h2[t] = dma(s, ssth2[t % 2],
                                 hq_scr[t * 128:(t + 1) * 128, :], hqb3[:])

            @blk.vector
            def _(v, t=t, tb=tb, ab=ab, pbs=pbs):
                wait(v, "vector", "sT", tb + 5)
                wait_t(v, "vector", tok3[t])
                if t >= 2:
                    wait_t(v, "vector", stok_y2[t - 2])
                for hh in range(H):
                    v.tensor_copy(zcol[:, hh:hh + 1],
                                  pbk[ATT_BANK[hh]][:, ATT_SLOT[hh] * 65 + 64:
                                                    ATT_SLOT[hh] * 65 + 65])
                v.drain()
                v.tensor_scalar(zcol[:, :], zcol[:, :], 2.56e-4, None, ALU.add)
                v.drain()
                v.reciprocal(zcol[:, :], zcol[:, :])
                v.drain()
                for hh in range(H):
                    v.scalar_tensor_tensor(
                        y2_sb[t % 2][:, hh * 64:hh * 64 + 64],
                        pbk[ATT_BANK[hh]][:, ATT_SLOT[hh] * 65:
                                          ATT_SLOT[hh] * 65 + 64],
                        zcol[:, hh:hh + 1], y1a[t % 2][:, hh * 64:hh * 64 + 64],
                        ALU.mult, ALU.add)
                inc(v.drain(), "sV")
                emit_stats_V(v, ab + 5, ab + 6)
                wait(v, "vector", "sA", ab + 7)
                emit_quant_v(v, lnf3[:], aqi3[:], sc("aqrec"),
                             (sc("colg"), sc("colu")))
                v.tensor_copy(aqb3[:], aqi3[:])
                dinc(v, "sV")
                for ib in range(NIB):
                    wait(v, "vector", "sT", tb + 7 + ib)
                    wait(v, "vector", "sA", ab + 9 + ib)
                    inc(v.scalar_tensor_tensor(
                        h3_sb[:, ib * 512:(ib + 1) * 512],
                        pbk[5 + 2 * (ib % 2)][:, :], sc("colu"),
                        silu3[ib % 2][:], ALU.mult, ALU.mult), "sV")
                v.drain()
                v.reduce_max(sc("hamax"), h3_sb[:], AX.X,
                             apply_absolute_value=True)
                v.drain()
                v.tensor_scalar(sc("amc"), sc("hamax"), sc("eps"), None, ALU.max)
                v.drain()
                v.reciprocal(sc("rc"), sc("amc"))
                v.drain()
                v.tensor_scalar(sc("hqs"), sc("rc"), 127.0, None, ALU.mult)
                v.drain()
                v.reciprocal(hrecip_all[:, t:t + 1], sc("hqs"))
                v.drain()
                v.tensor_scalar(hqi3[:], h3_sb[:, 0:2048], sc("hqs"), None,
                                ALU.mult)
                dinc(v, "sV")
                wait(v, "vector", "sA", ab + 17)
                v.tensor_scalar(hqi3[:], h3_sb[:, 2048:4096], sc("hqs"), None,
                                ALU.mult)
                dinc(v, "sV")

            @blk.tensor
            def _(te, t=t, ab=ab, vb=vb, pbs=pbs):
                wait_t(te, "tensor", tok3[t])
                if t == 0:
                    wait(te, "tensor", stern, ctr[stern])
                for g in range(4):
                    mc, par = g // 2, g % 2
                    bA = pbk[0] if g % 2 == 0 else pbk[2]
                    bB = pbk[1] if g % 2 == 0 else pbk[3]
                    if g >= 2:
                        wait(te, "tensor", "sA", ab + 1 + (g - 2))
                    oms = omdup_bf[par * 64:par * 64 + 64,
                                   mc * 128:mc * 128 + 128]
                    ngs = neghalf_bf[par * 64:par * 64 + 64, 0:128]
                    last = None
                    for c in range(8):
                        bank = bA if c < 4 else bB
                        sl = bank[:, (c % 4) * 128:(c % 4) * 128 + 128]
                        qsl = qba[t % 2][par * 64:par * 64 + 64,
                                         c * 128:(c + 1) * 128]
                        q2sl = q2a[t % 2][par * 64:par * 64 + 64,
                                          c * 128:(c + 1) * 128]
                        te.matmul(sl, oms, qsl, start=(c % 4 == 0), stop=False)
                        last = te.matmul(sl, ngs, q2sl, start=False,
                                         stop=(c % 4 == 3))
                    inc(last, "sT")
                wait(te, "tensor", "sA", ab + 4)
                last = None
                bank_started = set()
                for hh in range(H):
                    bk, slt = ATT_BANK[hh], ATT_SLOT[hh]
                    for mc in range(2):
                        g = mc * 2 + (hh % 2)
                        j = g * 8 + hh // 2
                        first = bk not in bank_started
                        bank_started.add(bk)
                        last = te.matmul(
                            pbk[bk][:, slt * 65:slt * 65 + 65],
                            pq_b[:, j * 128:(j + 1) * 128],
                            kvaug_bf[:, mc * (H * 65) + hh * 65:
                                     mc * (H * 65) + (hh + 1) * 65],
                            start=first,
                            stop=(mc == 1 and hh in (6, 13, 15)))
                inc(last, "sT")
                wait(te, "tensor", "sV", vb + 5)
                wait(te, "tensor", "sA", ab + 4)
                inc(emit_transposes(te, aqb3[:, :], [pbk[0], pbk[1]], 8), "sT")
                wait(te, "tensor", "sA", ab + 8)
                for ib in range(NIB):
                    if ib >= 2:
                        wait(te, "tensor", "sV", vb + 6 + (ib - 2))
                    else:
                        wait(te, "tensor", "sV", vb + 1)
                    gb = pbk[4 + 2 * (ib % 2)]
                    ub = pbk[5 + 2 * (ib % 2)]
                    for c in range(DC):
                        te.matmul(gb[:, :], aqT3[:, c * 128:(c + 1) * 128],
                                  wg3_sb[:, c * INNER + ib * 512:
                                         c * INNER + ib * 512 + 512],
                                  start=(c == 0), stop=(c == DC - 1))
                    last = None
                    for c in range(DC):
                        last = te.matmul(ub[:, :], aqT3[:, c * 128:(c + 1) * 128],
                                         wu3_sb[:, c * INNER + ib * 512:
                                                c * INNER + ib * 512 + 512],
                                         start=(c == 0), stop=(c == DC - 1))
                    inc(last, "sT")

        assert ctr["sT"] == tb + 6 + NIB, (ctr["sT"], tb)
        assert ctr["sA"] == ab + 10 + NIB
        assert ctr["sV"] == vb + 7 + NIB
    p3.close()

    if "p3b" not in phases:
        return finish()
    swdld = dma_one("wdld")

    def p3b_pre(wd_sb):
        with nc.Block() as blk:
            @blk.sync
            def _(sy):
                wait_stores(sy, "sync")
                for ci in range(32):
                    dma(sy, swdld, wd_sb[:, ci * 1024:(ci + 1) * 1024],
                        dt_scr[ci * 128:(ci + 1) * 128, :])
                wait(sy, "sync", swdld, ctr[swdld])

    run_down_phase(p3b_pre, hq_scr, y2_scr, y, final=True)

    return finish()


_NC_CACHE = {}


def _get_program():
    if "nc" not in _NC_CACHE:
        _NC_CACHE["nc"] = build_program()
    return _NC_CACHE["nc"]


def kernel(L0_attn_x, L0_attn_Wq, L0_attn_Wk, L0_attn_Wv, L0_attn_omega,
           L0_ffn_w_gate, L0_ffn_w_up, L0_ffn_w_down):
    xs = np.ascontiguousarray(np.asarray(L0_attn_x, dtype=np.float32))
    wq_ = np.ascontiguousarray(np.asarray(L0_attn_Wq, dtype=np.float32))
    wk_ = np.ascontiguousarray(np.asarray(L0_attn_Wk, dtype=np.float32))
    wv_ = np.ascontiguousarray(np.asarray(L0_attn_Wv, dtype=np.float32))
    omt_ = np.ascontiguousarray(np.asarray(L0_attn_omega, dtype=np.float32).T)
    wgt_ = np.ascontiguousarray(np.asarray(L0_ffn_w_gate, dtype=np.float32).T)
    wut_ = np.ascontiguousarray(np.asarray(L0_ffn_w_up, dtype=np.float32).T)
    wdt_ = np.ascontiguousarray(np.asarray(L0_ffn_w_down, dtype=np.float32).T)

    nc = _get_program()
    in_maps = [{"x": np.ascontiguousarray(xs[c]), "wq": wq_, "wk": wk_,
                "wv": wv_, "omt": omt_, "wgt": wgt_, "wut": wut_, "wdt": wdt_}
               for c in range(8)]
    res = run_bass_kernel_spmd(nc, in_maps, core_ids=list(range(8)))
    return np.stack([np.asarray(res.results[c]["y"], dtype=np.float32)
                     for c in range(8)], axis=0)


if __name__ == "__main__":
    print("building...")
    nc = build_program()
    print("instructions:", nc.n_instructions())


# version 06
# speedup vs baseline: 1.1371x; 1.0596x over previous
import contextlib
import numpy as np
import concourse.bass as bass
import concourse.mybir as mybir
from concourse.bass_utils import run_bass_kernel_spmd

F32 = mybir.dt.float32
BF16 = mybir.dt.bfloat16
I32 = mybir.dt.int32
AF = mybir.ActivationFunctionType
ALU = mybir.AluOpType
AX = mybir.AxisListType

N, D, INNER, M, H, HD = 4096, 1024, 4096, 256, 16, 64
TILES, DC, NIB = 32, 8, 8

_SC = ["sum", "ssq", "mean", "m2", "nm2", "var", "lnv", "rstd", "mr", "nmr",
       "amax", "amc", "rc", "qs", "aqrec", "colg", "colu", "hamax", "hqs",
       "colt", "c1024inv", "cneg1", "cneghalf", "eps", "srt"]
SCOL = {n: i for i, n in enumerate(_SC)}


def build_program(dbg=(), phases=("p0", "p1a", "p1b", "p2a", "p3a", "p3b"),
                  n_tiles=None):
    n_t = n_tiles or TILES
    n_rows = n_t * 128
    nc = bass.Bass("TRN2", target_bir_lowering=False, dynamic_dma_scratch_size=512)

    x = nc.dram_tensor("x", [n_rows, D], F32, kind="ExternalInput").ap()
    wq = nc.dram_tensor("wq", [D, D], F32, kind="ExternalInput").ap()
    wk = nc.dram_tensor("wk", [D, D], F32, kind="ExternalInput").ap()
    wv = nc.dram_tensor("wv", [D, D], F32, kind="ExternalInput").ap()
    omt = nc.dram_tensor("omt", [HD, M], F32, kind="ExternalInput").ap()
    wgt = nc.dram_tensor("wgt", [D, INNER], F32, kind="ExternalInput").ap()
    wut = nc.dram_tensor("wut", [D, INNER], F32, kind="ExternalInput").ap()
    wdt = nc.dram_tensor("wdt", [INNER, D], F32, kind="ExternalInput").ap()
    y = nc.dram_tensor("y", [n_rows, D], F32, kind="ExternalOutput").ap()

    def scr(name, shape, dt):
        kind = "ExternalOutput" if name in dbg else "Internal"
        return nc.dram_tensor(name, shape, dt, kind=kind).ap()

    hq_scr = scr("hq_scr", [n_rows, INNER], BF16)
    y1_scr = scr("y1_scr", [n_rows, D], F32)
    y2_scr = scr("y2_scr", [n_rows, D], F32)
    qb_scr = scr("qb_scr", [n_rows, D], BF16)
    q2_scr = scr("q2_scr", [n_rows, D], BF16)
    gt_scr = scr("gt_scr", [D, INNER], BF16)
    ut_scr = scr("ut_scr", [D, INNER], BF16)
    dt_scr = scr("dt_scr", [INNER, D], BF16)

    sems, ctr = {}, {}

    def new_sem(nm):
        sems[nm] = nc.ctx.enter_context(nc.semaphore(nm))
        ctr[nm] = 0
        return nm

    for nm in ("sA", "sV", "sP", "sT"):
        new_sem(nm)
    WAITED = {e: {} for e in ("sync", "scalar", "vector", "gpsimd", "tensor")}
    store_sems = []

    def dma_pair(stream, store=False):
        pair = []
        for i in range(2):
            nm = new_sem(f"d{stream}{i}")
            pair.append(nm)
            if store:
                store_sems.append(nm)
        return pair

    def dma_one(stream, store=False):
        nm = new_sem(f"d{stream}")
        if store:
            store_sems.append(nm)
        return nm

    def inc(instr, sem, n=1):
        ctr[sem] += n
        instr.then_inc(sems[sem], n)
        return ctr[sem]

    def dma(eng, sem, out, in_):
        v = inc(eng.dma_start(out=out, in_=in_), sem, 16)
        return (sem, v)

    def wait(eng, ename, sem, val):
        if val and WAITED[ename].get(sem, 0) < val:
            eng.wait_ge(sems[sem], val)
            WAITED[ename][sem] = val

    def wait_t(eng, ename, tok):
        if tok is not None:
            wait(eng, ename, tok[0], tok[1])

    def wait_stores(eng, ename):
        for s in store_sems:
            wait(eng, ename, s, ctr[s])

    def dinc(eng, sem, n=1):
        return inc(eng.drain(), sem, n)

    cm = nc.ctx.enter_context
    ident_bf = cm(nc.sbuf_tensor("ident_bf", [128, 128], BF16))
    omdup_bf = cm(nc.sbuf_tensor("omdup_bf", [128, M], BF16))
    neghalf_bf = cm(nc.sbuf_tensor("neghalf_bf", [128, M], BF16))
    onesbf_col = cm(nc.sbuf_tensor("onesbf_col", [128, 1], BF16))
    onesf_col = cm(nc.sbuf_tensor("onesf_col", [128, 1], F32))
    onesrow = cm(nc.sbuf_tensor("onesrow", [1, 128], F32))
    scols = cm(nc.sbuf_tensor("scols", [128, 64], F32))
    wcols = cm(nc.sbuf_tensor("wcols", [128, 8], F32))
    sdh_col = cm(nc.sbuf_tensor("sdh_col", [128, 1], F32))
    hrecip_all = cm(nc.sbuf_tensor("hrecip_all", [128, TILES], F32))
    kvaug_bf = cm(nc.sbuf_tensor("kvaug_bf", [128, 2 * H * 65], BF16))
    sqjunk = cm(nc.sbuf_tensor("sqjunk", [128, D], BF16))
    sqjunk2 = cm(nc.sbuf_tensor("sqjunk2", [128, D], BF16))
    zcol = cm(nc.sbuf_tensor("zcol", [128, H], F32))
    pbk = [cm(nc.psum_tensor(f"pb{i}", [128, 512], F32)) for i in range(8)]

    def sc(nm, par=0):
        i = SCOL[nm] + 32 * (par % 2)
        return scols[:, i:i + 1]

    _uid = [0]

    def _nm(p):
        _uid[0] += 1
        return f"{p}{_uid[0]}"

    def finish():
        with nc.Block() as blk:
            @blk.gpsimd
            def _(g):
                wait_stores(g, "gpsimd")
        return nc

    p0 = contextlib.ExitStack()
    wtmp0 = [p0.enter_context(nc.sbuf_tensor(f"p0w{i}", [128, INNER], F32))
             for i in range(2)]
    accs = p0.enter_context(nc.sbuf_tensor("accs", [128, 28], F32))

    with nc.Block() as blk:
        @blk.vector
        def _(v):
            v.memset(scols[:], 0.0)
            v.memset(onesf_col[:], 1.0)
            v.memset(onesrow[:], 1.0)
            v.memset(neghalf_bf[:], -0.5)
            v.memset(kvaug_bf[:], 0.0)
            v.memset(zcol[:], 0.0)

        @blk.gpsimd
        def _(g):
            g.memset(onesbf_col[:], 1.0)
            g.memset(sqjunk[:], 0.0)
            g.memset(accs[:], 0.0)
            g.iota(ident_bf[:, :], [[1, 128]], base=0, channel_multiplier=-1,
                   allow_small_or_imprecise_dtypes=True)

    with nc.Block() as blk:
        @blk.vector
        def _(v):
            v.tensor_scalar(ident_bf[:, :], ident_bf[:, :], 0.0, None, ALU.is_equal)

        @blk.gpsimd
        def _(g):
            for par in range(2):
                g.memset(sc("c1024inv", par), 1.0 / 1024.0)
                g.memset(sc("cneg1", par), -1.0)
                g.memset(sc("cneghalf", par), -0.5)
                g.memset(sc("eps", par), 1e-5)

    chunks = []
    for mi, w in enumerate((wgt, wut)):
        for i in range(8):
            chunks.append((mi, i, w[i * 128:(i + 1) * 128, :]))
    for i in range(8):
        chunks.append((2, i, wdt[i * 512:(i + 1) * 512, :]
                       .rearrange("(p r) n -> p (r n)", p=128)))
    sp0 = dma_pair("p0w")
    d0, v0base = [], ctr["sV"]
    with nc.Block() as blk:
        @blk.sync
        def _(sy):
            for j, (mi, i, ap) in enumerate(chunks):
                if j >= 2:
                    wait(sy, "sync", "sV", v0base + j - 1)
                d0.append(dma(sy, sp0[j % 2], wtmp0[j % 2][:], ap))

        @blk.vector
        def _(v):
            for j, (mi, i, ap) in enumerate(chunks):
                wait_t(v, "vector", d0[j])
                inc(v.reduce_sum(accs[:, mi * 8 + i:mi * 8 + i + 1],
                                 wtmp0[j % 2][:], AX.X,
                                 apply_absolute_value=True), "sV")

    v0tok = [None]
    with nc.Block() as blk:
        @blk.vector
        def _(v):
            v.drain()
            for mi in range(3):
                v.reduce_sum(accs[:, 24 + mi:25 + mi], accs[:, mi * 8:(mi + 1) * 8],
                             AX.X)
            v0tok[0] = dinc(v, "sV")

        @blk.tensor
        def _(t):
            wait(t, "tensor", "sV", v0tok[0])
            last = None
            for mi in range(3):
                last = t.matmul(pbk[0][0:1, mi:mi + 1], accs[:, 24 + mi:25 + mi],
                                onesf_col[:, :], start=(mi == 0), stop=(mi == 2))
            inc(last, "sT")

    t0a = ctr["sT"]
    with nc.Block() as blk:
        @blk.vector
        def _(v):
            wait(v, "vector", "sT", t0a)
            v.tensor_copy(accs[0:1, 0:3], pbk[0][0:1, 0:3])
            v.drain()
            v.tensor_scalar(accs[0:1, 4:7], accs[0:1, 0:3], 1.0 / 4194304.0,
                            1e-5, ALU.mult, ALU.max)
            v.drain()
            v.reciprocal(accs[0:1, 8:11], accs[0:1, 4:7])
            v0tok[0] = dinc(v, "sV")

        @blk.tensor
        def _(t):
            wait(t, "tensor", "sV", v0tok[0])
            last = None
            for j in range(6):
                src = accs[0:1, 4 + j:5 + j] if j < 3 else \
                    accs[0:1, 8 + j - 3:9 + j - 3]
                last = t.matmul(pbk[1][:, j:j + 1], onesrow[0:1, :], src,
                                start=(j == 0), stop=(j == 5))
            inc(last, "sT")

    t0b = ctr["sT"]
    with nc.Block() as blk:
        @blk.vector
        def _(v):
            wait(v, "vector", "sT", t0b)
            v.tensor_copy(wcols[:, 0:6], pbk[1][:, 0:6])
            v.drain()
            v.tensor_scalar(sdh_col[:, :], wcols[:, 2:3], 0.5, None, ALU.mult)
            v.drain()
    p0.close()

    def emit_stats_A(s, src_f, lnf_out, v_var_tok, v_nmr_tok, par=0):
        s.activation(sqjunk[:, :], src_f, AF.Copy, accum_out=sc("sum", par))
        s.activation(sqjunk2[:, :], src_f, AF.Square, accum_out=sc("ssq", par))
        dinc(s, "sA")
        wait(s, "scalar", "sV", v_var_tok)
        s.activation(sc("srt", par), sc("var", par), AF.Sqrt, bias=sc("eps", par))
        dinc(s, "sA")
        wait(s, "scalar", "sV", v_nmr_tok)
        s.activation(lnf_out, src_f, AF.Identity, scale=sc("rstd", par),
                     bias=sc("nmr", par))
        return dinc(s, "sA")

    def emit_stats_V(v, a_acc_tok, a_srt_tok, par=0):
        wait(v, "vector", "sA", a_acc_tok)
        v.tensor_scalar(sc("mean", par), sc("sum", par), 1.0 / 1024.0, None,
                        ALU.mult)
        v.drain()
        v.tensor_tensor(sc("m2", par), sc("mean", par), sc("mean", par),
                        ALU.mult)
        v.drain()
        v.scalar_tensor_tensor(sc("var", par), sc("ssq", par), 1.0 / 1024.0,
                               sc("m2", par), ALU.mult, ALU.subtract)
        t1 = dinc(v, "sV")
        wait(v, "vector", "sA", a_srt_tok)
        v.reciprocal(sc("rstd", par), sc("srt", par))
        v.drain()
        v.tensor_scalar(sc("nmr", par), sc("mean", par), sc("rstd", par), -1.0,
                        ALU.mult, ALU.mult)
        t2 = dinc(v, "sV")
        return t1, t2

    def emit_quant_v(v, src_f, aqi_out, rec_ap, colgu, par=0):
        v.reduce_max(sc("amax", par), src_f, AX.X, apply_absolute_value=True)
        v.drain()
        v.tensor_scalar(sc("amc", par), sc("amax", par), sc("eps", par), None,
                        ALU.max)
        v.drain()
        v.reciprocal(sc("rc", par), sc("amc", par))
        v.drain()
        v.tensor_scalar(sc("qs", par), sc("rc", par), 127.0, None, ALU.mult)
        v.drain()
        v.reciprocal(rec_ap, sc("qs", par))
        v.drain()
        if colgu:
            v.tensor_scalar(colgu[0], rec_ap, wcols[:, 0:1], None, ALU.mult)
            v.tensor_scalar(colgu[1], rec_ap, wcols[:, 1:2], None, ALU.mult)
        v.tensor_scalar(aqi_out, src_f, sc("qs", par), None, ALU.mult)
        return dinc(v, "sV")

    def emit_transposes(t, src_b, banks, n):
        last = None
        for c in range(n):
            bank = banks[(c // 4) % len(banks)]
            last = t.matmul(bank[:, (c % 4) * 128:(c % 4) * 128 + 128],
                            src_b[:, c * 128:(c + 1) * 128], ident_bf[:, :],
                            start=(c % 4 == 0), stop=(c % 4 == 3))
        return last

    def emit_wquant(wsrc_list, dst_sb_list, scr_list, wsi_list, nchunk, ccols):
        est = contextlib.ExitStack()
        wtmp = [est.enter_context(nc.sbuf_tensor(_nm("wqw"), [128, ccols], F32))
                for _ in range(2)]
        i32t = [est.enter_context(nc.sbuf_tensor(_nm("wqi"), [128, ccols], I32))
                for _ in range(2)]
        jobs = []
        for (ws, db, sd, wsi) in zip(wsrc_list, dst_sb_list, scr_list, wsi_list):
            for i in range(nchunk):
                jobs.append((ws, db, sd, wsi, i))
        spw = dma_pair(_nm("wql"))
        sps = dma_pair(_nm("wqs"), store=True)
        dt_ = []
        vbase = ctr["sV"]
        with nc.Block() as blk:
            @blk.sync
            def _(sy):
                wait_stores(sy, "sync")
                for j, (ws, db, sd, wsi, i) in enumerate(jobs):
                    if j >= 2:
                        wait(sy, "sync", "sV", vbase + 2 * (j - 1))
                    dt_.append(dma(sy, spw[j % 2], wtmp[j % 2][:],
                                   ws[i * 128:(i + 1) * 128, :]))

            @blk.scalar
            def _(s):
                for j, (ws, db, sd, wsi, i) in enumerate(jobs):
                    wait(s, "scalar", "sV", vbase + 2 * (j + 1))
                    dma(s, sps[j % 2], sd[i * 128:(i + 1) * 128, :],
                        db[:, i * ccols:(i + 1) * ccols])

            @blk.vector
            def _(v):
                for j, (ws, db, sd, wsi, i) in enumerate(jobs):
                    wait_t(v, "vector", dt_[j])
                    v.tensor_scalar(i32t[j % 2][:], wtmp[j % 2][:],
                                    wcols[:, 3 + wsi:4 + wsi], None, ALU.mult)
                    dinc(v, "sV")
                    v.tensor_scalar(db[:, i * ccols:(i + 1) * ccols],
                                    i32t[j % 2][:], -1.0, 1.0, ALU.max, ALU.min)
                    dinc(v, "sV")
        est.close()

    if "p1a" not in phases:
        return finish()
    p1 = contextlib.ExitStack()
    wg_sb = p1.enter_context(nc.sbuf_tensor("wg_sb", [128, DC * INNER], BF16))
    wu_sb = p1.enter_context(nc.sbuf_tensor("wu_sb", [128, DC * INNER], BF16))
    emit_wquant([wgt, wut], [wg_sb, wu_sb], [gt_scr, ut_scr], [0, 1], 8, INNER)

    xa = [p1.enter_context(nc.sbuf_tensor(f"xa{i}", [128, D], F32))
          for i in range(2)]
    lnf = [p1.enter_context(nc.sbuf_tensor(f"lnf{i}", [128, D], F32))
           for i in range(2)]
    aqi = p1.enter_context(nc.sbuf_tensor("aqi", [128, D], I32))
    aqb = [p1.enter_context(nc.sbuf_tensor(f"aqb{i}", [128, D], BF16))
           for i in range(2)]
    aqT = [p1.enter_context(nc.sbuf_tensor(f"aqT{i}", [128, D], BF16))
           for i in range(3)]
    qcols = p1.enter_context(nc.sbuf_tensor("qcols", [128, 6], F32))

    def qc(kind, slot):
        base = 0 if kind == "g" else 3
        return qcols[:, base + slot:base + slot + 1]
    silu_sb = [p1.enter_context(nc.sbuf_tensor(f"silu{i}", [128, 512], F32))
               for i in range(3)]
    h_sb = p1.enter_context(nc.sbuf_tensor("h_sb", [128, INNER], F32))
    hqi = p1.enter_context(nc.sbuf_tensor("hqi", [128, 2048], I32))
    hqb = [p1.enter_context(nc.sbuf_tensor(f"hqb{i}", [128, INNER], BF16))
           for i in range(2)]

    sxin = dma_pair("xin")
    sshq = dma_pair("sthq", store=True)
    tok_x = [None] * n_t
    stok_hq = [None] * n_t

    A0, V0, T0 = ctr["sA"], ctr["sV"], ctr["sT"]
    AT, VT, TT, GU = {}, {}, {}, {}
    _ca, _cv, _ct = [A0], [V0], [T0]

    def _fa(t):
        for j in (1, 2, 3):
            _ca[0] += 1
            AT[("s", t, j)] = _ca[0]

    def _ea(t):
        _ca[0] += 1
        AT[("e", t)] = _ca[0]

    def _fv(t):
        for j in (1, 2, 3, 4):
            _cv[0] += 1
            VT[("f", t, j)] = _cv[0]

    _fa(0), _fa(1), _ea(0), _ea(1)
    for _t in range(n_t):
        if _t + 2 < n_t:
            _fa(_t + 2)
        for _ib in range(NIB):
            _ca[0] += 1
            AT[("m", _t, _ib)] = _ca[0]
        for _j in (1, 2):
            _ca[0] += 1
            AT[("h", _t, _j)] = _ca[0]
        if _t + 2 < n_t:
            _ea(_t + 2)
    _fv(0), _fv(1)
    for _t in range(n_t):
        if _t + 2 < n_t:
            _fv(_t + 2)
        for _ib in range(NIB):
            _cv[0] += 1
            VT[("m", _t, _ib)] = _cv[0]
        for _j in (1, 2):
            _cv[0] += 1
            VT[("h", _t, _j)] = _cv[0]
    for _t in (0, 1):
        _ct[0] += 1
        TT[_t] = _ct[0]
    for _t in range(n_t):
        for _ib in range(NIB):
            _ct[0] += 1
            GU[(_t, _ib)] = _ct[0]
        if _t + 2 < n_t:
            _ct[0] += 1
            TT[_t + 2] = _ct[0]
    GBANK = [(pbk[2], pbk[3]), (pbk[4], pbk[5]), (pbk[6], pbk[7])]

    with nc.Block() as blk:
        @blk.sync
        def _(sy):
            for t in range(n_t):
                if t >= 2:
                    wait(sy, "sync", "sA", AT[("s", t - 2, 3)])
                tok_x[t] = dma(sy, sxin[t % 2], xa[t % 2][:],
                               x[t * 128:(t + 1) * 128, :])

        @blk.scalar
        def _(s):
            def front_a(t):
                par = t % 2
                wait_t(s, "scalar", tok_x[t])
                if t >= 2:
                    wait(s, "scalar", "sV", VT[("f", t - 2, 3)])
                tok = emit_stats_A(s, xa[par][:], lnf[par][:],
                                   VT[("f", t, 1)], VT[("f", t, 2)], par)
                assert tok == AT[("s", t, 3)]

            def evac_a(t):
                wait(s, "scalar", "sT", TT[t])
                if t >= 3:
                    wait(s, "scalar", "sT", GU[(t - 3, 7)])
                s.copy(aqT[t % 3][:, 0:512], pbk[0][:, :])
                s.copy(aqT[t % 3][:, 512:1024], pbk[1][:, :])
                tok = dinc(s, "sA")
                assert tok == AT[("e", t)]

            front_a(0), front_a(1), evac_a(0), evac_a(1)
            for t in range(n_t):
                par = t % 2
                if t + 2 < n_t:
                    front_a(t + 2)
                wait(s, "scalar", "sV", VT[("f", t, 3)])
                for ib in range(NIB):
                    gg = 8 * t + ib
                    wait(s, "scalar", "sT", GU[(t, ib)])
                    if gg >= 3:
                        tt, ii = divmod(gg - 3, 8)
                        wait(s, "scalar", "sV", VT[("m", tt, ii)])
                    tok = inc(s.activation(silu_sb[gg % 3][:],
                                           GBANK[gg % 3][0][:, :], AF.Silu,
                                           scale=qc("g", t % 3)), "sA")
                    assert tok == AT[("m", t, ib)]
                wait(s, "scalar", "sV", VT[("h", t, 1)])
                if t >= 2:
                    wait_t(s, "scalar", stok_hq[t - 2])
                tok = inc(s.activation(hqb[par][:, 0:2048], hqi[:],
                                       AF.Identity), "sA")
                assert tok == AT[("h", t, 1)]
                wait(s, "scalar", "sV", VT[("h", t, 2)])
                s.activation(hqb[par][:, 2048:4096], hqi[:], AF.Identity)
                tok = dinc(s, "sA")
                assert tok == AT[("h", t, 2)]
                stok_hq[t] = dma(s, sshq[par],
                                 hq_scr[t * 128:(t + 1) * 128, :], hqb[par][:])
                if t + 2 < n_t:
                    evac_a(t + 2)

        @blk.vector
        def _(v):
            def front_v(t):
                par = t % 2
                t1, t2 = emit_stats_V(v, AT[("s", t, 1)], AT[("s", t, 2)], par)
                assert (t1, t2) == (VT[("f", t, 1)], VT[("f", t, 2)])
                wait(v, "vector", "sA", AT[("s", t, 3)])
                if t >= 3:
                    wait(v, "vector", "sA", AT[("m", t - 3, 7)])
                tok = emit_quant_v(v, lnf[par][:], aqi[:], sc("aqrec", par),
                                   (qc("g", t % 3), qc("u", t % 3)), par)
                assert tok == VT[("f", t, 3)]
                if t >= 2:
                    wait(v, "vector", "sT", TT[t - 2])
                v.tensor_copy(aqb[par][:], aqi[:])
                tok = dinc(v, "sV")
                assert tok == VT[("f", t, 4)]

            front_v(0), front_v(1)
            for t in range(n_t):
                par = t % 2
                if t + 2 < n_t:
                    front_v(t + 2)
                for ib in range(NIB):
                    gg = 8 * t + ib
                    wait(v, "vector", "sT", GU[(t, ib)])
                    wait(v, "vector", "sA", AT[("m", t, ib)])
                    tok = inc(v.scalar_tensor_tensor(
                        h_sb[:, ib * 512:(ib + 1) * 512],
                        GBANK[gg % 3][1][:, :], qc("u", t % 3),
                        silu_sb[gg % 3][:], ALU.mult, ALU.mult), "sV")
                    assert tok == VT[("m", t, ib)]
                v.drain()
                v.reduce_max(sc("hamax", par), h_sb[:], AX.X,
                             apply_absolute_value=True)
                v.drain()
                v.tensor_scalar(sc("amc", par), sc("hamax", par),
                                sc("eps", par), None, ALU.max)
                v.drain()
                v.reciprocal(sc("rc", par), sc("amc", par))
                v.drain()
                v.tensor_scalar(sc("hqs", par), sc("rc", par), 127.0, None,
                                ALU.mult)
                v.drain()
                v.reciprocal(hrecip_all[:, t:t + 1], sc("hqs", par))
                v.drain()
                if t >= 1:
                    wait(v, "vector", "sA", AT[("h", t - 1, 2)])
                v.tensor_scalar(hqi[:], h_sb[:, 0:2048], sc("hqs", par), None,
                                ALU.mult)
                tok = dinc(v, "sV")
                assert tok == VT[("h", t, 1)]
                wait(v, "vector", "sA", AT[("h", t, 1)])
                v.tensor_scalar(hqi[:], h_sb[:, 2048:4096], sc("hqs", par),
                                None, ALU.mult)
                tok = dinc(v, "sV")
                assert tok == VT[("h", t, 2)]

        @blk.tensor
        def _(te):
            def trans(t):
                wait(te, "tensor", "sV", VT[("f", t, 4)])
                if t >= 1:
                    wait(te, "tensor", "sA", AT[("e", t - 1)])
                tok = inc(emit_transposes(te, aqb[t % 2][:, :],
                                          [pbk[0], pbk[1]], 8), "sT")
                assert tok == TT[t]

            trans(0), trans(1)
            for t in range(n_t):
                wait(te, "tensor", "sA", AT[("e", t)])
                for ib in range(NIB):
                    gg = 8 * t + ib
                    if gg >= 3:
                        tt, ii = divmod(gg - 3, 8)
                        wait(te, "tensor", "sV", VT[("m", tt, ii)])
                    gb, ub = GBANK[gg % 3]
                    for c in range(DC):
                        te.matmul(gb[:, :],
                                  aqT[t % 3][:, c * 128:(c + 1) * 128],
                                  wg_sb[:, c * INNER + ib * 512:
                                        c * INNER + ib * 512 + 512],
                                  start=(c == 0), stop=(c == DC - 1))
                    last = None
                    for c in range(DC):
                        last = te.matmul(ub[:, :],
                                         aqT[t % 3][:, c * 128:(c + 1) * 128],
                                         wu_sb[:, c * INNER + ib * 512:
                                               c * INNER + ib * 512 + 512],
                                         start=(c == 0), stop=(c == DC - 1))
                    tok = inc(last, "sT")
                    assert tok == GU[(t, ib)]
                if t + 2 < n_t:
                    trans(t + 2)

    assert ctr["sA"] == A0 + 14 * n_t
    assert ctr["sV"] == V0 + 14 * n_t
    assert ctr["sT"] == T0 + 9 * n_t
    p1.close()

    def run_down_phase(preamble, hq_src, res_src, out_dst, final):
        pd = contextlib.ExitStack()
        wd_sb = pd.enter_context(nc.sbuf_tensor(_nm("wd_sb"),
                                                [128, 32 * 1024], BF16))
        preamble(wd_sb)
        hqa = [pd.enter_context(nc.sbuf_tensor(_nm("hqa"), [128, INNER], BF16))
               for _ in range(2)]
        resa = [pd.enter_context(nc.sbuf_tensor(_nm("resa"), [128, D], F32))
                for _ in range(2)]
        hqT = [pd.enter_context(nc.sbuf_tensor(_nm("hqT"), [128, INNER], BF16))
               for _ in range(2)]
        y3sb = [pd.enter_context(nc.sbuf_tensor(_nm("y3sb"), [128, D], F32))
                for _ in range(2)]
        sdlh = dma_pair(_nm("dnlh"))
        sdlr = dma_pair(_nm("dnlr"))
        sds = dma_pair(_nm("dns"), store=True)
        tok_hq = [None] * n_t
        tok_res = [None] * n_t
        stok = [None] * n_t

        T0d, A0d, V0d = ctr["sT"], ctr["sA"], ctr["sV"]
        _pe_off = [1, 2, 3, 4, 6, 8, 10, 12]
        _dg_off = [5, 7, 9, 11, 13, 14, 15, 16]

        def TG(t, g):
            return T0d + 16 * t + _pe_off[g]

        def DG(t, dg):
            return T0d + 16 * t + _dg_off[dg]

        def EV(t, g):
            return A0d + 8 * t + 1 + g

        def Y3(t):
            return V0d + t + 1

        with nc.Block() as blk:
            @blk.sync
            def _(sy):
                wait_stores(sy, "sync")
                for t in range(n_t):
                    if t >= 2:
                        wait(sy, "sync", "sT", TG(t - 2, 7))
                        wait(sy, "sync", "sV", Y3(t - 2))
                    tok_hq[t] = dma(sy, sdlh[t % 2], hqa[t % 2][:],
                                    hq_src[t * 128:(t + 1) * 128, :])
                    tok_res[t] = dma(sy, sdlr[t % 2], resa[t % 2][:],
                                     res_src[t * 128:(t + 1) * 128, :])

            @blk.scalar
            def _(s):
                for t in range(n_t):
                    for g in range(8):
                        wait(s, "scalar", "sT", TG(t, g))
                        if t >= 2:
                            wait(s, "scalar", "sT", DG(t - 2, g))
                        s.copy(hqT[t % 2][:, g * 512:(g + 1) * 512],
                               pbk[g % 4][:, :])
                        tok = dinc(s, "sA")
                        assert tok == EV(t, g)
                    wait(s, "scalar", "sV", Y3(t))
                    stok[t] = dma(s, sds[t % 2],
                                  out_dst[t * 128:(t + 1) * 128, :],
                                  y3sb[t % 2][:])

            @blk.vector
            def _(v):
                for t in range(n_t):
                    dnb = (pbk[4], pbk[5]) if t % 2 == 0 else (pbk[6], pbk[7])
                    wait(v, "vector", "sT", DG(t, 7))
                    wait_t(v, "vector", tok_res[t])
                    if t >= 2:
                        wait_t(v, "vector", stok[t - 2])
                    v.tensor_scalar(sc("colt"), hrecip_all[:, t:t + 1],
                                    sdh_col[:, :], None, ALU.mult)
                    v.drain()
                    v.scalar_tensor_tensor(y3sb[t % 2][:, 0:512], dnb[0][:, :],
                                           sc("colt"), resa[t % 2][:, 0:512],
                                           ALU.mult, ALU.add)
                    v.scalar_tensor_tensor(y3sb[t % 2][:, 512:1024],
                                           dnb[1][:, :], sc("colt"),
                                           resa[t % 2][:, 512:1024],
                                           ALU.mult, ALU.add)
                    tok = inc(v.drain(), "sV")
                    assert tok == Y3(t)

            @blk.tensor
            def _(te):
                for t in range(n_t):
                    dnb = (pbk[4], pbk[5]) if t % 2 == 0 else (pbk[6], pbk[7])

                    def tgrp(g, t=t):
                        if t >= 1:
                            wait(te, "tensor", "sA", EV(t - 1, 4 + (g % 4)))
                        last = None
                        for k in range(4):
                            c = g * 4 + k
                            last = te.matmul(
                                pbk[g % 4][:, k * 128:(k + 1) * 128],
                                hqa[t % 2][:, c * 128:(c + 1) * 128],
                                ident_bf[:, :], start=(k == 0), stop=(k == 3))
                        tok = inc(last, "sT")
                        assert tok == TG(t, g)

                    def dgrp(dg, t=t, dnb=dnb):
                        wait(te, "tensor", "sA", EV(t, dg))
                        if dg == 0 and t >= 2:
                            wait(te, "tensor", "sV", Y3(t - 2))
                        last = None
                        for k in range(4):
                            ci = dg * 4 + k
                            te.matmul(dnb[0][:, :],
                                      hqT[t % 2][:, ci * 128:(ci + 1) * 128],
                                      wd_sb[:, ci * 1024:ci * 1024 + 512],
                                      start=(ci == 0), stop=(ci == 31))
                            last = te.matmul(
                                dnb[1][:, :],
                                hqT[t % 2][:, ci * 128:(ci + 1) * 128],
                                wd_sb[:, ci * 1024 + 512:ci * 1024 + 1024],
                                start=(ci == 0), stop=(ci == 31))
                        tok = inc(last, "sT")
                        assert tok == DG(t, dg)

                    wait_t(te, "tensor", tok_hq[t])
                    for g in range(4):
                        tgrp(g)
                    for g in range(4):
                        dgrp(g)
                        tgrp(4 + g)
                    for dg in range(4, 8):
                        dgrp(dg)

            assert ctr["sT"] == T0d + 16 * n_t
            assert ctr["sA"] == A0d + 8 * n_t
            assert ctr["sV"] == V0d + n_t

        pd.close()

    def p1b_pre(wd_sb):
        emit_wquant([wdt], [wd_sb], [dt_scr], [2], 32, 1024)

    if "p1b" not in phases:
        return finish()
    run_down_phase(p1b_pre, hq_scr, x, y1_scr, final=False)

    if "p2a" not in phases:
        return finish()
    p2 = contextlib.ExitStack()
    wq_sb = p2.enter_context(nc.sbuf_tensor("wq_sb", [128, DC * D], BF16))
    wk_sb = p2.enter_context(nc.sbuf_tensor("wk_sb", [128, DC * D], BF16))
    wv_sb = p2.enter_context(nc.sbuf_tensor("wv_sb", [128, DC * D], BF16))

    pre = contextlib.ExitStack()
    wtmp2 = [pre.enter_context(nc.sbuf_tensor(_nm("w2"), [128, D], F32))
             for _ in range(2)]
    jobs2 = [(w, dst, i) for (w, dst) in ((wq, wq_sb), (wk, wk_sb), (wv, wv_sb))
             for i in range(8)]
    sp2w = dma_pair("p2w")
    d2, v2base = [], ctr["sV"]
    with nc.Block() as blk:
        @blk.sync
        def _(sy):
            wait_stores(sy, "sync")
            for j, (w, dst, i) in enumerate(jobs2):
                if j >= 2:
                    wait(sy, "sync", "sV", v2base + j - 1)
                d2.append(dma(sy, sp2w[j % 2], wtmp2[j % 2][:],
                              w[i * 128:(i + 1) * 128, :]))

        @blk.vector
        def _(v):
            for j, (w, dst, i) in enumerate(jobs2):
                wait_t(v, "vector", d2[j])
                v.tensor_copy(dst[:, i * D:(i + 1) * D], wtmp2[j % 2][:])
                dinc(v, "sV")

    sp2o = dma_one("p2o")
    with nc.Block() as blk:
        @blk.sync
        def _(sy):
            dma(sy, sp2o, wtmp2[0][0:64, 0:M], omt)
            dma(sy, sp2o, wtmp2[0][64:128, 0:M], omt)

        @blk.vector
        def _(v):
            wait(v, "vector", sp2o, ctr[sp2o])
            v.tensor_copy(omdup_bf[:, :], wtmp2[0][:, 0:M])
            v.drain()
    pre.close()

    ya = [p2.enter_context(nc.sbuf_tensor(f"ya{i}", [128, D], F32))
          for i in range(2)]
    lnb_sb = p2.enter_context(nc.sbuf_tensor("lnb_sb", [128, D], BF16))
    lnT_sb = p2.enter_context(nc.sbuf_tensor("lnT_sb", [128, D], BF16))
    k_f = p2.enter_context(nc.sbuf_tensor("k_f", [128, D], F32))
    q_f = p2.enter_context(nc.sbuf_tensor("q_f", [128, D], F32))
    k_b = p2.enter_context(nc.sbuf_tensor("k_b", [128, D], BF16))
    k2_b = p2.enter_context(nc.sbuf_tensor("k2_b", [128, D], BF16))
    v_b = p2.enter_context(nc.sbuf_tensor("v_b", [128, D], BF16))
    pk_b = p2.enter_context(nc.sbuf_tensor("pk_b", [128, H * M], BF16))
    qb_sb = [p2.enter_context(nc.sbuf_tensor(f"qb{i}", [128, D], BF16))
             for i in range(2)]
    q2_sb = [p2.enter_context(nc.sbuf_tensor(f"q2{i}", [128, D], BF16))
             for i in range(2)]

    sy1in = dma_pair("y1in")
    sstq = dma_pair("stq", store=True)
    sstq2 = dma_pair("stq2", store=True)
    tok_y1 = [None] * (n_t + 1)
    stok_q = [None] * n_t
    stok_q2 = [None] * n_t

    with nc.Block() as blk:
        @blk.sync
        def _(sy):
            wait_stores(sy, "sync")
            tok_y1[0] = dma(sy, sy1in[0], ya[0][:], y1_scr[0:128, :])

    for t in range(n_t):
        tb, ab, vb = ctr["sT"], ctr["sA"], ctr["sV"]
        with nc.Block() as blk:
            @blk.sync
            def _(sy, t=t):
                if t + 1 < n_t:
                    tok_y1[t + 1] = dma(sy, sy1in[(t + 1) % 2],
                                        ya[(t + 1) % 2][:],
                                        y1_scr[(t + 1) * 128:(t + 2) * 128, :])

            @blk.scalar
            def _(s, t=t, tb=tb, vb=vb):
                wait_t(s, "scalar", tok_y1[t])
                emit_stats_A(s, ya[t % 2][:], lnb_sb[:], vb + 1, vb + 2)
                wait(s, "scalar", "sT", tb + 1)
                s.copy(lnT_sb[:, 0:512], pbk[5][:, :])
                s.copy(lnT_sb[:, 512:1024], pbk[6][:, :])
                dinc(s, "sA")
                wait(s, "scalar", "sT", tb + 2)
                s.copy(k_f[:, 0:512], pbk[5][:, :])
                s.copy(k_f[:, 512:1024], pbk[6][:, :])
                dinc(s, "sA")
                s.activation(k2_b[:, :], k_f[:, :], AF.Square)
                dinc(s, "sA")
                wait(s, "scalar", "sT", tb + 4)
                s.copy(q_f[:, 0:512], pbk[5][:, :])
                s.copy(q_f[:, 512:1024], pbk[6][:, :])
                dinc(s, "sA")
                if t >= 2:
                    wait_t(s, "scalar", stok_q2[t - 2])
                s.activation(q2_sb[t % 2][:, :], q_f[:, :], AF.Square)
                dinc(s, "sA")
                for hh in range(H):
                    wait(s, "scalar", "sT", tb + 5 + hh)
                    s.activation(
                        pk_b[:, hh * M:(hh + 1) * M],
                        pbk[5 + (hh % 2)][:, 0:M], AF.Exp)
                    dinc(s, "sA")
                wait(s, "scalar", "sV", vb + 5)
                stok_q[t] = dma(s, sstq[t % 2],
                                qb_scr[t * 128:(t + 1) * 128, :], qb_sb[t % 2][:])
                stok_q2[t] = dma(s, sstq2[t % 2],
                                 q2_scr[t * 128:(t + 1) * 128, :],
                                 q2_sb[t % 2][:])

            @blk.vector
            def _(v, t=t, tb=tb, ab=ab):
                emit_stats_V(v, ab + 1, ab + 2)
                wait(v, "vector", "sA", ab + 5)
                v.tensor_copy(k_b[:, :], k_f[:, :])
                dinc(v, "sV")
                wait(v, "vector", "sT", tb + 3)
                v.tensor_copy(v_b[:, 0:512], pbk[5][:, :])
                v.tensor_copy(v_b[:, 512:1024], pbk[6][:, :])
                dinc(v, "sV")
                wait(v, "vector", "sA", ab + 7)
                if t >= 2:
                    wait_t(v, "vector", stok_q[t - 2])
                v.tensor_copy(qb_sb[t % 2][:, :], q_f[:, :])
                inc(v.drain(), "sV")

            @blk.tensor
            def _(te, t=t, ab=ab, vb=vb):
                wait(te, "tensor", "sA", ab + 3)
                inc(emit_transposes(te, lnb_sb[:, :], [pbk[5], pbk[6]], 8), "sT")
                wait(te, "tensor", "sA", ab + 4)
                last = None
                for co in range(8):
                    bank = pbk[5] if co < 4 else pbk[6]
                    sl = bank[:, (co % 4) * 128:(co % 4) * 128 + 128]
                    for cd in range(DC):
                        last = te.matmul(
                            sl, wk_sb[:, cd * D + co * 128:cd * D + co * 128 + 128],
                            lnT_sb[:, cd * 128:(cd + 1) * 128],
                            start=(co % 4 == 0 and cd == 0),
                            stop=(co % 4 == 3 and cd == DC - 1))
                inc(last, "sT")
                wait(te, "tensor", "sA", ab + 5)
                last = None
                for half in range(2):
                    bank = pbk[5] if half == 0 else pbk[6]
                    for cd in range(DC):
                        last = te.matmul(
                            bank[:, :], lnT_sb[:, cd * 128:(cd + 1) * 128],
                            wv_sb[:, cd * D + half * 512:cd * D + half * 512 + 512],
                            start=(cd == 0), stop=(cd == DC - 1))
                inc(last, "sT")
                wait(te, "tensor", "sV", vb + 4)
                last = None
                for co in range(8):
                    bank = pbk[5] if co < 4 else pbk[6]
                    sl = bank[:, (co % 4) * 128:(co % 4) * 128 + 128]
                    for cd in range(DC):
                        last = te.matmul(
                            sl, wq_sb[:, cd * D + co * 128:cd * D + co * 128 + 128],
                            lnT_sb[:, cd * 128:(cd + 1) * 128],
                            start=(co % 4 == 0 and cd == 0),
                            stop=(co % 4 == 3 and cd == DC - 1))
                inc(last, "sT")
                wait(te, "tensor", "sV", vb + 3)
                wait(te, "tensor", "sA", ab + 7)
                for hh in range(H):
                    if hh >= 2:
                        wait(te, "tensor", "sA", ab + 9 + (hh - 2))
                    par = hh % 2
                    sl = pbk[5 + par][:, 0:M]
                    ksl = k_b[par * 64:par * 64 + 64,
                              (hh // 2) * 128:(hh // 2) * 128 + 128]
                    k2sl = k2_b[par * 64:par * 64 + 64,
                                (hh // 2) * 128:(hh // 2) * 128 + 128]
                    te.matmul(sl, ksl, omdup_bf[par * 64:par * 64 + 64, :],
                              start=True, stop=False)
                    inc(te.matmul(sl, k2sl, neghalf_bf[par * 64:par * 64 + 64, :],
                                  start=False, stop=True), "sT")
                last = None
                for hh in range(H):
                    wait(te, "tensor", "sA", ab + 9 + hh)
                    hl = hh - 8 if hh >= 8 else hh
                    for mc in range(2):
                        kvb = pbk[mc * 2 + (1 if hh >= 8 else 0)]
                        lhs = pk_b[:, hh * M + mc * 128:hh * M + mc * 128 + 128]
                        te.matmul(kvb[:, hl * 64:hl * 64 + 64], lhs,
                                  v_b[:, hh * 64:hh * 64 + 64],
                                  start=(t == 0 and hl == 0),
                                  stop=(t == n_t - 1 and hl == 7))
                        last = te.matmul(pbk[4][:, mc * 16 + hh:mc * 16 + hh + 1],
                                         lhs, onesbf_col[:, :],
                                         start=(t == 0 and hh == 0 and mc == 0),
                                         stop=(t == n_t - 1 and hh == H - 1
                                               and mc == 1))
                inc(last, "sT")

        assert ctr["sT"] == tb + 21, (ctr["sT"], tb)
        assert ctr["sA"] == ab + 8 + H
        assert ctr["sV"] == vb + 5

    with nc.Block() as blk:
        @blk.vector
        def _(v):
            for mc in range(2):
                for hh in range(H):
                    kvb = pbk[mc * 2 + (1 if hh >= 8 else 0)]
                    hl = hh - 8 if hh >= 8 else hh
                    base = mc * (H * 65) + hh * 65
                    v.tensor_copy(kvaug_bf[:, base:base + 64],
                                  kvb[:, hl * 64:hl * 64 + 64])
                    v.tensor_copy(kvaug_bf[:, base + 64:base + 65],
                                  pbk[4][:, mc * 16 + hh:mc * 16 + hh + 1])
            v.drain()
    p2.close()

    if "p3a" not in phases:
        return finish()
    p3 = contextlib.ExitStack()
    wg3_sb = p3.enter_context(nc.sbuf_tensor("wg3_sb", [128, DC * INNER], BF16))
    wu3_sb = p3.enter_context(nc.sbuf_tensor("wu3_sb", [128, DC * INNER], BF16))
    stern = dma_one("tern")
    with nc.Block() as blk:
        @blk.sync
        def _(sy):
            wait_stores(sy, "sync")
            for i in range(8):
                dma(sy, stern, wg3_sb[:, i * INNER:(i + 1) * INNER],
                    gt_scr[i * 128:(i + 1) * 128, :])
            for i in range(8):
                dma(sy, stern, wu3_sb[:, i * INNER:(i + 1) * INNER],
                    ut_scr[i * 128:(i + 1) * 128, :])

    y1a = [p3.enter_context(nc.sbuf_tensor(f"y1a{i}", [128, D], F32))
           for i in range(2)]
    qba = [p3.enter_context(nc.sbuf_tensor(f"qba{i}", [128, D], BF16))
           for i in range(2)]
    q2a = [p3.enter_context(nc.sbuf_tensor(f"q2a{i}", [128, D], BF16))
           for i in range(2)]
    pq_b = p3.enter_context(nc.sbuf_tensor("pq_b", [128, 32 * 128], BF16))
    y2_sb = [p3.enter_context(nc.sbuf_tensor(f"y2sb{i}", [128, D], F32))
             for i in range(2)]
    lnf3 = p3.enter_context(nc.sbuf_tensor("lnf3", [128, D], F32))
    aqi3 = p3.enter_context(nc.sbuf_tensor("aqi3", [128, D], I32))
    aqb3 = p3.enter_context(nc.sbuf_tensor("aqb3", [128, D], BF16))
    aqT3 = p3.enter_context(nc.sbuf_tensor("aqT3", [128, D], BF16))
    silu3 = [p3.enter_context(nc.sbuf_tensor(f"silu3{i}", [128, 512], F32))
             for i in range(2)]
    h3_sb = p3.enter_context(nc.sbuf_tensor("h3_sb", [128, INNER], F32))
    hqi3 = p3.enter_context(nc.sbuf_tensor("hqi3", [128, 2048], I32))
    hqb3 = p3.enter_context(nc.sbuf_tensor("hqb3", [128, INNER], BF16))
    xsq3 = p3.enter_context(nc.sbuf_tensor("xsq3", [128, D], F32))

    sp3in = dma_pair("p3in")
    ssty2 = dma_pair("sty2", store=True)
    ssth2 = dma_pair("sth2", store=True)
    tok3 = [None] * (n_t + 1)
    stok_h2 = [None] * n_t
    stok_y2 = [None] * n_t

    def p3_loads(sy, tt):
        dma(sy, sp3in[tt % 2], y1a[tt % 2][:],
            y1_scr[tt * 128:(tt + 1) * 128, :])
        dma(sy, sp3in[tt % 2], qba[tt % 2][:],
            qb_scr[tt * 128:(tt + 1) * 128, :])
        tok3[tt] = dma(sy, sp3in[tt % 2], q2a[tt % 2][:],
                       q2_scr[tt * 128:(tt + 1) * 128, :])

    with nc.Block() as blk:
        @blk.sync
        def _(sy):
            p3_loads(sy, 0)

    ATT_BANK = [4] * 7 + [5] * 7 + [6] * 2
    ATT_SLOT = list(range(7)) + list(range(7)) + list(range(2))
    GBANK3 = [(pbk[2], pbk[3]), (pbk[7], pbk[6])]

    for t in range(n_t + 1):
        att = t < n_t
        ffn = t >= 1
        u = t - 1
        na, nf = (1 if att else 0), (1 if ffn else 0)
        tb, ab, vb = ctr["sT"], ctr["sA"], ctr["sV"]
        T_ATT = tb + 5
        T_TR = tb + 5 * na + 1

        def T_GU(ib, tb=tb, na=na):
            return tb + 5 * na + 2 + ib

        A_SRT = ab + 4 * na + 1
        A_EVAC = ab + 4 * na + 2

        def A_SIL(ib, ab=ab, na=na):
            return ab + 4 * na + 3 + ib

        A_C1 = ab + 4 * na + 11
        A_C2 = ab + 4 * na + 12
        V_VAR = vb + 1
        V_AQB = vb + 2
        V_Y2 = vb + 2 * nf + 1

        def V_HM(ib, vb=vb, nf=nf, na=na):
            return vb + 2 * nf + na + 1 + ib

        V_H1 = vb + 2 * nf + na + 9
        V_H2 = vb + 2 * nf + na + 10

        with nc.Block() as blk:
            @blk.sync
            def _(sy, t=t):
                if t + 1 < n_t:
                    p3_loads(sy, t + 1)

            @blk.scalar
            def _(s, t=t, u=u, att=att, ffn=ffn, tb=tb, ab=ab):
                if att:
                    for g in range(4):
                        wait(s, "scalar", "sT", tb + 1 + g)
                        bA = pbk[0] if g % 2 == 0 else pbk[2]
                        bB = pbk[1] if g % 2 == 0 else pbk[3]
                        s.activation(pq_b[:, g * 1024:g * 1024 + 512], bA[:, :],
                                     AF.Exp)
                        s.activation(pq_b[:, g * 1024 + 512:(g + 1) * 1024],
                                     bB[:, :], AF.Exp)
                        dinc(s, "sA")
                if ffn:
                    wait(s, "scalar", "sV", V_VAR)
                    s.activation(sc("srt"), sc("var"), AF.Sqrt, bias=sc("eps"))
                    tok = dinc(s, "sA")
                    assert tok == A_SRT
                    wait(s, "scalar", "sT", T_TR)
                    s.copy(aqT3[:, 0:512], pbk[0][:, :])
                    s.copy(aqT3[:, 512:1024], pbk[1][:, :])
                    tok = dinc(s, "sA")
                    assert tok == A_EVAC
                    wait(s, "scalar", "sV", V_AQB)
                    for ib in range(NIB):
                        wait(s, "scalar", "sT", T_GU(ib))
                        if ib >= 2:
                            wait(s, "scalar", "sV", V_HM(ib - 2))
                        tok = inc(s.activation(silu3[ib % 2][:],
                                               GBANK3[ib % 2][0][:, :], AF.Silu,
                                               scale=sc("colg")), "sA")
                        assert tok == A_SIL(ib)
                    wait(s, "scalar", "sV", V_H1)
                    if u >= 1:
                        wait_t(s, "scalar", stok_h2[u - 1])
                    tok = inc(s.activation(hqb3[:, 0:2048], hqi3[:],
                                           AF.Identity), "sA")
                    assert tok == A_C1
                    wait(s, "scalar", "sV", V_H2)
                    s.activation(hqb3[:, 2048:4096], hqi3[:], AF.Identity)
                    tok = dinc(s, "sA")
                    assert tok == A_C2
                    stok_h2[u] = dma(s, ssth2[u % 2],
                                     hq_scr[u * 128:(u + 1) * 128, :], hqb3[:])
                if att:
                    wait(s, "scalar", "sV", V_Y2)
                    stok_y2[t] = dma(s, ssty2[t % 2],
                                     y2_scr[t * 128:(t + 1) * 128, :],
                                     y2_sb[t % 2][:])

            @blk.vector
            def _(v, t=t, u=u, att=att, ffn=ffn):
                if ffn:
                    src = y2_sb[u % 2][:]
                    v.reduce_sum(sc("sum"), src, AX.X)
                    v.drain()
                    v.tensor_scalar(sc("mean"), sc("sum"), 1.0 / 1024.0, None,
                                    ALU.mult)
                    v.drain()
                    v.tensor_tensor(xsq3[:], src, src, ALU.mult)
                    v.drain()
                    v.reduce_sum(sc("ssq"), xsq3[:], AX.X)
                    v.drain()
                    v.tensor_tensor(sc("m2"), sc("mean"), sc("mean"), ALU.mult)
                    v.drain()
                    v.scalar_tensor_tensor(sc("var"), sc("ssq"), 1.0 / 1024.0,
                                           sc("m2"), ALU.mult, ALU.subtract)
                    tok = dinc(v, "sV")
                    assert tok == V_VAR
                    wait(v, "vector", "sA", A_SRT)
                    v.reciprocal(sc("rstd"), sc("srt"))
                    v.drain()
                    v.tensor_scalar(sc("nmr"), sc("mean"), sc("rstd"), -1.0,
                                    ALU.mult, ALU.mult)
                    v.drain()
                    v.tensor_scalar(lnf3[:], src, sc("rstd"), sc("nmr"),
                                    ALU.mult, ALU.add)
                    v.drain()
                    v.reduce_max(sc("amax"), lnf3[:], AX.X,
                                 apply_absolute_value=True)
                    v.drain()
                    v.tensor_scalar(sc("amc"), sc("amax"), sc("eps"), None,
                                    ALU.max)
                    v.drain()
                    v.reciprocal(sc("rc"), sc("amc"))
                    v.drain()
                    v.tensor_scalar(sc("qs"), sc("rc"), 127.0, None, ALU.mult)
                    v.drain()
                    v.reciprocal(sc("aqrec"), sc("qs"))
                    v.drain()
                    v.tensor_scalar(sc("colg"), sc("aqrec"), wcols[:, 0:1],
                                    None, ALU.mult)
                    v.tensor_scalar(sc("colu"), sc("aqrec"), wcols[:, 1:2],
                                    None, ALU.mult)
                    v.drain()
                    v.tensor_scalar(aqi3[:], lnf3[:], sc("qs"), None, ALU.mult)
                    v.drain()
                    v.tensor_copy(aqb3[:], aqi3[:])
                    tok = dinc(v, "sV")
                    assert tok == V_AQB
                if att:
                    wait(v, "vector", "sT", T_ATT)
                    wait_t(v, "vector", tok3[t])
                    if t >= 2:
                        wait_t(v, "vector", stok_y2[t - 2])
                    for hh in range(H):
                        v.tensor_copy(zcol[:, hh:hh + 1],
                                      pbk[ATT_BANK[hh]][:, ATT_SLOT[hh] * 65 + 64:
                                                        ATT_SLOT[hh] * 65 + 65])
                    v.drain()
                    v.tensor_scalar(zcol[:, :], zcol[:, :], 2.56e-4, None,
                                    ALU.add)
                    v.drain()
                    v.reciprocal(zcol[:, :], zcol[:, :])
                    v.drain()
                    for hh in range(H):
                        v.scalar_tensor_tensor(
                            y2_sb[t % 2][:, hh * 64:hh * 64 + 64],
                            pbk[ATT_BANK[hh]][:, ATT_SLOT[hh] * 65:
                                              ATT_SLOT[hh] * 65 + 64],
                            zcol[:, hh:hh + 1],
                            y1a[t % 2][:, hh * 64:hh * 64 + 64],
                            ALU.mult, ALU.add)
                    tok = inc(v.drain(), "sV")
                    assert tok == V_Y2
                if ffn:
                    for ib in range(NIB):
                        wait(v, "vector", "sT", T_GU(ib))
                        wait(v, "vector", "sA", A_SIL(ib))
                        tok = inc(v.scalar_tensor_tensor(
                            h3_sb[:, ib * 512:(ib + 1) * 512],
                            GBANK3[ib % 2][1][:, :], sc("colu"),
                            silu3[ib % 2][:], ALU.mult, ALU.mult), "sV")
                        assert tok == V_HM(ib)
                    v.drain()
                    v.reduce_max(sc("hamax"), h3_sb[:], AX.X,
                                 apply_absolute_value=True)
                    v.drain()
                    v.tensor_scalar(sc("amc"), sc("hamax"), sc("eps"), None,
                                    ALU.max)
                    v.drain()
                    v.reciprocal(sc("rc"), sc("amc"))
                    v.drain()
                    v.tensor_scalar(sc("hqs"), sc("rc"), 127.0, None, ALU.mult)
                    v.drain()
                    v.reciprocal(hrecip_all[:, u:u + 1], sc("hqs"))
                    v.drain()
                    v.tensor_scalar(hqi3[:], h3_sb[:, 0:2048], sc("hqs"), None,
                                    ALU.mult)
                    tok = dinc(v, "sV")
                    assert tok == V_H1
                    wait(v, "vector", "sA", A_C1)
                    v.tensor_scalar(hqi3[:], h3_sb[:, 2048:4096], sc("hqs"),
                                    None, ALU.mult)
                    tok = dinc(v, "sV")
                    assert tok == V_H2

            @blk.tensor
            def _(te, t=t, u=u, att=att, ffn=ffn, tb=tb, ab=ab):
                if att:
                    wait_t(te, "tensor", tok3[t])
                    if t == 0:
                        wait(te, "tensor", stern, ctr[stern])
                    for g in range(4):
                        mc, par = g // 2, g % 2
                        bA = pbk[0] if g % 2 == 0 else pbk[2]
                        bB = pbk[1] if g % 2 == 0 else pbk[3]
                        if g >= 2:
                            wait(te, "tensor", "sA", ab + 1 + (g - 2))
                        oms = omdup_bf[par * 64:par * 64 + 64,
                                       mc * 128:mc * 128 + 128]
                        ngs = neghalf_bf[par * 64:par * 64 + 64, 0:128]
                        last = None
                        for c in range(8):
                            bank = bA if c < 4 else bB
                            sl = bank[:, (c % 4) * 128:(c % 4) * 128 + 128]
                            qsl = qba[t % 2][par * 64:par * 64 + 64,
                                             c * 128:(c + 1) * 128]
                            q2sl = q2a[t % 2][par * 64:par * 64 + 64,
                                              c * 128:(c + 1) * 128]
                            te.matmul(sl, oms, qsl, start=(c % 4 == 0),
                                      stop=False)
                            last = te.matmul(sl, ngs, q2sl, start=False,
                                             stop=(c % 4 == 3))
                        inc(last, "sT")
                    wait(te, "tensor", "sA", ab + 4)
                    last = None
                    bank_started = set()
                    for hh in range(H):
                        bk, slt = ATT_BANK[hh], ATT_SLOT[hh]
                        for mc in range(2):
                            g = mc * 2 + (hh % 2)
                            j = g * 8 + hh // 2
                            first = bk not in bank_started
                            bank_started.add(bk)
                            last = te.matmul(
                                pbk[bk][:, slt * 65:slt * 65 + 65],
                                pq_b[:, j * 128:(j + 1) * 128],
                                kvaug_bf[:, mc * (H * 65) + hh * 65:
                                         mc * (H * 65) + (hh + 1) * 65],
                                start=first,
                                stop=(mc == 1 and hh in (6, 13, 15)))
                    tok = inc(last, "sT")
                    assert tok == T_ATT
                if ffn:
                    wait(te, "tensor", "sV", V_AQB)
                    tok = inc(emit_transposes(te, aqb3[:, :], [pbk[0], pbk[1]],
                                              8), "sT")
                    assert tok == T_TR
                    wait(te, "tensor", "sA", A_EVAC)
                    for ib in range(NIB):
                        if ib == 1 and att:
                            wait(te, "tensor", "sV", V_Y2)
                        if ib >= 2:
                            wait(te, "tensor", "sV", V_HM(ib - 2))
                        gb, ub = GBANK3[ib % 2]
                        for c in range(DC):
                            te.matmul(gb[:, :], aqT3[:, c * 128:(c + 1) * 128],
                                      wg3_sb[:, c * INNER + ib * 512:
                                             c * INNER + ib * 512 + 512],
                                      start=(c == 0), stop=(c == DC - 1))
                        last = None
                        for c in range(DC):
                            last = te.matmul(ub[:, :],
                                             aqT3[:, c * 128:(c + 1) * 128],
                                             wu3_sb[:, c * INNER + ib * 512:
                                                    c * INNER + ib * 512 + 512],
                                             start=(c == 0), stop=(c == DC - 1))
                        tok = inc(last, "sT")
                        assert tok == T_GU(ib)

        assert ctr["sT"] == tb + 5 * na + 9 * nf, (ctr["sT"], tb)
        assert ctr["sA"] == ab + 4 * na + 12 * nf
        assert ctr["sV"] == vb + na + 12 * nf
    p3.close()

    if "p3b" not in phases:
        return finish()
    swdld = dma_one("wdld")

    def p3b_pre(wd_sb):
        with nc.Block() as blk:
            @blk.sync
            def _(sy):
                wait_stores(sy, "sync")
                for ci in range(32):
                    dma(sy, swdld, wd_sb[:, ci * 1024:(ci + 1) * 1024],
                        dt_scr[ci * 128:(ci + 1) * 128, :])
                wait(sy, "sync", swdld, ctr[swdld])

    run_down_phase(p3b_pre, hq_scr, y2_scr, y, final=True)

    return finish()


_NC_CACHE = {}


def _get_program():
    if "nc" not in _NC_CACHE:
        _NC_CACHE["nc"] = build_program()
    return _NC_CACHE["nc"]


def kernel(L0_attn_x, L0_attn_Wq, L0_attn_Wk, L0_attn_Wv, L0_attn_omega,
           L0_ffn_w_gate, L0_ffn_w_up, L0_ffn_w_down):
    xs = np.ascontiguousarray(np.asarray(L0_attn_x, dtype=np.float32))
    wq_ = np.ascontiguousarray(np.asarray(L0_attn_Wq, dtype=np.float32))
    wk_ = np.ascontiguousarray(np.asarray(L0_attn_Wk, dtype=np.float32))
    wv_ = np.ascontiguousarray(np.asarray(L0_attn_Wv, dtype=np.float32))
    omt_ = np.ascontiguousarray(np.asarray(L0_attn_omega, dtype=np.float32).T)
    wgt_ = np.ascontiguousarray(np.asarray(L0_ffn_w_gate, dtype=np.float32).T)
    wut_ = np.ascontiguousarray(np.asarray(L0_ffn_w_up, dtype=np.float32).T)
    wdt_ = np.ascontiguousarray(np.asarray(L0_ffn_w_down, dtype=np.float32).T)

    nc = _get_program()
    in_maps = [{"x": np.ascontiguousarray(xs[c]), "wq": wq_, "wk": wk_,
                "wv": wv_, "omt": omt_, "wgt": wgt_, "wut": wut_, "wdt": wdt_}
               for c in range(8)]
    res = run_bass_kernel_spmd(nc, in_maps, core_ids=list(range(8)))
    return np.stack([np.asarray(res.results[c]["y"], dtype=np.float32)
                     for c in range(8)], axis=0)


if __name__ == "__main__":
    print("building...")
    nc = build_program()
    print("instructions:", nc.n_instructions())


# version 07
# speedup vs baseline: 1.1901x; 1.0466x over previous
import contextlib
import numpy as np
import concourse.bass as bass
import concourse.mybir as mybir
from concourse.bass_utils import run_bass_kernel_spmd

F32 = mybir.dt.float32
BF16 = mybir.dt.bfloat16
I32 = mybir.dt.int32
AF = mybir.ActivationFunctionType
ALU = mybir.AluOpType
AX = mybir.AxisListType

N, D, INNER, M, H, HD = 4096, 1024, 4096, 256, 16, 64
TILES, DC, NIB = 32, 8, 8

_SC = ["sum", "ssq", "mean", "m2", "nm2", "var", "lnv", "rstd", "mr", "nmr",
       "amax", "amc", "rc", "qs", "aqrec", "colg", "colu", "hamax", "hqs",
       "colt", "c1024inv", "cneg1", "cneghalf", "eps", "srt", "hamc", "hrc"]
SCOL = {n: i for i, n in enumerate(_SC)}


def build_program(dbg=(), phases=("p0", "p1a", "p1b", "p2a", "p3a", "p3b"),
                  n_tiles=None):
    n_t = n_tiles or TILES
    n_rows = n_t * 128
    nc = bass.Bass("TRN2", target_bir_lowering=False, dynamic_dma_scratch_size=512)

    x = nc.dram_tensor("x", [n_rows, D], F32, kind="ExternalInput").ap()
    wq = nc.dram_tensor("wq", [D, D], F32, kind="ExternalInput").ap()
    wk = nc.dram_tensor("wk", [D, D], F32, kind="ExternalInput").ap()
    wv = nc.dram_tensor("wv", [D, D], F32, kind="ExternalInput").ap()
    omt = nc.dram_tensor("omt", [HD, M], F32, kind="ExternalInput").ap()
    wgt = nc.dram_tensor("wgt", [D, INNER], F32, kind="ExternalInput").ap()
    wut = nc.dram_tensor("wut", [D, INNER], F32, kind="ExternalInput").ap()
    wdt = nc.dram_tensor("wdt", [INNER, D], F32, kind="ExternalInput").ap()
    y = nc.dram_tensor("y", [n_rows, D], F32, kind="ExternalOutput").ap()

    def scr(name, shape, dt):
        kind = "ExternalOutput" if name in dbg else "Internal"
        return nc.dram_tensor(name, shape, dt, kind=kind).ap()

    hq_scr = scr("hq_scr", [n_rows, INNER], BF16)
    y1_scr = scr("y1_scr", [n_rows, D], F32)
    y2_scr = scr("y2_scr", [n_rows, D], F32)
    qb_scr = scr("qb_scr", [n_rows, D], BF16)
    q2_scr = scr("q2_scr", [n_rows, D], BF16)
    gt_scr = scr("gt_scr", [D, INNER], BF16)
    ut_scr = scr("ut_scr", [D, INNER], BF16)
    dt_scr = scr("dt_scr", [INNER, D], BF16)

    sems, ctr = {}, {}

    def new_sem(nm):
        sems[nm] = nc.ctx.enter_context(nc.semaphore(nm))
        ctr[nm] = 0
        return nm

    for nm in ("sA", "sV", "sP", "sT"):
        new_sem(nm)
    WAITED = {e: {} for e in ("sync", "scalar", "vector", "gpsimd", "tensor")}
    store_sems = []

    def dma_pair(stream, store=False):
        pair = []
        for i in range(2):
            nm = new_sem(f"d{stream}{i}")
            pair.append(nm)
            if store:
                store_sems.append(nm)
        return pair

    def dma_one(stream, store=False):
        nm = new_sem(f"d{stream}")
        if store:
            store_sems.append(nm)
        return nm

    def inc(instr, sem, n=1):
        ctr[sem] += n
        instr.then_inc(sems[sem], n)
        return ctr[sem]

    def dma(eng, sem, out, in_):
        v = inc(eng.dma_start(out=out, in_=in_), sem, 16)
        return (sem, v)

    def wait(eng, ename, sem, val):
        if val and WAITED[ename].get(sem, 0) < val:
            eng.wait_ge(sems[sem], val)
            WAITED[ename][sem] = val

    def wait_t(eng, ename, tok):
        if tok is not None:
            wait(eng, ename, tok[0], tok[1])

    def wait_stores(eng, ename):
        for s in store_sems:
            wait(eng, ename, s, ctr[s])

    def dinc(eng, sem, n=1):
        return inc(eng.drain(), sem, n)

    cm = nc.ctx.enter_context
    ident_bf = cm(nc.sbuf_tensor("ident_bf", [128, 128], BF16))
    omdup_bf = cm(nc.sbuf_tensor("omdup_bf", [128, M], BF16))
    neghalf_bf = cm(nc.sbuf_tensor("neghalf_bf", [128, M], BF16))
    onesbf_col = cm(nc.sbuf_tensor("onesbf_col", [128, 1], BF16))
    onesf_col = cm(nc.sbuf_tensor("onesf_col", [128, 1], F32))
    onesrow = cm(nc.sbuf_tensor("onesrow", [1, 128], F32))
    scols = cm(nc.sbuf_tensor("scols", [128, 64], F32))
    wcols = cm(nc.sbuf_tensor("wcols", [128, 8], F32))
    sdh_col = cm(nc.sbuf_tensor("sdh_col", [128, 1], F32))
    hrecip_all = cm(nc.sbuf_tensor("hrecip_all", [128, TILES], F32))
    kvaug_bf = cm(nc.sbuf_tensor("kvaug_bf", [128, 2 * H * 65], BF16))
    sqjunk = cm(nc.sbuf_tensor("sqjunk", [128, D], BF16))
    sqjunk2 = cm(nc.sbuf_tensor("sqjunk2", [128, D], BF16))
    zcol = cm(nc.sbuf_tensor("zcol", [128, H], F32))
    pbk = [cm(nc.psum_tensor(f"pb{i}", [128, 512], F32)) for i in range(8)]

    def sc(nm, par=0):
        i = SCOL[nm] + 32 * (par % 2)
        return scols[:, i:i + 1]

    _uid = [0]

    def _nm(p):
        _uid[0] += 1
        return f"{p}{_uid[0]}"

    def finish():
        with nc.Block() as blk:
            @blk.gpsimd
            def _(g):
                wait_stores(g, "gpsimd")
        return nc

    p0 = contextlib.ExitStack()
    wtmp0 = [p0.enter_context(nc.sbuf_tensor(f"p0w{i}", [128, INNER], F32))
             for i in range(2)]
    accs = p0.enter_context(nc.sbuf_tensor("accs", [128, 28], F32))

    with nc.Block() as blk:
        @blk.vector
        def _(v):
            v.memset(scols[:], 0.0)
            v.memset(onesf_col[:], 1.0)
            v.memset(onesrow[:], 1.0)
            v.memset(neghalf_bf[:], -0.5)
            v.memset(kvaug_bf[:], 0.0)
            v.memset(zcol[:], 0.0)

        @blk.gpsimd
        def _(g):
            g.memset(onesbf_col[:], 1.0)
            g.memset(sqjunk[:], 0.0)
            g.memset(accs[:], 0.0)
            g.iota(ident_bf[:, :], [[1, 128]], base=0, channel_multiplier=-1,
                   allow_small_or_imprecise_dtypes=True)

    with nc.Block() as blk:
        @blk.vector
        def _(v):
            v.tensor_scalar(ident_bf[:, :], ident_bf[:, :], 0.0, None, ALU.is_equal)

        @blk.gpsimd
        def _(g):
            for par in range(2):
                g.memset(sc("c1024inv", par), 1.0 / 1024.0)
                g.memset(sc("cneg1", par), -1.0)
                g.memset(sc("cneghalf", par), -0.5)
                g.memset(sc("eps", par), 1e-5)

    chunks = []
    for mi, w in enumerate((wgt, wut)):
        for i in range(8):
            chunks.append((mi, i, w[i * 128:(i + 1) * 128, :]))
    for i in range(8):
        chunks.append((2, i, wdt[i * 512:(i + 1) * 512, :]
                       .rearrange("(p r) n -> p (r n)", p=128)))
    sp0 = dma_pair("p0w")
    d0, v0base = [], ctr["sV"]
    with nc.Block() as blk:
        @blk.sync
        def _(sy):
            for j, (mi, i, ap) in enumerate(chunks):
                if j >= 2:
                    wait(sy, "sync", "sV", v0base + j - 1)
                d0.append(dma(sy, sp0[j % 2], wtmp0[j % 2][:], ap))

        @blk.vector
        def _(v):
            for j, (mi, i, ap) in enumerate(chunks):
                wait_t(v, "vector", d0[j])
                inc(v.reduce_sum(accs[:, mi * 8 + i:mi * 8 + i + 1],
                                 wtmp0[j % 2][:], AX.X,
                                 apply_absolute_value=True), "sV")

    v0tok = [None]
    with nc.Block() as blk:
        @blk.vector
        def _(v):
            v.drain()
            for mi in range(3):
                v.reduce_sum(accs[:, 24 + mi:25 + mi], accs[:, mi * 8:(mi + 1) * 8],
                             AX.X)
            v0tok[0] = dinc(v, "sV")

        @blk.tensor
        def _(t):
            wait(t, "tensor", "sV", v0tok[0])
            last = None
            for mi in range(3):
                last = t.matmul(pbk[0][0:1, mi:mi + 1], accs[:, 24 + mi:25 + mi],
                                onesf_col[:, :], start=(mi == 0), stop=(mi == 2))
            inc(last, "sT")

    t0a = ctr["sT"]
    with nc.Block() as blk:
        @blk.vector
        def _(v):
            wait(v, "vector", "sT", t0a)
            v.tensor_copy(accs[0:1, 0:3], pbk[0][0:1, 0:3])
            v.drain()
            v.tensor_scalar(accs[0:1, 4:7], accs[0:1, 0:3], 1.0 / 4194304.0,
                            1e-5, ALU.mult, ALU.max)
            v.drain()
            v.reciprocal(accs[0:1, 8:11], accs[0:1, 4:7])
            v0tok[0] = dinc(v, "sV")

        @blk.tensor
        def _(t):
            wait(t, "tensor", "sV", v0tok[0])
            last = None
            for j in range(6):
                src = accs[0:1, 4 + j:5 + j] if j < 3 else \
                    accs[0:1, 8 + j - 3:9 + j - 3]
                last = t.matmul(pbk[1][:, j:j + 1], onesrow[0:1, :], src,
                                start=(j == 0), stop=(j == 5))
            inc(last, "sT")

    t0b = ctr["sT"]
    with nc.Block() as blk:
        @blk.vector
        def _(v):
            wait(v, "vector", "sT", t0b)
            v.tensor_copy(wcols[:, 0:6], pbk[1][:, 0:6])
            v.drain()
            v.tensor_scalar(sdh_col[:, :], wcols[:, 2:3], 0.5, None, ALU.mult)
            v.drain()
    p0.close()

    def emit_stats_A(s, src_f, lnf_out, v_var_tok, v_nmr_tok, par=0):
        s.activation(sqjunk[:, :], src_f, AF.Copy, accum_out=sc("sum", par))
        s.activation(sqjunk2[:, :], src_f, AF.Square, accum_out=sc("ssq", par))
        dinc(s, "sA")
        wait(s, "scalar", "sV", v_var_tok)
        s.activation(sc("srt", par), sc("var", par), AF.Sqrt, bias=sc("eps", par))
        dinc(s, "sA")
        wait(s, "scalar", "sV", v_nmr_tok)
        s.activation(lnf_out, src_f, AF.Identity, scale=sc("rstd", par),
                     bias=sc("nmr", par))
        return dinc(s, "sA")

    def emit_stats_V(v, a_acc_tok, a_srt_tok, par=0):
        wait(v, "vector", "sA", a_acc_tok)
        v.tensor_scalar(sc("mean", par), sc("sum", par), 1.0 / 1024.0, None,
                        ALU.mult)
        v.drain()
        v.tensor_tensor(sc("m2", par), sc("mean", par), sc("mean", par),
                        ALU.mult)
        v.drain()
        v.scalar_tensor_tensor(sc("var", par), sc("ssq", par), 1.0 / 1024.0,
                               sc("m2", par), ALU.mult, ALU.subtract)
        t1 = dinc(v, "sV")
        wait(v, "vector", "sA", a_srt_tok)
        v.reciprocal(sc("rstd", par), sc("srt", par))
        v.drain()
        v.tensor_scalar(sc("nmr", par), sc("mean", par), sc("rstd", par), -1.0,
                        ALU.mult, ALU.mult)
        t2 = dinc(v, "sV")
        return t1, t2

    def emit_quant_v(v, src_f, aqi_out, rec_ap, colgu, par=0):
        v.reduce_max(sc("amax", par), src_f, AX.X, apply_absolute_value=True)
        v.drain()
        v.tensor_scalar(sc("amc", par), sc("amax", par), sc("eps", par), None,
                        ALU.max)
        v.drain()
        v.reciprocal(sc("rc", par), sc("amc", par))
        v.drain()
        v.tensor_scalar(sc("qs", par), sc("rc", par), 127.0, None, ALU.mult)
        v.drain()
        v.reciprocal(rec_ap, sc("qs", par))
        v.drain()
        if colgu:
            v.tensor_scalar(colgu[0], rec_ap, wcols[:, 0:1], None, ALU.mult)
            v.tensor_scalar(colgu[1], rec_ap, wcols[:, 1:2], None, ALU.mult)
        v.tensor_scalar(aqi_out, src_f, sc("qs", par), None, ALU.mult)
        return dinc(v, "sV")

    def emit_transposes(t, src_b, banks, n):
        last = None
        for c in range(n):
            bank = banks[(c // 4) % len(banks)]
            last = t.matmul(bank[:, (c % 4) * 128:(c % 4) * 128 + 128],
                            src_b[:, c * 128:(c + 1) * 128], ident_bf[:, :],
                            start=(c % 4 == 0), stop=(c % 4 == 3))
        return last

    def emit_wquant(wsrc_list, dst_sb_list, scr_list, wsi_list, nchunk, ccols):
        est = contextlib.ExitStack()
        wtmp = [est.enter_context(nc.sbuf_tensor(_nm("wqw"), [128, ccols], F32))
                for _ in range(2)]
        i32t = [est.enter_context(nc.sbuf_tensor(_nm("wqi"), [128, ccols], I32))
                for _ in range(2)]
        jobs = []
        for (ws, db, sd, wsi) in zip(wsrc_list, dst_sb_list, scr_list, wsi_list):
            for i in range(nchunk):
                jobs.append((ws, db, sd, wsi, i))
        spw = dma_pair(_nm("wql"))
        sps = dma_pair(_nm("wqs"), store=True)
        dt_ = []
        vbase = ctr["sV"]
        with nc.Block() as blk:
            @blk.sync
            def _(sy):
                wait_stores(sy, "sync")
                for j, (ws, db, sd, wsi, i) in enumerate(jobs):
                    if j >= 2:
                        wait(sy, "sync", "sV", vbase + 2 * (j - 1))
                    dt_.append(dma(sy, spw[j % 2], wtmp[j % 2][:],
                                   ws[i * 128:(i + 1) * 128, :]))

            @blk.scalar
            def _(s):
                for j, (ws, db, sd, wsi, i) in enumerate(jobs):
                    wait(s, "scalar", "sV", vbase + 2 * (j + 1))
                    dma(s, sps[j % 2], sd[i * 128:(i + 1) * 128, :],
                        db[:, i * ccols:(i + 1) * ccols])

            @blk.vector
            def _(v):
                for j, (ws, db, sd, wsi, i) in enumerate(jobs):
                    wait_t(v, "vector", dt_[j])
                    v.tensor_scalar(i32t[j % 2][:], wtmp[j % 2][:],
                                    wcols[:, 3 + wsi:4 + wsi], None, ALU.mult)
                    dinc(v, "sV")
                    v.tensor_scalar(db[:, i * ccols:(i + 1) * ccols],
                                    i32t[j % 2][:], -1.0, 1.0, ALU.max, ALU.min)
                    dinc(v, "sV")
        est.close()

    if "p1a" not in phases:
        return finish()
    p1 = contextlib.ExitStack()
    wg_sb = p1.enter_context(nc.sbuf_tensor("wg_sb", [128, DC * INNER], BF16))
    wu_sb = p1.enter_context(nc.sbuf_tensor("wu_sb", [128, DC * INNER], BF16))
    emit_wquant([wgt, wut], [wg_sb, wu_sb], [gt_scr, ut_scr], [0, 1], 8, INNER)

    xa = [p1.enter_context(nc.sbuf_tensor(f"xa{i}", [128, D], F32))
          for i in range(2)]
    lnf = [p1.enter_context(nc.sbuf_tensor(f"lnf{i}", [128, D], F32))
           for i in range(2)]
    aqi = p1.enter_context(nc.sbuf_tensor("aqi", [128, D], I32))
    aqb = [p1.enter_context(nc.sbuf_tensor(f"aqb{i}", [128, D], BF16))
           for i in range(3)]
    xsq1 = p1.enter_context(nc.sbuf_tensor("xsq1", [128, D], F32))
    aqT = [p1.enter_context(nc.sbuf_tensor(f"aqT{i}", [128, D], BF16))
           for i in range(3)]
    qcols = p1.enter_context(nc.sbuf_tensor("qcols", [128, 6], F32))

    def qc(kind, slot):
        base = 0 if kind == "g" else 3
        return qcols[:, base + slot:base + slot + 1]
    silu_sb = [p1.enter_context(nc.sbuf_tensor(f"silu{i}", [128, 512], F32))
               for i in range(3)]
    h_sb = p1.enter_context(nc.sbuf_tensor("h_sb", [128, INNER], F32))
    hqi = p1.enter_context(nc.sbuf_tensor("hqi", [128, 2048], I32))
    hqb = [p1.enter_context(nc.sbuf_tensor(f"hqb{i}", [128, INNER], BF16))
           for i in range(2)]

    sxin = dma_pair("xin")
    sshq = dma_pair("sthq", store=True)
    tok_x = [None] * n_t
    stok_hq = [None] * n_t

    A0, V0, T0 = ctr["sA"], ctr["sV"], ctr["sT"]
    ATm, ATq, ATh, ATe = {}, {}, {}, {}
    VHM, VVAR, VAQB, VH = {}, {}, {}, {}
    TT, GU = {}, {}
    _ca, _cv, _ct = [A0], [V0], [T0]

    def _tok(d, key, c):
        c[0] += 1
        d[key] = c[0]

    _tok(ATq, 0, _ca), _tok(ATq, 1, _ca), _tok(ATe, 0, _ca), _tok(ATe, 1, _ca)
    for _t in range(n_t):
        for _ib in range(NIB):
            _tok(ATm, (_t, _ib), _ca)
            if _ib == 2 and _t + 2 < n_t:
                _tok(ATq, _t + 2, _ca)
        _tok(ATh, (_t, 1), _ca)
        _tok(ATh, (_t, 2), _ca)
        if _t + 2 < n_t:
            _tok(ATe, _t + 2, _ca)
    for _t in (0, 1):
        _tok(VVAR, _t, _cv)
        _tok(VAQB, _t, _cv)
    for _t in range(n_t):
        for _ib in range(NIB):
            _tok(VHM, (_t, _ib), _cv)
            if _t + 2 < n_t:
                if _ib == 2:
                    _tok(VVAR, _t + 2, _cv)
                if _ib == 6:
                    _tok(VAQB, _t + 2, _cv)
        _tok(VH, (_t, 1), _cv)
        _tok(VH, (_t, 2), _cv)
    for _t in (0, 1):
        _tok(TT, _t, _ct)
    for _t in range(n_t):
        for _ib in range(NIB):
            _tok(GU, (_t, _ib), _ct)
        if _t + 2 < n_t:
            _tok(TT, _t + 2, _ct)
    GBANK = [(pbk[2], pbk[3]), (pbk[4], pbk[5]), (pbk[6], pbk[7])]

    with nc.Block() as blk:
        @blk.sync
        def _(sy):
            for t in range(n_t):
                if t >= 2:
                    wait(sy, "sync", "sV", VAQB[t - 2])
                tok_x[t] = dma(sy, sxin[t % 2], xa[t % 2][:],
                               x[t * 128:(t + 1) * 128, :])

        @blk.scalar
        def _(s):
            def sqrt_a(t):
                wait(s, "scalar", "sV", VVAR[t])
                s.activation(sc("srt", t % 2), sc("var", t % 2), AF.Sqrt,
                             bias=sc("eps", t % 2))
                tok = dinc(s, "sA")
                assert tok == ATq[t]

            def evac_a(t):
                wait(s, "scalar", "sT", TT[t])
                if t >= 3:
                    wait(s, "scalar", "sT", GU[(t - 3, 7)])
                s.copy(aqT[t % 3][:, 0:512], pbk[0][:, :])
                s.copy(aqT[t % 3][:, 512:1024], pbk[1][:, :])
                tok = dinc(s, "sA")
                assert tok == ATe[t]

            sqrt_a(0), sqrt_a(1), evac_a(0), evac_a(1)
            for t in range(n_t):
                par = t % 2
                wait(s, "scalar", "sV", VAQB[t])
                for ib in range(NIB):
                    gg = 8 * t + ib
                    wait(s, "scalar", "sT", GU[(t, ib)])
                    if gg >= 3:
                        tt, ii = divmod(gg - 3, 8)
                        wait(s, "scalar", "sV", VHM[(tt, ii)])
                    tok = inc(s.activation(silu_sb[gg % 3][:],
                                           GBANK[gg % 3][0][:, :], AF.Silu,
                                           scale=qc("g", t % 3)), "sA")
                    assert tok == ATm[(t, ib)]
                    if ib == 2 and t + 2 < n_t:
                        sqrt_a(t + 2)
                wait(s, "scalar", "sV", VH[(t, 1)])
                if t >= 2:
                    wait_t(s, "scalar", stok_hq[t - 2])
                tok = inc(s.activation(hqb[par][:, 0:2048], hqi[:],
                                       AF.Identity), "sA")
                assert tok == ATh[(t, 1)]
                wait(s, "scalar", "sV", VH[(t, 2)])
                s.activation(hqb[par][:, 2048:4096], hqi[:], AF.Identity)
                tok = dinc(s, "sA")
                assert tok == ATh[(t, 2)]
                stok_hq[t] = dma(s, sshq[par],
                                 hq_scr[t * 128:(t + 1) * 128, :], hqb[par][:])
                if t + 2 < n_t:
                    evac_a(t + 2)

        @blk.vector
        def _(v):
            def fchunk(t, j):
                par = t % 2
                sx = xa[par][:]
                if j == 0:
                    wait_t(v, "vector", tok_x[t])
                    v.reduce_sum(sc("sum", par), sx, AX.X)
                    v.drain()
                    v.tensor_scalar(sc("mean", par), sc("sum", par),
                                    1.0 / 1024.0, None, ALU.mult)
                    v.drain()
                elif j == 1:
                    v.tensor_tensor(xsq1[:], sx, sx, ALU.mult)
                    v.drain()
                elif j == 2:
                    v.reduce_sum(sc("ssq", par), xsq1[:], AX.X)
                    v.drain()
                    v.tensor_tensor(sc("m2", par), sc("mean", par),
                                    sc("mean", par), ALU.mult)
                    v.drain()
                    v.scalar_tensor_tensor(sc("var", par), sc("ssq", par),
                                           1.0 / 1024.0, sc("m2", par),
                                           ALU.mult, ALU.subtract)
                    tok = dinc(v, "sV")
                    assert tok == VVAR[t]
                elif j == 3:
                    wait(v, "vector", "sA", ATq[t])
                    v.reciprocal(sc("rstd", par), sc("srt", par))
                    v.drain()
                    v.tensor_scalar(sc("nmr", par), sc("mean", par),
                                    sc("rstd", par), -1.0, ALU.mult, ALU.mult)
                    v.drain()
                    v.tensor_scalar(lnf[par][:], sx, sc("rstd", par),
                                    sc("nmr", par), ALU.mult, ALU.add)
                    v.drain()
                elif j == 4:
                    v.reduce_max(sc("amax", par), lnf[par][:], AX.X,
                                 apply_absolute_value=True)
                    v.drain()
                    v.tensor_scalar(sc("amc", par), sc("amax", par),
                                    sc("eps", par), None, ALU.max)
                    v.drain()
                    v.reciprocal(sc("rc", par), sc("amc", par))
                    v.drain()
                elif j == 5:
                    v.tensor_scalar(sc("qs", par), sc("rc", par), 127.0, None,
                                    ALU.mult)
                    v.drain()
                    v.reciprocal(sc("aqrec", par), sc("qs", par))
                    v.drain()
                    v.tensor_scalar(qc("g", t % 3), sc("aqrec", par),
                                    wcols[:, 0:1], None, ALU.mult)
                    v.tensor_scalar(qc("u", t % 3), sc("aqrec", par),
                                    wcols[:, 1:2], None, ALU.mult)
                    v.drain()
                elif j == 6:
                    v.tensor_scalar(aqi[:], lnf[par][:], sc("qs", par), None,
                                    ALU.mult)
                    v.drain()
                    if t >= 3:
                        wait(v, "vector", "sT", TT[t - 1])
                    v.tensor_copy(aqb[t % 3][:], aqi[:])
                    tok = dinc(v, "sV")
                    assert tok == VAQB[t]

            for k in (0, 1):
                for j in range(7):
                    fchunk(k, j)
            for t in range(n_t):
                par = t % 2
                for ib in range(NIB):
                    gg = 8 * t + ib
                    wait(v, "vector", "sT", GU[(t, ib)])
                    wait(v, "vector", "sA", ATm[(t, ib)])
                    tok = inc(v.scalar_tensor_tensor(
                        h_sb[:, ib * 512:(ib + 1) * 512],
                        GBANK[gg % 3][1][:, :], qc("u", t % 3),
                        silu_sb[gg % 3][:], ALU.mult, ALU.mult), "sV")
                    assert tok == VHM[(t, ib)]
                    if t + 2 < n_t and ib < 7:
                        fchunk(t + 2, ib)
                v.drain()
                v.reduce_max(sc("hamax", par), h_sb[:], AX.X,
                             apply_absolute_value=True)
                v.drain()
                v.tensor_scalar(sc("hamc", par), sc("hamax", par),
                                sc("eps", par), None, ALU.max)
                v.drain()
                v.reciprocal(sc("hrc", par), sc("hamc", par))
                v.drain()
                v.tensor_scalar(sc("hqs", par), sc("hrc", par), 127.0, None,
                                ALU.mult)
                v.drain()
                v.reciprocal(hrecip_all[:, t:t + 1], sc("hqs", par))
                v.drain()
                if t >= 1:
                    wait(v, "vector", "sA", ATh[(t - 1, 2)])
                v.tensor_scalar(hqi[:], h_sb[:, 0:2048], sc("hqs", par), None,
                                ALU.mult)
                tok = dinc(v, "sV")
                assert tok == VH[(t, 1)]
                wait(v, "vector", "sA", ATh[(t, 1)])
                v.tensor_scalar(hqi[:], h_sb[:, 2048:4096], sc("hqs", par),
                                None, ALU.mult)
                tok = dinc(v, "sV")
                assert tok == VH[(t, 2)]

        @blk.tensor
        def _(te):
            def trans(t):
                wait(te, "tensor", "sV", VAQB[t])
                if t >= 1:
                    wait(te, "tensor", "sA", ATe[t - 1])
                tok = inc(emit_transposes(te, aqb[t % 3][:, :],
                                          [pbk[0], pbk[1]], 8), "sT")
                assert tok == TT[t]

            trans(0), trans(1)
            for t in range(n_t):
                wait(te, "tensor", "sA", ATe[t])
                for ib in range(NIB):
                    gg = 8 * t + ib
                    if gg >= 3:
                        tt, ii = divmod(gg - 3, 8)
                        wait(te, "tensor", "sV", VHM[(tt, ii)])
                    gb, ub = GBANK[gg % 3]
                    for c in range(DC):
                        te.matmul(gb[:, :],
                                  aqT[t % 3][:, c * 128:(c + 1) * 128],
                                  wg_sb[:, c * INNER + ib * 512:
                                        c * INNER + ib * 512 + 512],
                                  start=(c == 0), stop=(c == DC - 1))
                    last = None
                    for c in range(DC):
                        last = te.matmul(ub[:, :],
                                         aqT[t % 3][:, c * 128:(c + 1) * 128],
                                         wu_sb[:, c * INNER + ib * 512:
                                               c * INNER + ib * 512 + 512],
                                         start=(c == 0), stop=(c == DC - 1))
                    tok = inc(last, "sT")
                    assert tok == GU[(t, ib)]
                if t + 2 < n_t:
                    trans(t + 2)

    assert ctr["sA"] == _ca[0]
    assert ctr["sV"] == _cv[0]
    assert ctr["sT"] == _ct[0]
    p1.close()

    def run_down_phase(preamble, hq_src, res_src, out_dst, final):
        pd = contextlib.ExitStack()
        wd_sb = pd.enter_context(nc.sbuf_tensor(_nm("wd_sb"),
                                                [128, 32 * 1024], BF16))
        preamble(wd_sb)
        hqa = [pd.enter_context(nc.sbuf_tensor(_nm("hqa"), [128, INNER], BF16))
               for _ in range(2)]
        resa = [pd.enter_context(nc.sbuf_tensor(_nm("resa"), [128, D], F32))
                for _ in range(2)]
        hqT = [pd.enter_context(nc.sbuf_tensor(_nm("hqT"), [128, INNER], BF16))
               for _ in range(2)]
        y3sb = [pd.enter_context(nc.sbuf_tensor(_nm("y3sb"), [128, D], F32))
                for _ in range(2)]
        sdlh = dma_pair(_nm("dnlh"))
        sdlr = dma_pair(_nm("dnlr"))
        sds = dma_pair(_nm("dns"), store=True)
        tok_hq = [None] * n_t
        tok_res = [None] * n_t
        stok = [None] * n_t

        T0d, A0d, V0d = ctr["sT"], ctr["sA"], ctr["sV"]
        _pe_off = [1, 2, 3, 4, 6, 8, 10, 12]
        _dg_off = [5, 7, 9, 11, 13, 14, 15, 16]

        def TG(t, g):
            return T0d + 16 * t + _pe_off[g]

        def DG(t, dg):
            return T0d + 16 * t + _dg_off[dg]

        def EV(t, g):
            return A0d + 8 * t + 1 + g

        def Y3(t):
            return V0d + t + 1

        with nc.Block() as blk:
            @blk.sync
            def _(sy):
                wait_stores(sy, "sync")
                for t in range(n_t):
                    if t >= 2:
                        wait(sy, "sync", "sT", TG(t - 2, 7))
                        wait(sy, "sync", "sV", Y3(t - 2))
                    tok_hq[t] = dma(sy, sdlh[t % 2], hqa[t % 2][:],
                                    hq_src[t * 128:(t + 1) * 128, :])
                    tok_res[t] = dma(sy, sdlr[t % 2], resa[t % 2][:],
                                     res_src[t * 128:(t + 1) * 128, :])

            @blk.scalar
            def _(s):
                for t in range(n_t):
                    for g in range(8):
                        wait(s, "scalar", "sT", TG(t, g))
                        if t >= 2:
                            wait(s, "scalar", "sT", DG(t - 2, g))
                        s.copy(hqT[t % 2][:, g * 512:(g + 1) * 512],
                               pbk[g % 4][:, :])
                        tok = dinc(s, "sA")
                        assert tok == EV(t, g)
                    wait(s, "scalar", "sV", Y3(t))
                    stok[t] = dma(s, sds[t % 2],
                                  out_dst[t * 128:(t + 1) * 128, :],
                                  y3sb[t % 2][:])

            @blk.vector
            def _(v):
                for t in range(n_t):
                    dnb = (pbk[4], pbk[5]) if t % 2 == 0 else (pbk[6], pbk[7])
                    wait(v, "vector", "sT", DG(t, 7))
                    wait_t(v, "vector", tok_res[t])
                    if t >= 2:
                        wait_t(v, "vector", stok[t - 2])
                    v.tensor_scalar(sc("colt"), hrecip_all[:, t:t + 1],
                                    sdh_col[:, :], None, ALU.mult)
                    v.drain()
                    v.scalar_tensor_tensor(y3sb[t % 2][:, 0:512], dnb[0][:, :],
                                           sc("colt"), resa[t % 2][:, 0:512],
                                           ALU.mult, ALU.add)
                    v.scalar_tensor_tensor(y3sb[t % 2][:, 512:1024],
                                           dnb[1][:, :], sc("colt"),
                                           resa[t % 2][:, 512:1024],
                                           ALU.mult, ALU.add)
                    tok = inc(v.drain(), "sV")
                    assert tok == Y3(t)

            @blk.tensor
            def _(te):
                for t in range(n_t):
                    dnb = (pbk[4], pbk[5]) if t % 2 == 0 else (pbk[6], pbk[7])

                    def tgrp(g, t=t):
                        if t >= 1:
                            wait(te, "tensor", "sA", EV(t - 1, 4 + (g % 4)))
                        last = None
                        for k in range(4):
                            c = g * 4 + k
                            last = te.matmul(
                                pbk[g % 4][:, k * 128:(k + 1) * 128],
                                hqa[t % 2][:, c * 128:(c + 1) * 128],
                                ident_bf[:, :], start=(k == 0), stop=(k == 3))
                        tok = inc(last, "sT")
                        assert tok == TG(t, g)

                    def dgrp(dg, t=t, dnb=dnb):
                        wait(te, "tensor", "sA", EV(t, dg))
                        if dg == 0 and t >= 2:
                            wait(te, "tensor", "sV", Y3(t - 2))
                        last = None
                        for k in range(4):
                            ci = dg * 4 + k
                            te.matmul(dnb[0][:, :],
                                      hqT[t % 2][:, ci * 128:(ci + 1) * 128],
                                      wd_sb[:, ci * 1024:ci * 1024 + 512],
                                      start=(ci == 0), stop=(ci == 31))
                            last = te.matmul(
                                dnb[1][:, :],
                                hqT[t % 2][:, ci * 128:(ci + 1) * 128],
                                wd_sb[:, ci * 1024 + 512:ci * 1024 + 1024],
                                start=(ci == 0), stop=(ci == 31))
                        tok = inc(last, "sT")
                        assert tok == DG(t, dg)

                    wait_t(te, "tensor", tok_hq[t])
                    for g in range(4):
                        tgrp(g)
                    for g in range(4):
                        dgrp(g)
                        tgrp(4 + g)
                    for dg in range(4, 8):
                        dgrp(dg)

            assert ctr["sT"] == T0d + 16 * n_t
            assert ctr["sA"] == A0d + 8 * n_t
            assert ctr["sV"] == V0d + n_t

        pd.close()

    def p1b_pre(wd_sb):
        emit_wquant([wdt], [wd_sb], [dt_scr], [2], 32, 1024)

    if "p1b" not in phases:
        return finish()
    run_down_phase(p1b_pre, hq_scr, x, y1_scr, final=False)

    if "p2a" not in phases:
        return finish()
    p2 = contextlib.ExitStack()
    wq_sb = p2.enter_context(nc.sbuf_tensor("wq_sb", [128, DC * D], BF16))
    wk_sb = p2.enter_context(nc.sbuf_tensor("wk_sb", [128, DC * D], BF16))
    wv_sb = p2.enter_context(nc.sbuf_tensor("wv_sb", [128, DC * D], BF16))

    pre = contextlib.ExitStack()
    wtmp2 = [pre.enter_context(nc.sbuf_tensor(_nm("w2"), [128, D], F32))
             for _ in range(2)]
    jobs2 = [(w, dst, i) for (w, dst) in ((wq, wq_sb), (wk, wk_sb), (wv, wv_sb))
             for i in range(8)]
    sp2w = dma_pair("p2w")
    d2, v2base = [], ctr["sV"]
    with nc.Block() as blk:
        @blk.sync
        def _(sy):
            wait_stores(sy, "sync")
            for j, (w, dst, i) in enumerate(jobs2):
                if j >= 2:
                    wait(sy, "sync", "sV", v2base + j - 1)
                d2.append(dma(sy, sp2w[j % 2], wtmp2[j % 2][:],
                              w[i * 128:(i + 1) * 128, :]))

        @blk.vector
        def _(v):
            for j, (w, dst, i) in enumerate(jobs2):
                wait_t(v, "vector", d2[j])
                v.tensor_copy(dst[:, i * D:(i + 1) * D], wtmp2[j % 2][:])
                dinc(v, "sV")

    sp2o = dma_one("p2o")
    with nc.Block() as blk:
        @blk.sync
        def _(sy):
            dma(sy, sp2o, wtmp2[0][0:64, 0:M], omt)
            dma(sy, sp2o, wtmp2[0][64:128, 0:M], omt)

        @blk.vector
        def _(v):
            wait(v, "vector", sp2o, ctr[sp2o])
            v.tensor_copy(omdup_bf[:, :], wtmp2[0][:, 0:M])
            v.drain()
    pre.close()

    ya = [p2.enter_context(nc.sbuf_tensor(f"ya{i}", [128, D], F32))
          for i in range(2)]
    lnb_sb = p2.enter_context(nc.sbuf_tensor("lnb_sb", [128, D], BF16))
    lnT_sb = p2.enter_context(nc.sbuf_tensor("lnT_sb", [128, D], BF16))
    k_f = p2.enter_context(nc.sbuf_tensor("k_f", [128, D], F32))
    q_f = p2.enter_context(nc.sbuf_tensor("q_f", [128, D], F32))
    k_b = p2.enter_context(nc.sbuf_tensor("k_b", [128, D], BF16))
    k2_b = p2.enter_context(nc.sbuf_tensor("k2_b", [128, D], BF16))
    v_b = p2.enter_context(nc.sbuf_tensor("v_b", [128, D], BF16))
    pk_b = p2.enter_context(nc.sbuf_tensor("pk_b", [128, H * M], BF16))
    qb_sb = [p2.enter_context(nc.sbuf_tensor(f"qb{i}", [128, D], BF16))
             for i in range(2)]
    q2_sb = [p2.enter_context(nc.sbuf_tensor(f"q2{i}", [128, D], BF16))
             for i in range(2)]

    sy1in = dma_pair("y1in")
    sstq = dma_pair("stq", store=True)
    sstq2 = dma_pair("stq2", store=True)
    tok_y1 = [None] * (n_t + 1)
    stok_q = [None] * n_t
    stok_q2 = [None] * n_t

    with nc.Block() as blk:
        @blk.sync
        def _(sy):
            wait_stores(sy, "sync")
            tok_y1[0] = dma(sy, sy1in[0], ya[0][:], y1_scr[0:128, :])

    for t in range(n_t):
        tb, ab, vb = ctr["sT"], ctr["sA"], ctr["sV"]
        with nc.Block() as blk:
            @blk.sync
            def _(sy, t=t):
                if t + 1 < n_t:
                    tok_y1[t + 1] = dma(sy, sy1in[(t + 1) % 2],
                                        ya[(t + 1) % 2][:],
                                        y1_scr[(t + 1) * 128:(t + 2) * 128, :])

            @blk.scalar
            def _(s, t=t, tb=tb, vb=vb):
                wait_t(s, "scalar", tok_y1[t])
                emit_stats_A(s, ya[t % 2][:], lnb_sb[:], vb + 1, vb + 2)
                wait(s, "scalar", "sT", tb + 1)
                s.copy(lnT_sb[:, 0:512], pbk[5][:, :])
                s.copy(lnT_sb[:, 512:1024], pbk[6][:, :])
                dinc(s, "sA")
                wait(s, "scalar", "sT", tb + 2)
                s.copy(k_f[:, 0:512], pbk[5][:, :])
                s.copy(k_f[:, 512:1024], pbk[6][:, :])
                dinc(s, "sA")
                s.activation(k2_b[:, :], k_f[:, :], AF.Square)
                dinc(s, "sA")
                wait(s, "scalar", "sT", tb + 4)
                s.copy(q_f[:, 0:512], pbk[5][:, :])
                s.copy(q_f[:, 512:1024], pbk[6][:, :])
                dinc(s, "sA")
                if t >= 2:
                    wait_t(s, "scalar", stok_q2[t - 2])
                s.activation(q2_sb[t % 2][:, :], q_f[:, :], AF.Square)
                dinc(s, "sA")
                for hh in range(H):
                    wait(s, "scalar", "sT", tb + 5 + hh)
                    s.activation(
                        pk_b[:, hh * M:(hh + 1) * M],
                        pbk[5 + (hh % 2)][:, 0:M], AF.Exp)
                    dinc(s, "sA")
                wait(s, "scalar", "sV", vb + 5)
                stok_q[t] = dma(s, sstq[t % 2],
                                qb_scr[t * 128:(t + 1) * 128, :], qb_sb[t % 2][:])
                stok_q2[t] = dma(s, sstq2[t % 2],
                                 q2_scr[t * 128:(t + 1) * 128, :],
                                 q2_sb[t % 2][:])

            @blk.vector
            def _(v, t=t, tb=tb, ab=ab):
                emit_stats_V(v, ab + 1, ab + 2)
                wait(v, "vector", "sA", ab + 5)
                v.tensor_copy(k_b[:, :], k_f[:, :])
                dinc(v, "sV")
                wait(v, "vector", "sT", tb + 3)
                v.tensor_copy(v_b[:, 0:512], pbk[5][:, :])
                v.tensor_copy(v_b[:, 512:1024], pbk[6][:, :])
                dinc(v, "sV")
                wait(v, "vector", "sA", ab + 7)
                if t >= 2:
                    wait_t(v, "vector", stok_q[t - 2])
                v.tensor_copy(qb_sb[t % 2][:, :], q_f[:, :])
                inc(v.drain(), "sV")

            @blk.tensor
            def _(te, t=t, ab=ab, vb=vb):
                wait(te, "tensor", "sA", ab + 3)
                inc(emit_transposes(te, lnb_sb[:, :], [pbk[5], pbk[6]], 8), "sT")
                wait(te, "tensor", "sA", ab + 4)
                last = None
                for co in range(8):
                    bank = pbk[5] if co < 4 else pbk[6]
                    sl = bank[:, (co % 4) * 128:(co % 4) * 128 + 128]
                    for cd in range(DC):
                        last = te.matmul(
                            sl, wk_sb[:, cd * D + co * 128:cd * D + co * 128 + 128],
                            lnT_sb[:, cd * 128:(cd + 1) * 128],
                            start=(co % 4 == 0 and cd == 0),
                            stop=(co % 4 == 3 and cd == DC - 1))
                inc(last, "sT")
                wait(te, "tensor", "sA", ab + 5)
                last = None
                for half in range(2):
                    bank = pbk[5] if half == 0 else pbk[6]
                    for cd in range(DC):
                        last = te.matmul(
                            bank[:, :], lnT_sb[:, cd * 128:(cd + 1) * 128],
                            wv_sb[:, cd * D + half * 512:cd * D + half * 512 + 512],
                            start=(cd == 0), stop=(cd == DC - 1))
                inc(last, "sT")
                wait(te, "tensor", "sV", vb + 4)
                last = None
                for co in range(8):
                    bank = pbk[5] if co < 4 else pbk[6]
                    sl = bank[:, (co % 4) * 128:(co % 4) * 128 + 128]
                    for cd in range(DC):
                        last = te.matmul(
                            sl, wq_sb[:, cd * D + co * 128:cd * D + co * 128 + 128],
                            lnT_sb[:, cd * 128:(cd + 1) * 128],
                            start=(co % 4 == 0 and cd == 0),
                            stop=(co % 4 == 3 and cd == DC - 1))
                inc(last, "sT")
                wait(te, "tensor", "sV", vb + 3)
                wait(te, "tensor", "sA", ab + 7)
                for hh in range(H):
                    if hh >= 2:
                        wait(te, "tensor", "sA", ab + 9 + (hh - 2))
                    par = hh % 2
                    sl = pbk[5 + par][:, 0:M]
                    ksl = k_b[par * 64:par * 64 + 64,
                              (hh // 2) * 128:(hh // 2) * 128 + 128]
                    k2sl = k2_b[par * 64:par * 64 + 64,
                                (hh // 2) * 128:(hh // 2) * 128 + 128]
                    te.matmul(sl, ksl, omdup_bf[par * 64:par * 64 + 64, :],
                              start=True, stop=False)
                    inc(te.matmul(sl, k2sl, neghalf_bf[par * 64:par * 64 + 64, :],
                                  start=False, stop=True), "sT")
                last = None
                for hh in range(H):
                    wait(te, "tensor", "sA", ab + 9 + hh)
                    hl = hh - 8 if hh >= 8 else hh
                    for mc in range(2):
                        kvb = pbk[mc * 2 + (1 if hh >= 8 else 0)]
                        lhs = pk_b[:, hh * M + mc * 128:hh * M + mc * 128 + 128]
                        te.matmul(kvb[:, hl * 64:hl * 64 + 64], lhs,
                                  v_b[:, hh * 64:hh * 64 + 64],
                                  start=(t == 0 and hl == 0),
                                  stop=(t == n_t - 1 and hl == 7))
                        last = te.matmul(pbk[4][:, mc * 16 + hh:mc * 16 + hh + 1],
                                         lhs, onesbf_col[:, :],
                                         start=(t == 0 and hh == 0 and mc == 0),
                                         stop=(t == n_t - 1 and hh == H - 1
                                               and mc == 1))
                inc(last, "sT")

        assert ctr["sT"] == tb + 21, (ctr["sT"], tb)
        assert ctr["sA"] == ab + 8 + H
        assert ctr["sV"] == vb + 5

    with nc.Block() as blk:
        @blk.vector
        def _(v):
            for mc in range(2):
                for hh in range(H):
                    kvb = pbk[mc * 2 + (1 if hh >= 8 else 0)]
                    hl = hh - 8 if hh >= 8 else hh
                    base = mc * (H * 65) + hh * 65
                    v.tensor_copy(kvaug_bf[:, base:base + 64],
                                  kvb[:, hl * 64:hl * 64 + 64])
                    v.tensor_copy(kvaug_bf[:, base + 64:base + 65],
                                  pbk[4][:, mc * 16 + hh:mc * 16 + hh + 1])
            v.drain()
    p2.close()

    if "p3a" not in phases:
        return finish()
    p3 = contextlib.ExitStack()
    wg3_sb = p3.enter_context(nc.sbuf_tensor("wg3_sb", [128, DC * INNER], BF16))
    wu3_sb = p3.enter_context(nc.sbuf_tensor("wu3_sb", [128, DC * INNER], BF16))
    stern = dma_one("tern")
    with nc.Block() as blk:
        @blk.sync
        def _(sy):
            wait_stores(sy, "sync")
            for i in range(8):
                dma(sy, stern, wg3_sb[:, i * INNER:(i + 1) * INNER],
                    gt_scr[i * 128:(i + 1) * 128, :])
            for i in range(8):
                dma(sy, stern, wu3_sb[:, i * INNER:(i + 1) * INNER],
                    ut_scr[i * 128:(i + 1) * 128, :])

    y1a = [p3.enter_context(nc.sbuf_tensor(f"y1a{i}", [128, D], F32))
           for i in range(2)]
    qba = [p3.enter_context(nc.sbuf_tensor(f"qba{i}", [128, D], BF16))
           for i in range(2)]
    q2a = [p3.enter_context(nc.sbuf_tensor(f"q2a{i}", [128, D], BF16))
           for i in range(2)]
    pq_b = p3.enter_context(nc.sbuf_tensor("pq_b", [128, 32 * 128], BF16))
    y2_sb = [p3.enter_context(nc.sbuf_tensor(f"y2sb{i}", [128, D], F32))
             for i in range(2)]
    lnf3 = p3.enter_context(nc.sbuf_tensor("lnf3", [128, D], F32))
    aqi3 = p3.enter_context(nc.sbuf_tensor("aqi3", [128, D], I32))
    aqb3 = p3.enter_context(nc.sbuf_tensor("aqb3", [128, D], BF16))
    aqT3 = p3.enter_context(nc.sbuf_tensor("aqT3", [128, D], BF16))
    silu3 = [p3.enter_context(nc.sbuf_tensor(f"silu3{i}", [128, 512], F32))
             for i in range(2)]
    h3_sb = p3.enter_context(nc.sbuf_tensor("h3_sb", [128, INNER], F32))
    hqi3 = p3.enter_context(nc.sbuf_tensor("hqi3", [128, 2048], I32))
    hqb3 = p3.enter_context(nc.sbuf_tensor("hqb3", [128, INNER], BF16))
    xsq3 = p3.enter_context(nc.sbuf_tensor("xsq3", [128, D], F32))

    sp3in = dma_pair("p3in")
    ssty2 = dma_pair("sty2", store=True)
    ssth2 = dma_pair("sth2", store=True)
    tok3 = [None] * (n_t + 1)
    stok_h2 = [None] * n_t
    stok_y2 = [None] * n_t

    def p3_loads(sy, tt):
        dma(sy, sp3in[tt % 2], y1a[tt % 2][:],
            y1_scr[tt * 128:(tt + 1) * 128, :])
        dma(sy, sp3in[tt % 2], qba[tt % 2][:],
            qb_scr[tt * 128:(tt + 1) * 128, :])
        tok3[tt] = dma(sy, sp3in[tt % 2], q2a[tt % 2][:],
                       q2_scr[tt * 128:(tt + 1) * 128, :])

    with nc.Block() as blk:
        @blk.sync
        def _(sy):
            p3_loads(sy, 0)

    ATT_BANK = [4] * 7 + [5] * 7 + [6] * 2
    ATT_SLOT = list(range(7)) + list(range(7)) + list(range(2))
    GBANK3 = [(pbk[2], pbk[3]), (pbk[7], pbk[6])]

    for t in range(n_t + 1):
        att = t < n_t
        ffn = t >= 1
        u = t - 1
        na, nf = (1 if att else 0), (1 if ffn else 0)
        tb, ab, vb = ctr["sT"], ctr["sA"], ctr["sV"]
        T_ATT = tb + 5
        T_TR = tb + 5 * na + 1

        def T_GU(ib, tb=tb, na=na):
            return tb + 5 * na + 2 + ib

        A_SRT = ab + 4 * na + 1
        A_EVAC = ab + 4 * na + 2

        def A_SIL(ib, ab=ab, na=na):
            return ab + 4 * na + 3 + ib

        A_C1 = ab + 4 * na + 11
        A_C2 = ab + 4 * na + 12
        V_VAR = vb + 1
        V_AQB = vb + 2
        V_Y2 = vb + 2 * nf + 1

        def V_HM(ib, vb=vb, nf=nf, na=na):
            return vb + 2 * nf + na + 1 + ib

        V_H1 = vb + 2 * nf + na + 9
        V_H2 = vb + 2 * nf + na + 10

        with nc.Block() as blk:
            @blk.sync
            def _(sy, t=t):
                if t + 1 < n_t:
                    p3_loads(sy, t + 1)

            @blk.scalar
            def _(s, t=t, u=u, att=att, ffn=ffn, tb=tb, ab=ab):
                if att:
                    for g in range(4):
                        wait(s, "scalar", "sT", tb + 1 + g)
                        bA = pbk[0] if g % 2 == 0 else pbk[2]
                        bB = pbk[1] if g % 2 == 0 else pbk[3]
                        s.activation(pq_b[:, g * 1024:g * 1024 + 512], bA[:, :],
                                     AF.Exp)
                        s.activation(pq_b[:, g * 1024 + 512:(g + 1) * 1024],
                                     bB[:, :], AF.Exp)
                        dinc(s, "sA")
                if ffn:
                    wait(s, "scalar", "sV", V_VAR)
                    s.activation(sc("srt"), sc("var"), AF.Sqrt, bias=sc("eps"))
                    tok = dinc(s, "sA")
                    assert tok == A_SRT
                    wait(s, "scalar", "sT", T_TR)
                    s.copy(aqT3[:, 0:512], pbk[0][:, :])
                    s.copy(aqT3[:, 512:1024], pbk[1][:, :])
                    tok = dinc(s, "sA")
                    assert tok == A_EVAC
                    wait(s, "scalar", "sV", V_AQB)
                    for ib in range(NIB):
                        wait(s, "scalar", "sT", T_GU(ib))
                        if ib >= 2:
                            wait(s, "scalar", "sV", V_HM(ib - 2))
                        tok = inc(s.activation(silu3[ib % 2][:],
                                               GBANK3[ib % 2][0][:, :], AF.Silu,
                                               scale=sc("colg")), "sA")
                        assert tok == A_SIL(ib)
                    wait(s, "scalar", "sV", V_H1)
                    if u >= 1:
                        wait_t(s, "scalar", stok_h2[u - 1])
                    tok = inc(s.activation(hqb3[:, 0:2048], hqi3[:],
                                           AF.Identity), "sA")
                    assert tok == A_C1
                    wait(s, "scalar", "sV", V_H2)
                    s.activation(hqb3[:, 2048:4096], hqi3[:], AF.Identity)
                    tok = dinc(s, "sA")
                    assert tok == A_C2
                    stok_h2[u] = dma(s, ssth2[u % 2],
                                     hq_scr[u * 128:(u + 1) * 128, :], hqb3[:])
                if att:
                    wait(s, "scalar", "sV", V_Y2)
                    stok_y2[t] = dma(s, ssty2[t % 2],
                                     y2_scr[t * 128:(t + 1) * 128, :],
                                     y2_sb[t % 2][:])

            @blk.vector
            def _(v, t=t, u=u, att=att, ffn=ffn):
                if ffn:
                    src = y2_sb[u % 2][:]
                    v.reduce_sum(sc("sum"), src, AX.X)
                    v.drain()
                    v.tensor_scalar(sc("mean"), sc("sum"), 1.0 / 1024.0, None,
                                    ALU.mult)
                    v.drain()
                    v.tensor_tensor(xsq3[:], src, src, ALU.mult)
                    v.drain()
                    v.reduce_sum(sc("ssq"), xsq3[:], AX.X)
                    v.drain()
                    v.tensor_tensor(sc("m2"), sc("mean"), sc("mean"), ALU.mult)
                    v.drain()
                    v.scalar_tensor_tensor(sc("var"), sc("ssq"), 1.0 / 1024.0,
                                           sc("m2"), ALU.mult, ALU.subtract)
                    tok = dinc(v, "sV")
                    assert tok == V_VAR
                    wait(v, "vector", "sA", A_SRT)
                    v.reciprocal(sc("rstd"), sc("srt"))
                    v.drain()
                    v.tensor_scalar(sc("nmr"), sc("mean"), sc("rstd"), -1.0,
                                    ALU.mult, ALU.mult)
                    v.drain()
                    v.tensor_scalar(lnf3[:], src, sc("rstd"), sc("nmr"),
                                    ALU.mult, ALU.add)
                    v.drain()
                    v.reduce_max(sc("amax"), lnf3[:], AX.X,
                                 apply_absolute_value=True)
                    v.drain()
                    v.tensor_scalar(sc("amc"), sc("amax"), sc("eps"), None,
                                    ALU.max)
                    v.drain()
                    v.reciprocal(sc("rc"), sc("amc"))
                    v.drain()
                    v.tensor_scalar(sc("qs"), sc("rc"), 127.0, None, ALU.mult)
                    v.drain()
                    v.reciprocal(sc("aqrec"), sc("qs"))
                    v.drain()
                    v.tensor_scalar(sc("colg"), sc("aqrec"), wcols[:, 0:1],
                                    None, ALU.mult)
                    v.tensor_scalar(sc("colu"), sc("aqrec"), wcols[:, 1:2],
                                    None, ALU.mult)
                    v.drain()
                    v.tensor_scalar(aqi3[:], lnf3[:], sc("qs"), None, ALU.mult)
                    v.drain()
                    v.tensor_copy(aqb3[:], aqi3[:])
                    tok = dinc(v, "sV")
                    assert tok == V_AQB
                if att:
                    wait(v, "vector", "sT", T_ATT)
                    wait_t(v, "vector", tok3[t])
                    if t >= 2:
                        wait_t(v, "vector", stok_y2[t - 2])
                    for hh in range(H):
                        v.tensor_copy(zcol[:, hh:hh + 1],
                                      pbk[ATT_BANK[hh]][:, ATT_SLOT[hh] * 65 + 64:
                                                        ATT_SLOT[hh] * 65 + 65])
                    v.drain()
                    v.tensor_scalar(zcol[:, :], zcol[:, :], 2.56e-4, None,
                                    ALU.add)
                    v.drain()
                    v.reciprocal(zcol[:, :], zcol[:, :])
                    v.drain()
                    for hh in range(H):
                        v.scalar_tensor_tensor(
                            y2_sb[t % 2][:, hh * 64:hh * 64 + 64],
                            pbk[ATT_BANK[hh]][:, ATT_SLOT[hh] * 65:
                                              ATT_SLOT[hh] * 65 + 64],
                            zcol[:, hh:hh + 1],
                            y1a[t % 2][:, hh * 64:hh * 64 + 64],
                            ALU.mult, ALU.add)
                    tok = inc(v.drain(), "sV")
                    assert tok == V_Y2
                if ffn:
                    for ib in range(NIB):
                        wait(v, "vector", "sT", T_GU(ib))
                        wait(v, "vector", "sA", A_SIL(ib))
                        tok = inc(v.scalar_tensor_tensor(
                            h3_sb[:, ib * 512:(ib + 1) * 512],
                            GBANK3[ib % 2][1][:, :], sc("colu"),
                            silu3[ib % 2][:], ALU.mult, ALU.mult), "sV")
                        assert tok == V_HM(ib)
                    v.drain()
                    v.reduce_max(sc("hamax"), h3_sb[:], AX.X,
                                 apply_absolute_value=True)
                    v.drain()
                    v.tensor_scalar(sc("amc"), sc("hamax"), sc("eps"), None,
                                    ALU.max)
                    v.drain()
                    v.reciprocal(sc("rc"), sc("amc"))
                    v.drain()
                    v.tensor_scalar(sc("hqs"), sc("rc"), 127.0, None, ALU.mult)
                    v.drain()
                    v.reciprocal(hrecip_all[:, u:u + 1], sc("hqs"))
                    v.drain()
                    v.tensor_scalar(hqi3[:], h3_sb[:, 0:2048], sc("hqs"), None,
                                    ALU.mult)
                    tok = dinc(v, "sV")
                    assert tok == V_H1
                    wait(v, "vector", "sA", A_C1)
                    v.tensor_scalar(hqi3[:], h3_sb[:, 2048:4096], sc("hqs"),
                                    None, ALU.mult)
                    tok = dinc(v, "sV")
                    assert tok == V_H2

            @blk.tensor
            def _(te, t=t, u=u, att=att, ffn=ffn, tb=tb, ab=ab):
                if att:
                    wait_t(te, "tensor", tok3[t])
                    if t == 0:
                        wait(te, "tensor", stern, ctr[stern])
                    for g in range(4):
                        mc, par = g // 2, g % 2
                        bA = pbk[0] if g % 2 == 0 else pbk[2]
                        bB = pbk[1] if g % 2 == 0 else pbk[3]
                        if g >= 2:
                            wait(te, "tensor", "sA", ab + 1 + (g - 2))
                        oms = omdup_bf[par * 64:par * 64 + 64,
                                       mc * 128:mc * 128 + 128]
                        ngs = neghalf_bf[par * 64:par * 64 + 64, 0:128]
                        last = None
                        for c in range(8):
                            bank = bA if c < 4 else bB
                            sl = bank[:, (c % 4) * 128:(c % 4) * 128 + 128]
                            qsl = qba[t % 2][par * 64:par * 64 + 64,
                                             c * 128:(c + 1) * 128]
                            q2sl = q2a[t % 2][par * 64:par * 64 + 64,
                                              c * 128:(c + 1) * 128]
                            te.matmul(sl, oms, qsl, start=(c % 4 == 0),
                                      stop=False)
                            last = te.matmul(sl, ngs, q2sl, start=False,
                                             stop=(c % 4 == 3))
                        inc(last, "sT")
                    wait(te, "tensor", "sA", ab + 4)
                    last = None
                    bank_started = set()
                    for hh in range(H):
                        bk, slt = ATT_BANK[hh], ATT_SLOT[hh]
                        for mc in range(2):
                            g = mc * 2 + (hh % 2)
                            j = g * 8 + hh // 2
                            first = bk not in bank_started
                            bank_started.add(bk)
                            last = te.matmul(
                                pbk[bk][:, slt * 65:slt * 65 + 65],
                                pq_b[:, j * 128:(j + 1) * 128],
                                kvaug_bf[:, mc * (H * 65) + hh * 65:
                                         mc * (H * 65) + (hh + 1) * 65],
                                start=first,
                                stop=(mc == 1 and hh in (6, 13, 15)))
                    tok = inc(last, "sT")
                    assert tok == T_ATT
                if ffn:
                    wait(te, "tensor", "sV", V_AQB)
                    tok = inc(emit_transposes(te, aqb3[:, :], [pbk[0], pbk[1]],
                                              8), "sT")
                    assert tok == T_TR
                    wait(te, "tensor", "sA", A_EVAC)
                    for ib in range(NIB):
                        if ib == 1 and att:
                            wait(te, "tensor", "sV", V_Y2)
                        if ib >= 2:
                            wait(te, "tensor", "sV", V_HM(ib - 2))
                        gb, ub = GBANK3[ib % 2]
                        for c in range(DC):
                            te.matmul(gb[:, :], aqT3[:, c * 128:(c + 1) * 128],
                                      wg3_sb[:, c * INNER + ib * 512:
                                             c * INNER + ib * 512 + 512],
                                      start=(c == 0), stop=(c == DC - 1))
                        last = None
                        for c in range(DC):
                            last = te.matmul(ub[:, :],
                                             aqT3[:, c * 128:(c + 1) * 128],
                                             wu3_sb[:, c * INNER + ib * 512:
                                                    c * INNER + ib * 512 + 512],
                                             start=(c == 0), stop=(c == DC - 1))
                        tok = inc(last, "sT")
                        assert tok == T_GU(ib)

        assert ctr["sT"] == tb + 5 * na + 9 * nf, (ctr["sT"], tb)
        assert ctr["sA"] == ab + 4 * na + 12 * nf
        assert ctr["sV"] == vb + na + 12 * nf
    p3.close()

    if "p3b" not in phases:
        return finish()
    swdld = dma_one("wdld")

    def p3b_pre(wd_sb):
        with nc.Block() as blk:
            @blk.sync
            def _(sy):
                wait_stores(sy, "sync")
                for ci in range(32):
                    dma(sy, swdld, wd_sb[:, ci * 1024:(ci + 1) * 1024],
                        dt_scr[ci * 128:(ci + 1) * 128, :])
                wait(sy, "sync", swdld, ctr[swdld])

    run_down_phase(p3b_pre, hq_scr, y2_scr, y, final=True)

    return finish()


_NC_CACHE = {}


def _get_program():
    if "nc" not in _NC_CACHE:
        _NC_CACHE["nc"] = build_program()
    return _NC_CACHE["nc"]


def kernel(L0_attn_x, L0_attn_Wq, L0_attn_Wk, L0_attn_Wv, L0_attn_omega,
           L0_ffn_w_gate, L0_ffn_w_up, L0_ffn_w_down):
    xs = np.ascontiguousarray(np.asarray(L0_attn_x, dtype=np.float32))
    wq_ = np.ascontiguousarray(np.asarray(L0_attn_Wq, dtype=np.float32))
    wk_ = np.ascontiguousarray(np.asarray(L0_attn_Wk, dtype=np.float32))
    wv_ = np.ascontiguousarray(np.asarray(L0_attn_Wv, dtype=np.float32))
    omt_ = np.ascontiguousarray(np.asarray(L0_attn_omega, dtype=np.float32).T)
    wgt_ = np.ascontiguousarray(np.asarray(L0_ffn_w_gate, dtype=np.float32).T)
    wut_ = np.ascontiguousarray(np.asarray(L0_ffn_w_up, dtype=np.float32).T)
    wdt_ = np.ascontiguousarray(np.asarray(L0_ffn_w_down, dtype=np.float32).T)

    nc = _get_program()
    in_maps = [{"x": np.ascontiguousarray(xs[c]), "wq": wq_, "wk": wk_,
                "wv": wv_, "omt": omt_, "wgt": wgt_, "wut": wut_, "wdt": wdt_}
               for c in range(8)]
    res = run_bass_kernel_spmd(nc, in_maps, core_ids=list(range(8)))
    return np.stack([np.asarray(res.results[c]["y"], dtype=np.float32)
                     for c in range(8)], axis=0)


if __name__ == "__main__":
    print("building...")
    nc = build_program()
    print("instructions:", nc.n_instructions())


# version 08
# speedup vs baseline: 1.2234x; 1.0280x over previous
import contextlib
import numpy as np
import concourse.bass as bass
import concourse.mybir as mybir
from concourse.bass_utils import run_bass_kernel_spmd

F32 = mybir.dt.float32
BF16 = mybir.dt.bfloat16
I32 = mybir.dt.int32
AF = mybir.ActivationFunctionType
ALU = mybir.AluOpType
AX = mybir.AxisListType

N, D, INNER, M, H, HD = 4096, 1024, 4096, 256, 16, 64
TILES, DC, NIB = 32, 8, 8

_SC = ["sum", "ssq", "mean", "m2", "nm2", "var", "lnv", "rstd", "mr", "nmr",
       "amax", "amc", "rc", "qs", "aqrec", "colg", "colu", "hamax", "hqs",
       "colt", "c1024inv", "cneg1", "cneghalf", "eps", "srt", "hamc", "hrc"]
SCOL = {n: i for i, n in enumerate(_SC)}


def build_program(dbg=(), phases=("p0", "p1a", "p1b", "p2a", "p3a", "p3b"),
                  n_tiles=None):
    n_t = n_tiles or TILES
    n_rows = n_t * 128
    nc = bass.Bass("TRN2", target_bir_lowering=False, dynamic_dma_scratch_size=512)

    x = nc.dram_tensor("x", [n_rows, D], F32, kind="ExternalInput").ap()
    wq = nc.dram_tensor("wq", [D, D], F32, kind="ExternalInput").ap()
    wk = nc.dram_tensor("wk", [D, D], F32, kind="ExternalInput").ap()
    wv = nc.dram_tensor("wv", [D, D], F32, kind="ExternalInput").ap()
    omt = nc.dram_tensor("omt", [HD, M], F32, kind="ExternalInput").ap()
    wgt = nc.dram_tensor("wgt", [D, INNER], F32, kind="ExternalInput").ap()
    wut = nc.dram_tensor("wut", [D, INNER], F32, kind="ExternalInput").ap()
    wdt = nc.dram_tensor("wdt", [INNER, D], F32, kind="ExternalInput").ap()
    y = nc.dram_tensor("y", [n_rows, D], F32, kind="ExternalOutput").ap()

    def scr(name, shape, dt):
        kind = "ExternalOutput" if name in dbg else "Internal"
        return nc.dram_tensor(name, shape, dt, kind=kind).ap()

    hq_scr = scr("hq_scr", [n_rows, INNER], BF16)
    y1_scr = scr("y1_scr", [n_rows, D], F32)
    y2_scr = scr("y2_scr", [n_rows, D], F32)
    qb_scr = scr("qb_scr", [n_rows, D], BF16)
    q2_scr = scr("q2_scr", [n_rows, D], BF16)
    gt_scr = scr("gt_scr", [D, INNER], BF16)
    ut_scr = scr("ut_scr", [D, INNER], BF16)
    dt_scr = scr("dt_scr", [INNER, D], BF16)

    sems, ctr = {}, {}

    def new_sem(nm):
        sems[nm] = nc.ctx.enter_context(nc.semaphore(nm))
        ctr[nm] = 0
        return nm

    for nm in ("sA", "sV", "sP", "sT"):
        new_sem(nm)
    WAITED = {e: {} for e in ("sync", "scalar", "vector", "gpsimd", "tensor")}
    store_sems = []

    def dma_pair(stream, store=False):
        pair = []
        for i in range(2):
            nm = new_sem(f"d{stream}{i}")
            pair.append(nm)
            if store:
                store_sems.append(nm)
        return pair

    def dma_one(stream, store=False):
        nm = new_sem(f"d{stream}")
        if store:
            store_sems.append(nm)
        return nm

    def inc(instr, sem, n=1):
        ctr[sem] += n
        instr.then_inc(sems[sem], n)
        return ctr[sem]

    def dma(eng, sem, out, in_):
        v = inc(eng.dma_start(out=out, in_=in_), sem, 16)
        return (sem, v)

    def wait(eng, ename, sem, val):
        if val and WAITED[ename].get(sem, 0) < val:
            eng.wait_ge(sems[sem], val)
            WAITED[ename][sem] = val

    def wait_t(eng, ename, tok):
        if tok is not None:
            wait(eng, ename, tok[0], tok[1])

    def wait_stores(eng, ename):
        for s in store_sems:
            wait(eng, ename, s, ctr[s])

    def dinc(eng, sem, n=1):
        return inc(eng.drain(), sem, n)

    cm = nc.ctx.enter_context
    ident_bf = cm(nc.sbuf_tensor("ident_bf", [128, 128], BF16))
    omdup_bf = cm(nc.sbuf_tensor("omdup_bf", [128, M], BF16))
    neghalf_bf = cm(nc.sbuf_tensor("neghalf_bf", [128, M], BF16))
    onesbf_col = cm(nc.sbuf_tensor("onesbf_col", [128, 1], BF16))
    onesf_col = cm(nc.sbuf_tensor("onesf_col", [128, 1], F32))
    onesrow = cm(nc.sbuf_tensor("onesrow", [1, 128], F32))
    scols = cm(nc.sbuf_tensor("scols", [128, 64], F32))
    wcols = cm(nc.sbuf_tensor("wcols", [128, 8], F32))
    sdh_col = cm(nc.sbuf_tensor("sdh_col", [128, 1], F32))
    hrecip_all = cm(nc.sbuf_tensor("hrecip_all", [128, TILES], F32))
    kvaug_bf = cm(nc.sbuf_tensor("kvaug_bf", [128, 2 * H * 65], BF16))
    sqjunk = cm(nc.sbuf_tensor("sqjunk", [128, D], BF16))
    sqjunk2 = cm(nc.sbuf_tensor("sqjunk2", [128, D], BF16))
    zcol = cm(nc.sbuf_tensor("zcol", [128, H], F32))
    pbk = [cm(nc.psum_tensor(f"pb{i}", [128, 512], F32)) for i in range(8)]

    def sc(nm, par=0):
        i = SCOL[nm] + 32 * (par % 2)
        return scols[:, i:i + 1]

    _uid = [0]

    def _nm(p):
        _uid[0] += 1
        return f"{p}{_uid[0]}"

    def finish():
        with nc.Block() as blk:
            @blk.gpsimd
            def _(g):
                wait_stores(g, "gpsimd")
        return nc

    p0 = contextlib.ExitStack()
    wtmp0 = [p0.enter_context(nc.sbuf_tensor(f"p0w{i}", [128, INNER], F32))
             for i in range(2)]
    accs = p0.enter_context(nc.sbuf_tensor("accs", [128, 28], F32))

    with nc.Block() as blk:
        @blk.vector
        def _(v):
            v.memset(scols[:], 0.0)
            v.memset(onesf_col[:], 1.0)
            v.memset(onesrow[:], 1.0)
            v.memset(neghalf_bf[:], -0.5)
            v.memset(kvaug_bf[:], 0.0)
            v.memset(zcol[:], 0.0)

        @blk.gpsimd
        def _(g):
            g.memset(onesbf_col[:], 1.0)
            g.memset(sqjunk[:], 0.0)
            g.memset(accs[:], 0.0)
            g.iota(ident_bf[:, :], [[1, 128]], base=0, channel_multiplier=-1,
                   allow_small_or_imprecise_dtypes=True)

    with nc.Block() as blk:
        @blk.vector
        def _(v):
            v.tensor_scalar(ident_bf[:, :], ident_bf[:, :], 0.0, None, ALU.is_equal)

        @blk.gpsimd
        def _(g):
            for par in range(2):
                g.memset(sc("c1024inv", par), 1.0 / 1024.0)
                g.memset(sc("cneg1", par), -1.0)
                g.memset(sc("cneghalf", par), -0.5)
                g.memset(sc("eps", par), 1e-5)

    chunks = []
    for mi, w in enumerate((wgt, wut)):
        for i in range(8):
            chunks.append((mi, i, w[i * 128:(i + 1) * 128, :]))
    for i in range(8):
        chunks.append((2, i, wdt[i * 512:(i + 1) * 512, :]
                       .rearrange("(p r) n -> p (r n)", p=128)))
    sp0 = dma_pair("p0w")
    d0, v0base = [], ctr["sV"]
    with nc.Block() as blk:
        @blk.sync
        def _(sy):
            for j, (mi, i, ap) in enumerate(chunks):
                if j >= 2:
                    wait(sy, "sync", "sV", v0base + j - 1)
                d0.append(dma(sy, sp0[j % 2], wtmp0[j % 2][:], ap))

        @blk.vector
        def _(v):
            for j, (mi, i, ap) in enumerate(chunks):
                wait_t(v, "vector", d0[j])
                inc(v.reduce_sum(accs[:, mi * 8 + i:mi * 8 + i + 1],
                                 wtmp0[j % 2][:], AX.X,
                                 apply_absolute_value=True), "sV")

    v0tok = [None]
    with nc.Block() as blk:
        @blk.vector
        def _(v):
            v.drain()
            for mi in range(3):
                v.reduce_sum(accs[:, 24 + mi:25 + mi], accs[:, mi * 8:(mi + 1) * 8],
                             AX.X)
            v0tok[0] = dinc(v, "sV")

        @blk.tensor
        def _(t):
            wait(t, "tensor", "sV", v0tok[0])
            last = None
            for mi in range(3):
                last = t.matmul(pbk[0][0:1, mi:mi + 1], accs[:, 24 + mi:25 + mi],
                                onesf_col[:, :], start=(mi == 0), stop=(mi == 2))
            inc(last, "sT")

    t0a = ctr["sT"]
    with nc.Block() as blk:
        @blk.vector
        def _(v):
            wait(v, "vector", "sT", t0a)
            v.tensor_copy(accs[0:1, 0:3], pbk[0][0:1, 0:3])
            v.drain()
            v.tensor_scalar(accs[0:1, 4:7], accs[0:1, 0:3], 1.0 / 4194304.0,
                            1e-5, ALU.mult, ALU.max)
            v.drain()
            v.reciprocal(accs[0:1, 8:11], accs[0:1, 4:7])
            v0tok[0] = dinc(v, "sV")

        @blk.tensor
        def _(t):
            wait(t, "tensor", "sV", v0tok[0])
            last = None
            for j in range(6):
                src = accs[0:1, 4 + j:5 + j] if j < 3 else \
                    accs[0:1, 8 + j - 3:9 + j - 3]
                last = t.matmul(pbk[1][:, j:j + 1], onesrow[0:1, :], src,
                                start=(j == 0), stop=(j == 5))
            inc(last, "sT")

    t0b = ctr["sT"]
    with nc.Block() as blk:
        @blk.vector
        def _(v):
            wait(v, "vector", "sT", t0b)
            v.tensor_copy(wcols[:, 0:6], pbk[1][:, 0:6])
            v.drain()
            v.tensor_scalar(sdh_col[:, :], wcols[:, 2:3], 0.5, None, ALU.mult)
            v.drain()
    p0.close()

    def emit_stats_A(s, src_f, lnf_out, v_var_tok, v_nmr_tok, par=0):
        s.activation(sqjunk[:, :], src_f, AF.Copy, accum_out=sc("sum", par))
        s.activation(sqjunk2[:, :], src_f, AF.Square, accum_out=sc("ssq", par))
        dinc(s, "sA")
        wait(s, "scalar", "sV", v_var_tok)
        s.activation(sc("srt", par), sc("var", par), AF.Sqrt, bias=sc("eps", par))
        dinc(s, "sA")
        wait(s, "scalar", "sV", v_nmr_tok)
        s.activation(lnf_out, src_f, AF.Identity, scale=sc("rstd", par),
                     bias=sc("nmr", par))
        return dinc(s, "sA")

    def emit_stats_V(v, a_acc_tok, a_srt_tok, par=0):
        wait(v, "vector", "sA", a_acc_tok)
        v.tensor_scalar(sc("mean", par), sc("sum", par), 1.0 / 1024.0, None,
                        ALU.mult)
        v.drain()
        v.tensor_tensor(sc("m2", par), sc("mean", par), sc("mean", par),
                        ALU.mult)
        v.drain()
        v.scalar_tensor_tensor(sc("var", par), sc("ssq", par), 1.0 / 1024.0,
                               sc("m2", par), ALU.mult, ALU.subtract)
        t1 = dinc(v, "sV")
        wait(v, "vector", "sA", a_srt_tok)
        v.reciprocal(sc("rstd", par), sc("srt", par))
        v.drain()
        v.tensor_scalar(sc("nmr", par), sc("mean", par), sc("rstd", par), -1.0,
                        ALU.mult, ALU.mult)
        t2 = dinc(v, "sV")
        return t1, t2

    def emit_quant_v(v, src_f, aqi_out, rec_ap, colgu, par=0):
        v.reduce_max(sc("amax", par), src_f, AX.X, apply_absolute_value=True)
        v.drain()
        v.tensor_scalar(sc("amc", par), sc("amax", par), sc("eps", par), None,
                        ALU.max)
        v.drain()
        v.reciprocal(sc("rc", par), sc("amc", par))
        v.drain()
        v.tensor_scalar(sc("qs", par), sc("rc", par), 127.0, None, ALU.mult)
        v.drain()
        v.reciprocal(rec_ap, sc("qs", par))
        v.drain()
        if colgu:
            v.tensor_scalar(colgu[0], rec_ap, wcols[:, 0:1], None, ALU.mult)
            v.tensor_scalar(colgu[1], rec_ap, wcols[:, 1:2], None, ALU.mult)
        v.tensor_scalar(aqi_out, src_f, sc("qs", par), None, ALU.mult)
        return dinc(v, "sV")

    def emit_transposes(t, src_b, banks, n):
        last = None
        for c in range(n):
            bank = banks[(c // 4) % len(banks)]
            last = t.matmul(bank[:, (c % 4) * 128:(c % 4) * 128 + 128],
                            src_b[:, c * 128:(c + 1) * 128], ident_bf[:, :],
                            start=(c % 4 == 0), stop=(c % 4 == 3))
        return last

    def emit_wquant(wsrc_list, dst_sb_list, scr_list, wsi_list, nchunk, ccols):
        est = contextlib.ExitStack()
        wtmp = [est.enter_context(nc.sbuf_tensor(_nm("wqw"), [128, ccols], F32))
                for _ in range(2)]
        i32t = [est.enter_context(nc.sbuf_tensor(_nm("wqi"), [128, ccols], I32))
                for _ in range(2)]
        jobs = []
        for (ws, db, sd, wsi) in zip(wsrc_list, dst_sb_list, scr_list, wsi_list):
            for i in range(nchunk):
                jobs.append((ws, db, sd, wsi, i))
        spw = dma_pair(_nm("wql"))
        sps = dma_pair(_nm("wqs"), store=True)
        dt_ = []
        vbase = ctr["sV"]
        with nc.Block() as blk:
            @blk.sync
            def _(sy):
                wait_stores(sy, "sync")
                for j, (ws, db, sd, wsi, i) in enumerate(jobs):
                    if j >= 2:
                        wait(sy, "sync", "sV", vbase + 2 * (j - 1))
                    dt_.append(dma(sy, spw[j % 2], wtmp[j % 2][:],
                                   ws[i * 128:(i + 1) * 128, :]))

            @blk.scalar
            def _(s):
                for j, (ws, db, sd, wsi, i) in enumerate(jobs):
                    wait(s, "scalar", "sV", vbase + 2 * (j + 1))
                    dma(s, sps[j % 2], sd[i * 128:(i + 1) * 128, :],
                        db[:, i * ccols:(i + 1) * ccols])

            @blk.vector
            def _(v):
                for j, (ws, db, sd, wsi, i) in enumerate(jobs):
                    wait_t(v, "vector", dt_[j])
                    v.tensor_scalar(i32t[j % 2][:], wtmp[j % 2][:],
                                    wcols[:, 3 + wsi:4 + wsi], None, ALU.mult)
                    dinc(v, "sV")
                    v.tensor_scalar(db[:, i * ccols:(i + 1) * ccols],
                                    i32t[j % 2][:], -1.0, 1.0, ALU.max, ALU.min)
                    dinc(v, "sV")
        est.close()

    if "p1a" not in phases:
        return finish()
    p1 = contextlib.ExitStack()
    wg_sb = p1.enter_context(nc.sbuf_tensor("wg_sb", [128, DC * INNER], BF16))
    wu_sb = p1.enter_context(nc.sbuf_tensor("wu_sb", [128, DC * INNER], BF16))
    emit_wquant([wgt, wut], [wg_sb, wu_sb], [gt_scr, ut_scr], [0, 1], 8, INNER)

    xa = [p1.enter_context(nc.sbuf_tensor(f"xa{i}", [128, D], F32))
          for i in range(2)]
    lnf = [p1.enter_context(nc.sbuf_tensor(f"lnf{i}", [128, D], F32))
           for i in range(2)]
    aqi = p1.enter_context(nc.sbuf_tensor("aqi", [128, D], I32))
    aqb = [p1.enter_context(nc.sbuf_tensor(f"aqb{i}", [128, D], BF16))
           for i in range(3)]
    xsq1 = p1.enter_context(nc.sbuf_tensor("xsq1", [128, D], F32))
    aqT = [p1.enter_context(nc.sbuf_tensor(f"aqT{i}", [128, D], BF16))
           for i in range(3)]
    qcols = p1.enter_context(nc.sbuf_tensor("qcols", [128, 6], F32))

    def qc(kind, slot):
        base = 0 if kind == "g" else 3
        return qcols[:, base + slot:base + slot + 1]
    silu_sb = [p1.enter_context(nc.sbuf_tensor(f"silu{i}", [128, 512], F32))
               for i in range(3)]
    h_sb = p1.enter_context(nc.sbuf_tensor("h_sb", [128, INNER], F32))
    hqi = p1.enter_context(nc.sbuf_tensor("hqi", [128, 2048], I32))
    hqb = [p1.enter_context(nc.sbuf_tensor(f"hqb{i}", [128, INNER], BF16))
           for i in range(2)]

    sxin = dma_pair("xin")
    sshq = dma_pair("sthq", store=True)
    tok_x = [None] * n_t
    stok_hq = [None] * n_t

    A0, V0, T0 = ctr["sA"], ctr["sV"], ctr["sT"]
    ATm, ATq, ATh, ATe = {}, {}, {}, {}
    VHM, VVAR, VAQB, VH = {}, {}, {}, {}
    TT, GU = {}, {}
    _ca, _cv, _ct = [A0], [V0], [T0]

    def _tok(d, key, c):
        c[0] += 1
        d[key] = c[0]

    _tok(ATq, 0, _ca), _tok(ATq, 1, _ca), _tok(ATe, 0, _ca), _tok(ATe, 1, _ca)
    for _t in range(n_t):
        for _ib in range(NIB):
            _tok(ATm, (_t, _ib), _ca)
            if _ib == 2 and _t + 2 < n_t:
                _tok(ATq, _t + 2, _ca)
        _tok(ATh, (_t, 1), _ca)
        _tok(ATh, (_t, 2), _ca)
        if _t + 2 < n_t:
            _tok(ATe, _t + 2, _ca)
    for _t in (0, 1):
        _tok(VVAR, _t, _cv)
        _tok(VAQB, _t, _cv)
    for _t in range(n_t):
        for _ib in range(NIB):
            _tok(VHM, (_t, _ib), _cv)
            if _t + 2 < n_t:
                if _ib == 2:
                    _tok(VVAR, _t + 2, _cv)
                if _ib == 6:
                    _tok(VAQB, _t + 2, _cv)
        _tok(VH, (_t, 1), _cv)
        _tok(VH, (_t, 2), _cv)
    for _t in (0, 1):
        _tok(TT, _t, _ct)
    for _t in range(n_t):
        for _ib in range(NIB):
            _tok(GU, (_t, _ib), _ct)
        if _t + 2 < n_t:
            _tok(TT, _t + 2, _ct)
    GBANK = [(pbk[2], pbk[3]), (pbk[4], pbk[5]), (pbk[6], pbk[7])]

    with nc.Block() as blk:
        @blk.sync
        def _(sy):
            for t in range(n_t):
                if t >= 2:
                    wait(sy, "sync", "sV", VAQB[t - 2])
                tok_x[t] = dma(sy, sxin[t % 2], xa[t % 2][:],
                               x[t * 128:(t + 1) * 128, :])

        @blk.scalar
        def _(s):
            def sqrt_a(t):
                wait(s, "scalar", "sV", VVAR[t])
                s.activation(sc("srt", t % 2), sc("var", t % 2), AF.Sqrt,
                             bias=sc("eps", t % 2))
                tok = dinc(s, "sA")
                assert tok == ATq[t]

            def evac_a(t):
                wait(s, "scalar", "sT", TT[t])
                if t >= 3:
                    wait(s, "scalar", "sT", GU[(t - 3, 7)])
                s.copy(aqT[t % 3][:, 0:512], pbk[0][:, :])
                s.copy(aqT[t % 3][:, 512:1024], pbk[1][:, :])
                tok = dinc(s, "sA")
                assert tok == ATe[t]

            sqrt_a(0), sqrt_a(1), evac_a(0), evac_a(1)
            for t in range(n_t):
                par = t % 2
                wait(s, "scalar", "sV", VAQB[t])
                for ib in range(NIB):
                    gg = 8 * t + ib
                    wait(s, "scalar", "sT", GU[(t, ib)])
                    if gg >= 3:
                        tt, ii = divmod(gg - 3, 8)
                        wait(s, "scalar", "sV", VHM[(tt, ii)])
                    tok = inc(s.activation(silu_sb[gg % 3][:],
                                           GBANK[gg % 3][0][:, :], AF.Silu,
                                           scale=qc("g", t % 3)), "sA")
                    assert tok == ATm[(t, ib)]
                    if ib == 2 and t + 2 < n_t:
                        sqrt_a(t + 2)
                wait(s, "scalar", "sV", VH[(t, 1)])
                if t >= 2:
                    wait_t(s, "scalar", stok_hq[t - 2])
                tok = inc(s.activation(hqb[par][:, 0:2048], hqi[:],
                                       AF.Identity), "sA")
                assert tok == ATh[(t, 1)]
                wait(s, "scalar", "sV", VH[(t, 2)])
                s.activation(hqb[par][:, 2048:4096], hqi[:], AF.Identity)
                tok = dinc(s, "sA")
                assert tok == ATh[(t, 2)]
                stok_hq[t] = dma(s, sshq[par],
                                 hq_scr[t * 128:(t + 1) * 128, :], hqb[par][:])
                if t + 2 < n_t:
                    evac_a(t + 2)

        @blk.vector
        def _(v):
            def fchunk(t, j):
                par = t % 2
                sx = xa[par][:]
                if j == 0:
                    wait_t(v, "vector", tok_x[t])
                    v.reduce_sum(sc("sum", par), sx, AX.X)
                    v.drain()
                    v.tensor_scalar(sc("mean", par), sc("sum", par),
                                    1.0 / 1024.0, None, ALU.mult)
                    v.drain()
                elif j == 1:
                    v.tensor_tensor(xsq1[:], sx, sx, ALU.mult)
                    v.drain()
                elif j == 2:
                    v.reduce_sum(sc("ssq", par), xsq1[:], AX.X)
                    v.drain()
                    v.tensor_tensor(sc("m2", par), sc("mean", par),
                                    sc("mean", par), ALU.mult)
                    v.drain()
                    v.scalar_tensor_tensor(sc("var", par), sc("ssq", par),
                                           1.0 / 1024.0, sc("m2", par),
                                           ALU.mult, ALU.subtract)
                    tok = dinc(v, "sV")
                    assert tok == VVAR[t]
                elif j == 3:
                    wait(v, "vector", "sA", ATq[t])
                    v.reciprocal(sc("rstd", par), sc("srt", par))
                    v.drain()
                    v.tensor_scalar(sc("nmr", par), sc("mean", par),
                                    sc("rstd", par), -1.0, ALU.mult, ALU.mult)
                    v.drain()
                    v.tensor_scalar(lnf[par][:], sx, sc("rstd", par),
                                    sc("nmr", par), ALU.mult, ALU.add)
                    v.drain()
                elif j == 4:
                    v.reduce_max(sc("amax", par), lnf[par][:], AX.X,
                                 apply_absolute_value=True)
                    v.drain()
                    v.tensor_scalar(sc("amc", par), sc("amax", par),
                                    sc("eps", par), None, ALU.max)
                    v.drain()
                    v.reciprocal(sc("rc", par), sc("amc", par))
                    v.drain()
                elif j == 5:
                    v.tensor_scalar(sc("qs", par), sc("rc", par), 127.0, None,
                                    ALU.mult)
                    v.drain()
                    v.reciprocal(sc("aqrec", par), sc("qs", par))
                    v.drain()
                    v.tensor_scalar(qc("g", t % 3), sc("aqrec", par),
                                    wcols[:, 0:1], None, ALU.mult)
                    v.tensor_scalar(qc("u", t % 3), sc("aqrec", par),
                                    wcols[:, 1:2], None, ALU.mult)
                    v.drain()
                elif j == 6:
                    v.tensor_scalar(aqi[:], lnf[par][:], sc("qs", par), None,
                                    ALU.mult)
                    v.drain()
                    if t >= 3:
                        wait(v, "vector", "sT", TT[t - 1])
                    v.tensor_copy(aqb[t % 3][:], aqi[:])
                    tok = dinc(v, "sV")
                    assert tok == VAQB[t]

            for k in (0, 1):
                for j in range(7):
                    fchunk(k, j)
            for t in range(n_t):
                par = t % 2
                for ib in range(NIB):
                    gg = 8 * t + ib
                    wait(v, "vector", "sT", GU[(t, ib)])
                    wait(v, "vector", "sA", ATm[(t, ib)])
                    tok = inc(v.scalar_tensor_tensor(
                        h_sb[:, ib * 512:(ib + 1) * 512],
                        GBANK[gg % 3][1][:, :], qc("u", t % 3),
                        silu_sb[gg % 3][:], ALU.mult, ALU.mult), "sV")
                    assert tok == VHM[(t, ib)]
                    if t + 2 < n_t and ib < 7:
                        fchunk(t + 2, ib)
                v.drain()
                v.reduce_max(sc("hamax", par), h_sb[:], AX.X,
                             apply_absolute_value=True)
                v.drain()
                v.tensor_scalar(sc("hamc", par), sc("hamax", par),
                                sc("eps", par), None, ALU.max)
                v.drain()
                v.reciprocal(sc("hrc", par), sc("hamc", par))
                v.drain()
                v.tensor_scalar(sc("hqs", par), sc("hrc", par), 127.0, None,
                                ALU.mult)
                v.drain()
                v.reciprocal(hrecip_all[:, t:t + 1], sc("hqs", par))
                v.drain()
                if t >= 1:
                    wait(v, "vector", "sA", ATh[(t - 1, 2)])
                v.tensor_scalar(hqi[:], h_sb[:, 0:2048], sc("hqs", par), None,
                                ALU.mult)
                tok = dinc(v, "sV")
                assert tok == VH[(t, 1)]
                wait(v, "vector", "sA", ATh[(t, 1)])
                v.tensor_scalar(hqi[:], h_sb[:, 2048:4096], sc("hqs", par),
                                None, ALU.mult)
                tok = dinc(v, "sV")
                assert tok == VH[(t, 2)]

        @blk.tensor
        def _(te):
            def trans(t):
                wait(te, "tensor", "sV", VAQB[t])
                if t >= 1:
                    wait(te, "tensor", "sA", ATe[t - 1])
                tok = inc(emit_transposes(te, aqb[t % 3][:, :],
                                          [pbk[0], pbk[1]], 8), "sT")
                assert tok == TT[t]

            trans(0), trans(1)
            for t in range(n_t):
                wait(te, "tensor", "sA", ATe[t])
                for ib in range(NIB):
                    gg = 8 * t + ib
                    if gg >= 3:
                        tt, ii = divmod(gg - 3, 8)
                        wait(te, "tensor", "sV", VHM[(tt, ii)])
                    gb, ub = GBANK[gg % 3]
                    for c in range(DC):
                        te.matmul(gb[:, :],
                                  aqT[t % 3][:, c * 128:(c + 1) * 128],
                                  wg_sb[:, c * INNER + ib * 512:
                                        c * INNER + ib * 512 + 512],
                                  start=(c == 0), stop=(c == DC - 1))
                    last = None
                    for c in range(DC):
                        last = te.matmul(ub[:, :],
                                         aqT[t % 3][:, c * 128:(c + 1) * 128],
                                         wu_sb[:, c * INNER + ib * 512:
                                               c * INNER + ib * 512 + 512],
                                         start=(c == 0), stop=(c == DC - 1))
                    tok = inc(last, "sT")
                    assert tok == GU[(t, ib)]
                if t + 2 < n_t:
                    trans(t + 2)

    assert ctr["sA"] == _ca[0]
    assert ctr["sV"] == _cv[0]
    assert ctr["sT"] == _ct[0]
    p1.close()

    def run_down_phase(preamble, hq_src, res_src, out_dst, final):
        pd = contextlib.ExitStack()
        wd_sb = pd.enter_context(nc.sbuf_tensor(_nm("wd_sb"),
                                                [128, 32 * 1024], BF16))
        preamble(wd_sb)
        hqa = [pd.enter_context(nc.sbuf_tensor(_nm("hqa"), [128, INNER], BF16))
               for _ in range(2)]
        resa = [pd.enter_context(nc.sbuf_tensor(_nm("resa"), [128, D], F32))
                for _ in range(2)]
        hqT = [pd.enter_context(nc.sbuf_tensor(_nm("hqT"), [128, INNER], BF16))
               for _ in range(2)]
        y3sb = [pd.enter_context(nc.sbuf_tensor(_nm("y3sb"), [128, D], F32))
                for _ in range(2)]
        sdlh = dma_pair(_nm("dnlh"))
        sdlr = dma_pair(_nm("dnlr"))
        sds = dma_pair(_nm("dns"), store=True)
        tok_hq = [None] * n_t
        tok_res = [None] * n_t
        stok = [None] * n_t

        T0d, A0d, V0d = ctr["sT"], ctr["sA"], ctr["sV"]
        _pe_off = [1, 2, 3, 4, 6, 8, 10, 12]
        _dg_off = [5, 7, 9, 11, 13, 14, 15, 16]

        def TG(t, g):
            return T0d + 16 * t + _pe_off[g]

        def DG(t, dg):
            return T0d + 16 * t + _dg_off[dg]

        def EV(t, g):
            return A0d + 8 * t + 1 + g

        def Y3(t):
            return V0d + t + 1

        with nc.Block() as blk:
            @blk.sync
            def _(sy):
                wait_stores(sy, "sync")
                for t in range(n_t):
                    if t >= 2:
                        wait(sy, "sync", "sT", TG(t - 2, 7))
                        wait(sy, "sync", "sV", Y3(t - 2))
                    tok_hq[t] = dma(sy, sdlh[t % 2], hqa[t % 2][:],
                                    hq_src[t * 128:(t + 1) * 128, :])
                    tok_res[t] = dma(sy, sdlr[t % 2], resa[t % 2][:],
                                     res_src[t * 128:(t + 1) * 128, :])

            @blk.scalar
            def _(s):
                for t in range(n_t):
                    for g in range(8):
                        wait(s, "scalar", "sT", TG(t, g))
                        if t >= 2:
                            wait(s, "scalar", "sT", DG(t - 2, g))
                        s.copy(hqT[t % 2][:, g * 512:(g + 1) * 512],
                               pbk[g % 4][:, :])
                        tok = dinc(s, "sA")
                        assert tok == EV(t, g)
                    wait(s, "scalar", "sV", Y3(t))
                    stok[t] = dma(s, sds[t % 2],
                                  out_dst[t * 128:(t + 1) * 128, :],
                                  y3sb[t % 2][:])

            @blk.vector
            def _(v):
                for t in range(n_t):
                    dnb = (pbk[4], pbk[5]) if t % 2 == 0 else (pbk[6], pbk[7])
                    wait(v, "vector", "sT", DG(t, 7))
                    wait_t(v, "vector", tok_res[t])
                    if t >= 2:
                        wait_t(v, "vector", stok[t - 2])
                    v.tensor_scalar(sc("colt"), hrecip_all[:, t:t + 1],
                                    sdh_col[:, :], None, ALU.mult)
                    v.drain()
                    v.scalar_tensor_tensor(y3sb[t % 2][:, 0:512], dnb[0][:, :],
                                           sc("colt"), resa[t % 2][:, 0:512],
                                           ALU.mult, ALU.add)
                    v.scalar_tensor_tensor(y3sb[t % 2][:, 512:1024],
                                           dnb[1][:, :], sc("colt"),
                                           resa[t % 2][:, 512:1024],
                                           ALU.mult, ALU.add)
                    tok = inc(v.drain(), "sV")
                    assert tok == Y3(t)

            @blk.tensor
            def _(te):
                for t in range(n_t):
                    dnb = (pbk[4], pbk[5]) if t % 2 == 0 else (pbk[6], pbk[7])

                    def tgrp(g, t=t):
                        if t >= 1:
                            wait(te, "tensor", "sA", EV(t - 1, 4 + (g % 4)))
                        last = None
                        for k in range(4):
                            c = g * 4 + k
                            last = te.matmul(
                                pbk[g % 4][:, k * 128:(k + 1) * 128],
                                hqa[t % 2][:, c * 128:(c + 1) * 128],
                                ident_bf[:, :], start=(k == 0), stop=(k == 3))
                        tok = inc(last, "sT")
                        assert tok == TG(t, g)

                    def dgrp(dg, t=t, dnb=dnb):
                        wait(te, "tensor", "sA", EV(t, dg))
                        if dg == 0 and t >= 2:
                            wait(te, "tensor", "sV", Y3(t - 2))
                        last = None
                        for k in range(4):
                            ci = dg * 4 + k
                            te.matmul(dnb[0][:, :],
                                      hqT[t % 2][:, ci * 128:(ci + 1) * 128],
                                      wd_sb[:, ci * 1024:ci * 1024 + 512],
                                      start=(ci == 0), stop=(ci == 31))
                            last = te.matmul(
                                dnb[1][:, :],
                                hqT[t % 2][:, ci * 128:(ci + 1) * 128],
                                wd_sb[:, ci * 1024 + 512:ci * 1024 + 1024],
                                start=(ci == 0), stop=(ci == 31))
                        tok = inc(last, "sT")
                        assert tok == DG(t, dg)

                    wait_t(te, "tensor", tok_hq[t])
                    for g in range(4):
                        tgrp(g)
                    for g in range(4):
                        dgrp(g)
                        tgrp(4 + g)
                    for dg in range(4, 8):
                        dgrp(dg)

            assert ctr["sT"] == T0d + 16 * n_t
            assert ctr["sA"] == A0d + 8 * n_t
            assert ctr["sV"] == V0d + n_t

        pd.close()

    def p1b_pre(wd_sb):
        emit_wquant([wdt], [wd_sb], [dt_scr], [2], 32, 1024)

    if "p1b" not in phases:
        return finish()
    run_down_phase(p1b_pre, hq_scr, x, y1_scr, final=False)

    if "p2a" not in phases:
        return finish()
    p2 = contextlib.ExitStack()
    wq_sb = p2.enter_context(nc.sbuf_tensor("wq_sb", [128, DC * D], BF16))
    wk_sb = p2.enter_context(nc.sbuf_tensor("wk_sb", [128, DC * D], BF16))
    wv_sb = p2.enter_context(nc.sbuf_tensor("wv_sb", [128, DC * D], BF16))

    pre = contextlib.ExitStack()
    wtmp2 = [pre.enter_context(nc.sbuf_tensor(_nm("w2"), [128, D], F32))
             for _ in range(2)]
    jobs2 = [(w, dst, i) for (w, dst) in ((wq, wq_sb), (wk, wk_sb), (wv, wv_sb))
             for i in range(8)]
    sp2w = dma_pair("p2w")
    d2, v2base = [], ctr["sV"]
    with nc.Block() as blk:
        @blk.sync
        def _(sy):
            wait_stores(sy, "sync")
            for j, (w, dst, i) in enumerate(jobs2):
                if j >= 2:
                    wait(sy, "sync", "sV", v2base + j - 1)
                d2.append(dma(sy, sp2w[j % 2], wtmp2[j % 2][:],
                              w[i * 128:(i + 1) * 128, :]))

        @blk.vector
        def _(v):
            for j, (w, dst, i) in enumerate(jobs2):
                wait_t(v, "vector", d2[j])
                v.tensor_copy(dst[:, i * D:(i + 1) * D], wtmp2[j % 2][:])
                dinc(v, "sV")

    sp2o = dma_one("p2o")
    with nc.Block() as blk:
        @blk.sync
        def _(sy):
            dma(sy, sp2o, wtmp2[0][0:64, 0:M], omt)
            dma(sy, sp2o, wtmp2[0][64:128, 0:M], omt)

        @blk.vector
        def _(v):
            wait(v, "vector", sp2o, ctr[sp2o])
            v.tensor_copy(omdup_bf[:, :], wtmp2[0][:, 0:M])
            v.drain()
    pre.close()

    ya = [p2.enter_context(nc.sbuf_tensor(f"ya{i}", [128, D], F32))
          for i in range(2)]
    lnb_sb = [p2.enter_context(nc.sbuf_tensor(f"lnb{i}", [128, D], BF16))
              for i in range(2)]
    xsq2 = p2.enter_context(nc.sbuf_tensor("xsq2", [128, D], F32))
    lnT_sb = p2.enter_context(nc.sbuf_tensor("lnT_sb", [128, D], BF16))
    k_f = p2.enter_context(nc.sbuf_tensor("k_f", [128, D], F32))
    q_f = p2.enter_context(nc.sbuf_tensor("q_f", [128, D], F32))
    k_b = p2.enter_context(nc.sbuf_tensor("k_b", [128, D], BF16))
    k2_b = p2.enter_context(nc.sbuf_tensor("k2_b", [128, D], BF16))
    v_b = p2.enter_context(nc.sbuf_tensor("v_b", [128, D], BF16))
    pk_b = p2.enter_context(nc.sbuf_tensor("pk_b", [128, H * M], BF16))
    qb_sb = [p2.enter_context(nc.sbuf_tensor(f"qb{i}", [128, D], BF16))
             for i in range(2)]
    q2_sb = [p2.enter_context(nc.sbuf_tensor(f"q2{i}", [128, D], BF16))
             for i in range(2)]

    sy1in = dma_pair("y1in")
    sstq = dma_pair("stq", store=True)
    sstq2 = dma_pair("stq2", store=True)
    tok_y1 = [None] * (n_t + 1)
    stok_q = [None] * n_t
    stok_q2 = [None] * n_t

    A0p, V0p = ctr["sA"], ctr["sV"]

    def front2_stats(v, t):
        par = t % 2
        sx = ya[par][:]
        v.reduce_sum(sc("sum", par), sx, AX.X)
        v.drain()
        v.tensor_scalar(sc("mean", par), sc("sum", par), 1.0 / 1024.0, None,
                        ALU.mult)
        v.drain()
        v.tensor_tensor(xsq2[:], sx, sx, ALU.mult)
        v.drain()
        v.reduce_sum(sc("ssq", par), xsq2[:], AX.X)
        v.drain()
        v.tensor_tensor(sc("m2", par), sc("mean", par), sc("mean", par),
                        ALU.mult)
        v.drain()
        v.scalar_tensor_tensor(sc("var", par), sc("ssq", par), 1.0 / 1024.0,
                               sc("m2", par), ALU.mult, ALU.subtract)
        return dinc(v, "sV")

    def front2_ln(v, t, a_sq_tok):
        par = t % 2
        wait(v, "vector", "sA", a_sq_tok)
        v.reciprocal(sc("rstd", par), sc("srt", par))
        v.drain()
        v.tensor_scalar(sc("nmr", par), sc("mean", par), sc("rstd", par),
                        -1.0, ALU.mult, ALU.mult)
        v.drain()
        v.tensor_scalar(lnb_sb[par][:], ya[par][:], sc("rstd", par),
                        sc("nmr", par), ALU.mult, ALU.add)
        return dinc(v, "sV")

    def sqrt2_a(s, t, v_var_tok):
        wait(s, "scalar", "sV", v_var_tok)
        s.activation(sc("srt", t % 2), sc("var", t % 2), AF.Sqrt,
                     bias=sc("eps", t % 2))
        return dinc(s, "sA")

    with nc.Block() as blk:
        @blk.sync
        def _(sy):
            wait_stores(sy, "sync")
            tok_y1[0] = dma(sy, sy1in[0], ya[0][:], y1_scr[0:128, :])

        @blk.scalar
        def _(s):
            tok = sqrt2_a(s, 0, V0p + 1)
            assert tok == A0p + 1

        @blk.vector
        def _(v):
            wait_t(v, "vector", tok_y1[0])
            tok = front2_stats(v, 0)
            assert tok == V0p + 1
            tok = front2_ln(v, 0, A0p + 1)
            assert tok == V0p + 2

    for t in range(n_t):
        tb, ab, vb = ctr["sT"], ctr["sA"], ctr["sV"]
        with nc.Block() as blk:
            @blk.sync
            def _(sy, t=t):
                if t + 1 < n_t:
                    tok_y1[t + 1] = dma(sy, sy1in[(t + 1) % 2],
                                        ya[(t + 1) % 2][:],
                                        y1_scr[(t + 1) * 128:(t + 2) * 128, :])

            @blk.scalar
            def _(s, t=t, tb=tb, vb=vb):
                wait(s, "scalar", "sT", tb + 1)
                s.copy(lnT_sb[:, 0:512], pbk[5][:, :])
                s.copy(lnT_sb[:, 512:1024], pbk[6][:, :])
                dinc(s, "sA")
                wait(s, "scalar", "sT", tb + 2)
                s.copy(k_f[:, 0:512], pbk[5][:, :])
                s.copy(k_f[:, 512:1024], pbk[6][:, :])
                dinc(s, "sA")
                s.activation(k2_b[:, :], k_f[:, :], AF.Square)
                dinc(s, "sA")
                wait(s, "scalar", "sT", tb + 4)
                s.copy(q_f[:, 0:512], pbk[5][:, :])
                s.copy(q_f[:, 512:1024], pbk[6][:, :])
                dinc(s, "sA")
                if t >= 2:
                    wait_t(s, "scalar", stok_q2[t - 2])
                s.activation(q2_sb[t % 2][:, :], q_f[:, :], AF.Square)
                dinc(s, "sA")
                for hh in range(H):
                    wait(s, "scalar", "sT", tb + 5 + hh)
                    s.activation(
                        pk_b[:, hh * M:(hh + 1) * M],
                        pbk[5 + (hh % 2)][:, 0:M], AF.Exp)
                    dinc(s, "sA")
                if t + 1 < n_t:
                    tok = sqrt2_a(s, t + 1, vb + 4)
                    assert tok == ab + 22
                wait(s, "scalar", "sV", vb + 3)
                stok_q[t] = dma(s, sstq[t % 2],
                                qb_scr[t * 128:(t + 1) * 128, :], qb_sb[t % 2][:])
                stok_q2[t] = dma(s, sstq2[t % 2],
                                 q2_scr[t * 128:(t + 1) * 128, :],
                                 q2_sb[t % 2][:])

            @blk.vector
            def _(v, t=t, tb=tb, ab=ab, vb=vb):
                wait(v, "vector", "sA", ab + 2)
                v.tensor_copy(k_b[:, :], k_f[:, :])
                dinc(v, "sV")
                wait(v, "vector", "sT", tb + 3)
                v.tensor_copy(v_b[:, 0:512], pbk[5][:, :])
                v.tensor_copy(v_b[:, 512:1024], pbk[6][:, :])
                dinc(v, "sV")
                wait(v, "vector", "sA", ab + 4)
                if t >= 2:
                    wait_t(v, "vector", stok_q[t - 2])
                v.tensor_copy(qb_sb[t % 2][:, :], q_f[:, :])
                inc(v.drain(), "sV")
                if t + 1 < n_t:
                    wait_t(v, "vector", tok_y1[t + 1])
                    tok = front2_stats(v, t + 1)
                    assert tok == vb + 4
                    tok = front2_ln(v, t + 1, ab + 22)
                    assert tok == vb + 5

            @blk.tensor
            def _(te, t=t, ab=ab, vb=vb):
                wait(te, "tensor", "sV", V0p + 2 + 5 * t)
                inc(emit_transposes(te, lnb_sb[t % 2][:, :],
                                    [pbk[5], pbk[6]], 8), "sT")
                wait(te, "tensor", "sA", ab + 1)
                last = None
                for co in range(8):
                    bank = pbk[5] if co < 4 else pbk[6]
                    sl = bank[:, (co % 4) * 128:(co % 4) * 128 + 128]
                    for cd in range(DC):
                        last = te.matmul(
                            sl, wk_sb[:, cd * D + co * 128:cd * D + co * 128 + 128],
                            lnT_sb[:, cd * 128:(cd + 1) * 128],
                            start=(co % 4 == 0 and cd == 0),
                            stop=(co % 4 == 3 and cd == DC - 1))
                inc(last, "sT")
                wait(te, "tensor", "sA", ab + 2)
                last = None
                for half in range(2):
                    bank = pbk[5] if half == 0 else pbk[6]
                    for cd in range(DC):
                        last = te.matmul(
                            bank[:, :], lnT_sb[:, cd * 128:(cd + 1) * 128],
                            wv_sb[:, cd * D + half * 512:cd * D + half * 512 + 512],
                            start=(cd == 0), stop=(cd == DC - 1))
                inc(last, "sT")
                wait(te, "tensor", "sV", vb + 2)
                last = None
                for co in range(8):
                    bank = pbk[5] if co < 4 else pbk[6]
                    sl = bank[:, (co % 4) * 128:(co % 4) * 128 + 128]
                    for cd in range(DC):
                        last = te.matmul(
                            sl, wq_sb[:, cd * D + co * 128:cd * D + co * 128 + 128],
                            lnT_sb[:, cd * 128:(cd + 1) * 128],
                            start=(co % 4 == 0 and cd == 0),
                            stop=(co % 4 == 3 and cd == DC - 1))
                inc(last, "sT")
                wait(te, "tensor", "sV", vb + 1)
                wait(te, "tensor", "sA", ab + 4)
                for hh in range(H):
                    if hh >= 2:
                        wait(te, "tensor", "sA", ab + 6 + (hh - 2))
                    par = hh % 2
                    sl = pbk[5 + par][:, 0:M]
                    ksl = k_b[par * 64:par * 64 + 64,
                              (hh // 2) * 128:(hh // 2) * 128 + 128]
                    k2sl = k2_b[par * 64:par * 64 + 64,
                                (hh // 2) * 128:(hh // 2) * 128 + 128]
                    te.matmul(sl, ksl, omdup_bf[par * 64:par * 64 + 64, :],
                              start=True, stop=False)
                    inc(te.matmul(sl, k2sl, neghalf_bf[par * 64:par * 64 + 64, :],
                                  start=False, stop=True), "sT")
                last = None
                for hh in range(H):
                    wait(te, "tensor", "sA", ab + 6 + hh)
                    hl = hh - 8 if hh >= 8 else hh
                    for mc in range(2):
                        kvb = pbk[mc * 2 + (1 if hh >= 8 else 0)]
                        lhs = pk_b[:, hh * M + mc * 128:hh * M + mc * 128 + 128]
                        te.matmul(kvb[:, hl * 64:hl * 64 + 64], lhs,
                                  v_b[:, hh * 64:hh * 64 + 64],
                                  start=(t == 0 and hl == 0),
                                  stop=(t == n_t - 1 and hl == 7))
                        last = te.matmul(pbk[4][:, mc * 16 + hh:mc * 16 + hh + 1],
                                         lhs, onesbf_col[:, :],
                                         start=(t == 0 and hh == 0 and mc == 0),
                                         stop=(t == n_t - 1 and hh == H - 1
                                               and mc == 1))
                inc(last, "sT")

        assert ctr["sT"] == tb + 21, (ctr["sT"], tb)
        assert ctr["sA"] == ab + 5 + H + (1 if t + 1 < n_t else 0)
        assert ctr["sV"] == vb + 3 + (2 if t + 1 < n_t else 0)

    with nc.Block() as blk:
        @blk.vector
        def _(v):
            for mc in range(2):
                for hh in range(H):
                    kvb = pbk[mc * 2 + (1 if hh >= 8 else 0)]
                    hl = hh - 8 if hh >= 8 else hh
                    base = mc * (H * 65) + hh * 65
                    v.tensor_copy(kvaug_bf[:, base:base + 64],
                                  kvb[:, hl * 64:hl * 64 + 64])
                    v.tensor_copy(kvaug_bf[:, base + 64:base + 65],
                                  pbk[4][:, mc * 16 + hh:mc * 16 + hh + 1])
            v.drain()
    p2.close()

    if "p3a" not in phases:
        return finish()
    p3 = contextlib.ExitStack()
    wg3_sb = p3.enter_context(nc.sbuf_tensor("wg3_sb", [128, DC * INNER], BF16))
    wu3_sb = p3.enter_context(nc.sbuf_tensor("wu3_sb", [128, DC * INNER], BF16))
    stern = dma_one("tern")
    with nc.Block() as blk:
        @blk.sync
        def _(sy):
            wait_stores(sy, "sync")
            for i in range(8):
                dma(sy, stern, wg3_sb[:, i * INNER:(i + 1) * INNER],
                    gt_scr[i * 128:(i + 1) * 128, :])
            for i in range(8):
                dma(sy, stern, wu3_sb[:, i * INNER:(i + 1) * INNER],
                    ut_scr[i * 128:(i + 1) * 128, :])

    y1a = [p3.enter_context(nc.sbuf_tensor(f"y1a{i}", [128, D], F32))
           for i in range(2)]
    qba = [p3.enter_context(nc.sbuf_tensor(f"qba{i}", [128, D], BF16))
           for i in range(2)]
    q2a = [p3.enter_context(nc.sbuf_tensor(f"q2a{i}", [128, D], BF16))
           for i in range(2)]
    pq_b = p3.enter_context(nc.sbuf_tensor("pq_b", [128, 32 * 128], BF16))
    y2_sb = [p3.enter_context(nc.sbuf_tensor(f"y2sb{i}", [128, D], F32))
             for i in range(2)]
    lnf3 = p3.enter_context(nc.sbuf_tensor("lnf3", [128, D], F32))
    aqi3 = p3.enter_context(nc.sbuf_tensor("aqi3", [128, D], I32))
    aqb3 = p3.enter_context(nc.sbuf_tensor("aqb3", [128, D], BF16))
    aqT3 = p3.enter_context(nc.sbuf_tensor("aqT3", [128, D], BF16))
    silu3 = [p3.enter_context(nc.sbuf_tensor(f"silu3{i}", [128, 512], F32))
             for i in range(2)]
    h3_sb = p3.enter_context(nc.sbuf_tensor("h3_sb", [128, INNER], F32))
    hqi3 = p3.enter_context(nc.sbuf_tensor("hqi3", [128, 2048], I32))
    hqb3 = p3.enter_context(nc.sbuf_tensor("hqb3", [128, INNER], BF16))
    xsq3 = p3.enter_context(nc.sbuf_tensor("xsq3", [128, D], F32))

    sp3in = dma_pair("p3in")
    ssty2 = dma_pair("sty2", store=True)
    ssth2 = dma_pair("sth2", store=True)
    tok3 = [None] * (n_t + 1)
    stok_h2 = [None] * n_t
    stok_y2 = [None] * n_t

    def p3_loads(sy, tt):
        dma(sy, sp3in[tt % 2], y1a[tt % 2][:],
            y1_scr[tt * 128:(tt + 1) * 128, :])
        dma(sy, sp3in[tt % 2], qba[tt % 2][:],
            qb_scr[tt * 128:(tt + 1) * 128, :])
        tok3[tt] = dma(sy, sp3in[tt % 2], q2a[tt % 2][:],
                       q2_scr[tt * 128:(tt + 1) * 128, :])

    with nc.Block() as blk:
        @blk.sync
        def _(sy):
            p3_loads(sy, 0)

    ATT_BANK = [4] * 7 + [5] * 7 + [6] * 2
    ATT_SLOT = list(range(7)) + list(range(7)) + list(range(2))
    GBANK3 = [(pbk[2], pbk[3]), (pbk[7], pbk[6])]

    for t in range(n_t + 1):
        att = t < n_t
        ffn = t >= 1
        u = t - 1
        na, nf = (1 if att else 0), (1 if ffn else 0)
        tb, ab, vb = ctr["sT"], ctr["sA"], ctr["sV"]
        T_ATT = tb + 5
        T_TR = tb + 5 * na + 1

        def T_GU(ib, tb=tb, na=na):
            return tb + 5 * na + 2 + ib

        A_SRT = ab + 4 * na + 1
        A_EVAC = ab + 4 * na + 2

        def A_SIL(ib, ab=ab, na=na):
            return ab + 4 * na + 3 + ib

        A_C1 = ab + 4 * na + 11
        A_C2 = ab + 4 * na + 12
        V_VAR = vb + 1
        V_AQB = vb + 2
        V_Y2 = vb + 2 * nf + 1

        def V_HM(ib, vb=vb, nf=nf, na=na):
            return vb + 2 * nf + na + 1 + ib

        V_H1 = vb + 2 * nf + na + 9
        V_H2 = vb + 2 * nf + na + 10

        with nc.Block() as blk:
            @blk.sync
            def _(sy, t=t):
                if t + 1 < n_t:
                    p3_loads(sy, t + 1)

            @blk.scalar
            def _(s, t=t, u=u, att=att, ffn=ffn, tb=tb, ab=ab):
                if att:
                    for g in range(4):
                        wait(s, "scalar", "sT", tb + 1 + g)
                        bA = pbk[0] if g % 2 == 0 else pbk[2]
                        bB = pbk[1] if g % 2 == 0 else pbk[3]
                        s.activation(pq_b[:, g * 1024:g * 1024 + 512], bA[:, :],
                                     AF.Exp)
                        s.activation(pq_b[:, g * 1024 + 512:(g + 1) * 1024],
                                     bB[:, :], AF.Exp)
                        dinc(s, "sA")
                if ffn:
                    wait(s, "scalar", "sV", V_VAR)
                    s.activation(sc("srt"), sc("var"), AF.Sqrt, bias=sc("eps"))
                    tok = dinc(s, "sA")
                    assert tok == A_SRT
                    wait(s, "scalar", "sT", T_TR)
                    s.copy(aqT3[:, 0:512], pbk[0][:, :])
                    s.copy(aqT3[:, 512:1024], pbk[1][:, :])
                    tok = dinc(s, "sA")
                    assert tok == A_EVAC
                    wait(s, "scalar", "sV", V_AQB)
                    for ib in range(NIB):
                        wait(s, "scalar", "sT", T_GU(ib))
                        if ib >= 2:
                            wait(s, "scalar", "sV", V_HM(ib - 2))
                        tok = inc(s.activation(silu3[ib % 2][:],
                                               GBANK3[ib % 2][0][:, :], AF.Silu,
                                               scale=sc("colg")), "sA")
                        assert tok == A_SIL(ib)
                    wait(s, "scalar", "sV", V_H1)
                    if u >= 1:
                        wait_t(s, "scalar", stok_h2[u - 1])
                    tok = inc(s.activation(hqb3[:, 0:2048], hqi3[:],
                                           AF.Identity), "sA")
                    assert tok == A_C1
                    wait(s, "scalar", "sV", V_H2)
                    s.activation(hqb3[:, 2048:4096], hqi3[:], AF.Identity)
                    tok = dinc(s, "sA")
                    assert tok == A_C2
                    stok_h2[u] = dma(s, ssth2[u % 2],
                                     hq_scr[u * 128:(u + 1) * 128, :], hqb3[:])
                if att:
                    wait(s, "scalar", "sV", V_Y2)
                    stok_y2[t] = dma(s, ssty2[t % 2],
                                     y2_scr[t * 128:(t + 1) * 128, :],
                                     y2_sb[t % 2][:])

            @blk.vector
            def _(v, t=t, u=u, att=att, ffn=ffn):
                if ffn:
                    src = y2_sb[u % 2][:]
                    v.reduce_sum(sc("sum"), src, AX.X)
                    v.drain()
                    v.tensor_scalar(sc("mean"), sc("sum"), 1.0 / 1024.0, None,
                                    ALU.mult)
                    v.drain()
                    v.tensor_tensor(xsq3[:], src, src, ALU.mult)
                    v.drain()
                    v.reduce_sum(sc("ssq"), xsq3[:], AX.X)
                    v.drain()
                    v.tensor_tensor(sc("m2"), sc("mean"), sc("mean"), ALU.mult)
                    v.drain()
                    v.scalar_tensor_tensor(sc("var"), sc("ssq"), 1.0 / 1024.0,
                                           sc("m2"), ALU.mult, ALU.subtract)
                    tok = dinc(v, "sV")
                    assert tok == V_VAR
                    wait(v, "vector", "sA", A_SRT)
                    v.reciprocal(sc("rstd"), sc("srt"))
                    v.drain()
                    v.tensor_scalar(sc("nmr"), sc("mean"), sc("rstd"), -1.0,
                                    ALU.mult, ALU.mult)
                    v.drain()
                    v.tensor_scalar(lnf3[:], src, sc("rstd"), sc("nmr"),
                                    ALU.mult, ALU.add)
                    v.drain()
                    v.reduce_max(sc("amax"), lnf3[:], AX.X,
                                 apply_absolute_value=True)
                    v.drain()
                    v.tensor_scalar(sc("amc"), sc("amax"), sc("eps"), None,
                                    ALU.max)
                    v.drain()
                    v.reciprocal(sc("rc"), sc("amc"))
                    v.drain()
                    v.tensor_scalar(sc("qs"), sc("rc"), 127.0, None, ALU.mult)
                    v.drain()
                    v.reciprocal(sc("aqrec"), sc("qs"))
                    v.drain()
                    v.tensor_scalar(sc("colg"), sc("aqrec"), wcols[:, 0:1],
                                    None, ALU.mult)
                    v.tensor_scalar(sc("colu"), sc("aqrec"), wcols[:, 1:2],
                                    None, ALU.mult)
                    v.drain()
                    v.tensor_scalar(aqi3[:], lnf3[:], sc("qs"), None, ALU.mult)
                    v.drain()
                    v.tensor_copy(aqb3[:], aqi3[:])
                    tok = dinc(v, "sV")
                    assert tok == V_AQB
                if att:
                    wait(v, "vector", "sT", T_ATT)
                    wait_t(v, "vector", tok3[t])
                    if t >= 2:
                        wait_t(v, "vector", stok_y2[t - 2])
                    for hh in range(H):
                        v.tensor_copy(zcol[:, hh:hh + 1],
                                      pbk[ATT_BANK[hh]][:, ATT_SLOT[hh] * 65 + 64:
                                                        ATT_SLOT[hh] * 65 + 65])
                    v.drain()
                    v.tensor_scalar(zcol[:, :], zcol[:, :], 2.56e-4, None,
                                    ALU.add)
                    v.drain()
                    v.reciprocal(zcol[:, :], zcol[:, :])
                    v.drain()
                    for hh in range(H):
                        v.scalar_tensor_tensor(
                            y2_sb[t % 2][:, hh * 64:hh * 64 + 64],
                            pbk[ATT_BANK[hh]][:, ATT_SLOT[hh] * 65:
                                              ATT_SLOT[hh] * 65 + 64],
                            zcol[:, hh:hh + 1],
                            y1a[t % 2][:, hh * 64:hh * 64 + 64],
                            ALU.mult, ALU.add)
                    tok = inc(v.drain(), "sV")
                    assert tok == V_Y2
                if ffn:
                    for ib in range(NIB):
                        wait(v, "vector", "sT", T_GU(ib))
                        wait(v, "vector", "sA", A_SIL(ib))
                        tok = inc(v.scalar_tensor_tensor(
                            h3_sb[:, ib * 512:(ib + 1) * 512],
                            GBANK3[ib % 2][1][:, :], sc("colu"),
                            silu3[ib % 2][:], ALU.mult, ALU.mult), "sV")
                        assert tok == V_HM(ib)
                    v.drain()
                    v.reduce_max(sc("hamax"), h3_sb[:], AX.X,
                                 apply_absolute_value=True)
                    v.drain()
                    v.tensor_scalar(sc("amc"), sc("hamax"), sc("eps"), None,
                                    ALU.max)
                    v.drain()
                    v.reciprocal(sc("rc"), sc("amc"))
                    v.drain()
                    v.tensor_scalar(sc("hqs"), sc("rc"), 127.0, None, ALU.mult)
                    v.drain()
                    v.reciprocal(hrecip_all[:, u:u + 1], sc("hqs"))
                    v.drain()
                    v.tensor_scalar(hqi3[:], h3_sb[:, 0:2048], sc("hqs"), None,
                                    ALU.mult)
                    tok = dinc(v, "sV")
                    assert tok == V_H1
                    wait(v, "vector", "sA", A_C1)
                    v.tensor_scalar(hqi3[:], h3_sb[:, 2048:4096], sc("hqs"),
                                    None, ALU.mult)
                    tok = dinc(v, "sV")
                    assert tok == V_H2

            @blk.tensor
            def _(te, t=t, u=u, att=att, ffn=ffn, tb=tb, ab=ab):
                if att:
                    wait_t(te, "tensor", tok3[t])
                    if t == 0:
                        wait(te, "tensor", stern, ctr[stern])
                    for g in range(4):
                        mc, par = g // 2, g % 2
                        bA = pbk[0] if g % 2 == 0 else pbk[2]
                        bB = pbk[1] if g % 2 == 0 else pbk[3]
                        if g >= 2:
                            wait(te, "tensor", "sA", ab + 1 + (g - 2))
                        oms = omdup_bf[par * 64:par * 64 + 64,
                                       mc * 128:mc * 128 + 128]
                        ngs = neghalf_bf[par * 64:par * 64 + 64, 0:128]
                        last = None
                        for c in range(8):
                            bank = bA if c < 4 else bB
                            sl = bank[:, (c % 4) * 128:(c % 4) * 128 + 128]
                            qsl = qba[t % 2][par * 64:par * 64 + 64,
                                             c * 128:(c + 1) * 128]
                            q2sl = q2a[t % 2][par * 64:par * 64 + 64,
                                              c * 128:(c + 1) * 128]
                            te.matmul(sl, oms, qsl, start=(c % 4 == 0),
                                      stop=False)
                            last = te.matmul(sl, ngs, q2sl, start=False,
                                             stop=(c % 4 == 3))
                        inc(last, "sT")
                    wait(te, "tensor", "sA", ab + 4)
                    last = None
                    bank_started = set()
                    for hh in range(H):
                        bk, slt = ATT_BANK[hh], ATT_SLOT[hh]
                        for mc in range(2):
                            g = mc * 2 + (hh % 2)
                            j = g * 8 + hh // 2
                            first = bk not in bank_started
                            bank_started.add(bk)
                            last = te.matmul(
                                pbk[bk][:, slt * 65:slt * 65 + 65],
                                pq_b[:, j * 128:(j + 1) * 128],
                                kvaug_bf[:, mc * (H * 65) + hh * 65:
                                         mc * (H * 65) + (hh + 1) * 65],
                                start=first,
                                stop=(mc == 1 and hh in (6, 13, 15)))
                    tok = inc(last, "sT")
                    assert tok == T_ATT
                if ffn:
                    wait(te, "tensor", "sV", V_AQB)
                    tok = inc(emit_transposes(te, aqb3[:, :], [pbk[0], pbk[1]],
                                              8), "sT")
                    assert tok == T_TR
                    wait(te, "tensor", "sA", A_EVAC)
                    for ib in range(NIB):
                        if ib == 1 and att:
                            wait(te, "tensor", "sV", V_Y2)
                        if ib >= 2:
                            wait(te, "tensor", "sV", V_HM(ib - 2))
                        gb, ub = GBANK3[ib % 2]
                        for c in range(DC):
                            te.matmul(gb[:, :], aqT3[:, c * 128:(c + 1) * 128],
                                      wg3_sb[:, c * INNER + ib * 512:
                                             c * INNER + ib * 512 + 512],
                                      start=(c == 0), stop=(c == DC - 1))
                        last = None
                        for c in range(DC):
                            last = te.matmul(ub[:, :],
                                             aqT3[:, c * 128:(c + 1) * 128],
                                             wu3_sb[:, c * INNER + ib * 512:
                                                    c * INNER + ib * 512 + 512],
                                             start=(c == 0), stop=(c == DC - 1))
                        tok = inc(last, "sT")
                        assert tok == T_GU(ib)

        assert ctr["sT"] == tb + 5 * na + 9 * nf, (ctr["sT"], tb)
        assert ctr["sA"] == ab + 4 * na + 12 * nf
        assert ctr["sV"] == vb + na + 12 * nf
    p3.close()

    if "p3b" not in phases:
        return finish()
    swdld = dma_one("wdld")

    def p3b_pre(wd_sb):
        with nc.Block() as blk:
            @blk.sync
            def _(sy):
                wait_stores(sy, "sync")
                for ci in range(32):
                    dma(sy, swdld, wd_sb[:, ci * 1024:(ci + 1) * 1024],
                        dt_scr[ci * 128:(ci + 1) * 128, :])
                wait(sy, "sync", swdld, ctr[swdld])

    run_down_phase(p3b_pre, hq_scr, y2_scr, y, final=True)

    return finish()


_NC_CACHE = {}


def _get_program():
    if "nc" not in _NC_CACHE:
        _NC_CACHE["nc"] = build_program()
    return _NC_CACHE["nc"]


def kernel(L0_attn_x, L0_attn_Wq, L0_attn_Wk, L0_attn_Wv, L0_attn_omega,
           L0_ffn_w_gate, L0_ffn_w_up, L0_ffn_w_down):
    xs = np.ascontiguousarray(np.asarray(L0_attn_x, dtype=np.float32))
    wq_ = np.ascontiguousarray(np.asarray(L0_attn_Wq, dtype=np.float32))
    wk_ = np.ascontiguousarray(np.asarray(L0_attn_Wk, dtype=np.float32))
    wv_ = np.ascontiguousarray(np.asarray(L0_attn_Wv, dtype=np.float32))
    omt_ = np.ascontiguousarray(np.asarray(L0_attn_omega, dtype=np.float32).T)
    wgt_ = np.ascontiguousarray(np.asarray(L0_ffn_w_gate, dtype=np.float32).T)
    wut_ = np.ascontiguousarray(np.asarray(L0_ffn_w_up, dtype=np.float32).T)
    wdt_ = np.ascontiguousarray(np.asarray(L0_ffn_w_down, dtype=np.float32).T)

    nc = _get_program()
    in_maps = [{"x": np.ascontiguousarray(xs[c]), "wq": wq_, "wk": wk_,
                "wv": wv_, "omt": omt_, "wgt": wgt_, "wut": wut_, "wdt": wdt_}
               for c in range(8)]
    res = run_bass_kernel_spmd(nc, in_maps, core_ids=list(range(8)))
    return np.stack([np.asarray(res.results[c]["y"], dtype=np.float32)
                     for c in range(8)], axis=0)


if __name__ == "__main__":
    print("building...")
    nc = build_program()
    print("instructions:", nc.n_instructions())


# version 09 (final)
# speedup vs baseline: 1.2401x; 1.0137x over previous
import contextlib
import numpy as np
import concourse.bass as bass
import concourse.mybir as mybir
from concourse.bass_utils import run_bass_kernel_spmd

F32 = mybir.dt.float32
BF16 = mybir.dt.bfloat16
I32 = mybir.dt.int32
AF = mybir.ActivationFunctionType
ALU = mybir.AluOpType
AX = mybir.AxisListType

N, D, INNER, M, H, HD = 4096, 1024, 4096, 256, 16, 64
TILES, DC, NIB = 32, 8, 8

_SC = ["sum", "ssq", "mean", "m2", "nm2", "var", "lnv", "rstd", "mr", "nmr",
       "amax", "amc", "rc", "qs", "aqrec", "colg", "colu", "hamax", "hqs",
       "colt", "c1024inv", "cneg1", "cneghalf", "eps", "srt", "hamc", "hrc"]
SCOL = {n: i for i, n in enumerate(_SC)}


def build_program(dbg=(), phases=("p0", "p1a", "p1b", "p2a", "p3a", "p3b"),
                  n_tiles=None):
    n_t = n_tiles or TILES
    n_rows = n_t * 128
    nc = bass.Bass("TRN2", target_bir_lowering=False, dynamic_dma_scratch_size=512)

    x = nc.dram_tensor("x", [n_rows, D], F32, kind="ExternalInput").ap()
    wq = nc.dram_tensor("wq", [D, D], F32, kind="ExternalInput").ap()
    wk = nc.dram_tensor("wk", [D, D], F32, kind="ExternalInput").ap()
    wv = nc.dram_tensor("wv", [D, D], F32, kind="ExternalInput").ap()
    omt = nc.dram_tensor("omt", [HD, M], F32, kind="ExternalInput").ap()
    wgt = nc.dram_tensor("wgt", [D, INNER], F32, kind="ExternalInput").ap()
    wut = nc.dram_tensor("wut", [D, INNER], F32, kind="ExternalInput").ap()
    wdt = nc.dram_tensor("wdt", [INNER, D], F32, kind="ExternalInput").ap()
    y = nc.dram_tensor("y", [n_rows, D], F32, kind="ExternalOutput").ap()

    def scr(name, shape, dt):
        kind = "ExternalOutput" if name in dbg else "Internal"
        return nc.dram_tensor(name, shape, dt, kind=kind).ap()

    hq_scr = scr("hq_scr", [n_rows, INNER], BF16)
    y1_scr = scr("y1_scr", [n_rows, D], F32)
    y2_scr = scr("y2_scr", [n_rows, D], F32)
    qb_scr = scr("qb_scr", [n_rows, D], BF16)
    q2_scr = scr("q2_scr", [n_rows, D], BF16)
    gt_scr = scr("gt_scr", [D, INNER], BF16)
    ut_scr = scr("ut_scr", [D, INNER], BF16)
    dt_scr = scr("dt_scr", [INNER, D], BF16)

    sems, ctr = {}, {}

    def new_sem(nm):
        sems[nm] = nc.ctx.enter_context(nc.semaphore(nm))
        ctr[nm] = 0
        return nm

    for nm in ("sA", "sV", "sP", "sT"):
        new_sem(nm)
    WAITED = {e: {} for e in ("sync", "scalar", "vector", "gpsimd", "tensor")}
    store_sems = []

    def dma_pair(stream, store=False):
        pair = []
        for i in range(2):
            nm = new_sem(f"d{stream}{i}")
            pair.append(nm)
            if store:
                store_sems.append(nm)
        return pair

    def dma_one(stream, store=False):
        nm = new_sem(f"d{stream}")
        if store:
            store_sems.append(nm)
        return nm

    def inc(instr, sem, n=1):
        ctr[sem] += n
        instr.then_inc(sems[sem], n)
        return ctr[sem]

    def dma(eng, sem, out, in_):
        v = inc(eng.dma_start(out=out, in_=in_), sem, 16)
        return (sem, v)

    def wait(eng, ename, sem, val):
        if val and WAITED[ename].get(sem, 0) < val:
            eng.wait_ge(sems[sem], val)
            WAITED[ename][sem] = val

    def wait_t(eng, ename, tok):
        if tok is not None:
            wait(eng, ename, tok[0], tok[1])

    def wait_stores(eng, ename):
        for s in store_sems:
            wait(eng, ename, s, ctr[s])

    def dinc(eng, sem, n=1):
        return inc(eng.drain(), sem, n)

    cm = nc.ctx.enter_context
    ident_bf = cm(nc.sbuf_tensor("ident_bf", [128, 128], BF16))
    omdup_bf = cm(nc.sbuf_tensor("omdup_bf", [128, M], BF16))
    neghalf_bf = cm(nc.sbuf_tensor("neghalf_bf", [128, M], BF16))
    onesbf_col = cm(nc.sbuf_tensor("onesbf_col", [128, 1], BF16))
    onesf_col = cm(nc.sbuf_tensor("onesf_col", [128, 1], F32))
    onesrow = cm(nc.sbuf_tensor("onesrow", [1, 128], F32))
    scols = cm(nc.sbuf_tensor("scols", [128, 64], F32))
    wcols = cm(nc.sbuf_tensor("wcols", [128, 8], F32))
    sdh_col = cm(nc.sbuf_tensor("sdh_col", [128, 1], F32))
    hrecip_all = cm(nc.sbuf_tensor("hrecip_all", [128, TILES], F32))
    kvaug_bf = cm(nc.sbuf_tensor("kvaug_bf", [128, 2 * H * 65], BF16))
    sqjunk = cm(nc.sbuf_tensor("sqjunk", [128, D], BF16))
    sqjunk2 = cm(nc.sbuf_tensor("sqjunk2", [128, D], BF16))
    zcol = cm(nc.sbuf_tensor("zcol", [128, H], F32))
    pbk = [cm(nc.psum_tensor(f"pb{i}", [128, 512], F32)) for i in range(8)]

    def sc(nm, par=0):
        i = SCOL[nm] + 32 * (par % 2)
        return scols[:, i:i + 1]

    _uid = [0]

    def _nm(p):
        _uid[0] += 1
        return f"{p}{_uid[0]}"

    def finish():
        with nc.Block() as blk:
            @blk.gpsimd
            def _(g):
                wait_stores(g, "gpsimd")
        return nc

    p0 = contextlib.ExitStack()
    wtmp0 = [p0.enter_context(nc.sbuf_tensor(f"p0w{i}", [128, INNER], F32))
             for i in range(2)]
    accs = p0.enter_context(nc.sbuf_tensor("accs", [128, 28], F32))

    with nc.Block() as blk:
        @blk.vector
        def _(v):
            v.memset(scols[:], 0.0)
            v.memset(onesf_col[:], 1.0)
            v.memset(onesrow[:], 1.0)
            v.memset(neghalf_bf[:], -0.5)
            v.memset(kvaug_bf[:], 0.0)
            v.memset(zcol[:], 0.0)

        @blk.gpsimd
        def _(g):
            g.memset(onesbf_col[:], 1.0)
            g.memset(sqjunk[:], 0.0)
            g.memset(accs[:], 0.0)
            g.iota(ident_bf[:, :], [[1, 128]], base=0, channel_multiplier=-1,
                   allow_small_or_imprecise_dtypes=True)

    with nc.Block() as blk:
        @blk.vector
        def _(v):
            v.tensor_scalar(ident_bf[:, :], ident_bf[:, :], 0.0, None, ALU.is_equal)

        @blk.gpsimd
        def _(g):
            for par in range(2):
                g.memset(sc("c1024inv", par), 1.0 / 1024.0)
                g.memset(sc("cneg1", par), -1.0)
                g.memset(sc("cneghalf", par), -0.5)
                g.memset(sc("eps", par), 1e-5)

    chunks = []
    for mi, w in enumerate((wgt, wut)):
        for i in range(8):
            chunks.append((mi, i, w[i * 128:(i + 1) * 128, :]))
    for i in range(8):
        chunks.append((2, i, wdt[i * 512:(i + 1) * 512, :]
                       .rearrange("(p r) n -> p (r n)", p=128)))
    sp0 = dma_pair("p0w")
    d0, v0base = [], ctr["sV"]
    with nc.Block() as blk:
        @blk.sync
        def _(sy):
            for j, (mi, i, ap) in enumerate(chunks):
                if j >= 2:
                    wait(sy, "sync", "sV", v0base + j - 1)
                d0.append(dma(sy, sp0[j % 2], wtmp0[j % 2][:], ap))

        @blk.vector
        def _(v):
            for j, (mi, i, ap) in enumerate(chunks):
                wait_t(v, "vector", d0[j])
                inc(v.reduce_sum(accs[:, mi * 8 + i:mi * 8 + i + 1],
                                 wtmp0[j % 2][:], AX.X,
                                 apply_absolute_value=True), "sV")

    v0tok = [None]
    with nc.Block() as blk:
        @blk.vector
        def _(v):
            v.drain()
            for mi in range(3):
                v.reduce_sum(accs[:, 24 + mi:25 + mi], accs[:, mi * 8:(mi + 1) * 8],
                             AX.X)
            v0tok[0] = dinc(v, "sV")

        @blk.tensor
        def _(t):
            wait(t, "tensor", "sV", v0tok[0])
            last = None
            for mi in range(3):
                last = t.matmul(pbk[0][0:1, mi:mi + 1], accs[:, 24 + mi:25 + mi],
                                onesf_col[:, :], start=(mi == 0), stop=(mi == 2))
            inc(last, "sT")

    t0a = ctr["sT"]
    with nc.Block() as blk:
        @blk.vector
        def _(v):
            wait(v, "vector", "sT", t0a)
            v.tensor_copy(accs[0:1, 0:3], pbk[0][0:1, 0:3])
            v.drain()
            v.tensor_scalar(accs[0:1, 4:7], accs[0:1, 0:3], 1.0 / 4194304.0,
                            1e-5, ALU.mult, ALU.max)
            v.drain()
            v.reciprocal(accs[0:1, 8:11], accs[0:1, 4:7])
            v0tok[0] = dinc(v, "sV")

        @blk.tensor
        def _(t):
            wait(t, "tensor", "sV", v0tok[0])
            last = None
            for j in range(6):
                src = accs[0:1, 4 + j:5 + j] if j < 3 else \
                    accs[0:1, 8 + j - 3:9 + j - 3]
                last = t.matmul(pbk[1][:, j:j + 1], onesrow[0:1, :], src,
                                start=(j == 0), stop=(j == 5))
            inc(last, "sT")

    t0b = ctr["sT"]
    with nc.Block() as blk:
        @blk.vector
        def _(v):
            wait(v, "vector", "sT", t0b)
            v.tensor_copy(wcols[:, 0:6], pbk[1][:, 0:6])
            v.drain()
            v.tensor_scalar(sdh_col[:, :], wcols[:, 2:3], 0.5, None, ALU.mult)
            v.drain()
    p0.close()

    def emit_stats_A(s, src_f, lnf_out, v_var_tok, v_nmr_tok, par=0):
        s.activation(sqjunk[:, :], src_f, AF.Copy, accum_out=sc("sum", par))
        s.activation(sqjunk2[:, :], src_f, AF.Square, accum_out=sc("ssq", par))
        dinc(s, "sA")
        wait(s, "scalar", "sV", v_var_tok)
        s.activation(sc("srt", par), sc("var", par), AF.Sqrt, bias=sc("eps", par))
        dinc(s, "sA")
        wait(s, "scalar", "sV", v_nmr_tok)
        s.activation(lnf_out, src_f, AF.Identity, scale=sc("rstd", par),
                     bias=sc("nmr", par))
        return dinc(s, "sA")

    def emit_stats_V(v, a_acc_tok, a_srt_tok, par=0):
        wait(v, "vector", "sA", a_acc_tok)
        v.tensor_scalar(sc("mean", par), sc("sum", par), 1.0 / 1024.0, None,
                        ALU.mult)
        v.drain()
        v.tensor_tensor(sc("m2", par), sc("mean", par), sc("mean", par),
                        ALU.mult)
        v.drain()
        v.scalar_tensor_tensor(sc("var", par), sc("ssq", par), 1.0 / 1024.0,
                               sc("m2", par), ALU.mult, ALU.subtract)
        t1 = dinc(v, "sV")
        wait(v, "vector", "sA", a_srt_tok)
        v.reciprocal(sc("rstd", par), sc("srt", par))
        v.drain()
        v.tensor_scalar(sc("nmr", par), sc("mean", par), sc("rstd", par), -1.0,
                        ALU.mult, ALU.mult)
        t2 = dinc(v, "sV")
        return t1, t2

    def emit_quant_v(v, src_f, aqi_out, rec_ap, colgu, par=0):
        v.reduce_max(sc("amax", par), src_f, AX.X, apply_absolute_value=True)
        v.drain()
        v.tensor_scalar(sc("amc", par), sc("amax", par), sc("eps", par), None,
                        ALU.max)
        v.drain()
        v.reciprocal(sc("rc", par), sc("amc", par))
        v.drain()
        v.tensor_scalar(sc("qs", par), sc("rc", par), 127.0, None, ALU.mult)
        v.drain()
        v.reciprocal(rec_ap, sc("qs", par))
        v.drain()
        if colgu:
            v.tensor_scalar(colgu[0], rec_ap, wcols[:, 0:1], None, ALU.mult)
            v.tensor_scalar(colgu[1], rec_ap, wcols[:, 1:2], None, ALU.mult)
        v.tensor_scalar(aqi_out, src_f, sc("qs", par), None, ALU.mult)
        return dinc(v, "sV")

    def emit_transposes(t, src_b, banks, n):
        last = None
        for c in range(n):
            bank = banks[(c // 4) % len(banks)]
            last = t.matmul(bank[:, (c % 4) * 128:(c % 4) * 128 + 128],
                            src_b[:, c * 128:(c + 1) * 128], ident_bf[:, :],
                            start=(c % 4 == 0), stop=(c % 4 == 3))
        return last

    def emit_wquant(wsrc_list, dst_sb_list, scr_list, wsi_list, nchunk, ccols):
        est = contextlib.ExitStack()
        wtmp = [est.enter_context(nc.sbuf_tensor(_nm("wqw"), [128, ccols], F32))
                for _ in range(2)]
        i32t = [est.enter_context(nc.sbuf_tensor(_nm("wqi"), [128, ccols], I32))
                for _ in range(2)]
        jobs = []
        for (ws, db, sd, wsi) in zip(wsrc_list, dst_sb_list, scr_list, wsi_list):
            for i in range(nchunk):
                jobs.append((ws, db, sd, wsi, i))
        spw = dma_pair(_nm("wql"))
        sps = dma_pair(_nm("wqs"), store=True)
        dt_ = []
        vbase = ctr["sV"]
        with nc.Block() as blk:
            @blk.sync
            def _(sy):
                wait_stores(sy, "sync")
                for j, (ws, db, sd, wsi, i) in enumerate(jobs):
                    if j >= 2:
                        wait(sy, "sync", "sV", vbase + 2 * (j - 1))
                    dt_.append(dma(sy, spw[j % 2], wtmp[j % 2][:],
                                   ws[i * 128:(i + 1) * 128, :]))

            @blk.scalar
            def _(s):
                for j, (ws, db, sd, wsi, i) in enumerate(jobs):
                    wait(s, "scalar", "sV", vbase + 2 * (j + 1))
                    dma(s, sps[j % 2], sd[i * 128:(i + 1) * 128, :],
                        db[:, i * ccols:(i + 1) * ccols])

            @blk.vector
            def _(v):
                for j, (ws, db, sd, wsi, i) in enumerate(jobs):
                    wait_t(v, "vector", dt_[j])
                    v.tensor_scalar(i32t[j % 2][:], wtmp[j % 2][:],
                                    wcols[:, 3 + wsi:4 + wsi], None, ALU.mult)
                    dinc(v, "sV")
                    v.tensor_scalar(db[:, i * ccols:(i + 1) * ccols],
                                    i32t[j % 2][:], -1.0, 1.0, ALU.max, ALU.min)
                    dinc(v, "sV")
        est.close()

    if "p1a" not in phases:
        return finish()
    p1 = contextlib.ExitStack()
    wg_sb = p1.enter_context(nc.sbuf_tensor("wg_sb", [128, DC * INNER], BF16))
    wu_sb = p1.enter_context(nc.sbuf_tensor("wu_sb", [128, DC * INNER], BF16))
    emit_wquant([wgt, wut], [wg_sb, wu_sb], [gt_scr, ut_scr], [0, 1], 8, INNER)

    xa = [p1.enter_context(nc.sbuf_tensor(f"xa{i}", [128, D], F32))
          for i in range(2)]
    lnf = [p1.enter_context(nc.sbuf_tensor(f"lnf{i}", [128, D], F32))
           for i in range(2)]
    aqi = p1.enter_context(nc.sbuf_tensor("aqi", [128, D], I32))
    aqb = [p1.enter_context(nc.sbuf_tensor(f"aqb{i}", [128, D], BF16))
           for i in range(3)]
    xsq1 = p1.enter_context(nc.sbuf_tensor("xsq1", [128, D], F32))
    aqT = [p1.enter_context(nc.sbuf_tensor(f"aqT{i}", [128, D], BF16))
           for i in range(3)]
    qcols = p1.enter_context(nc.sbuf_tensor("qcols", [128, 6], F32))

    def qc(kind, slot):
        base = 0 if kind == "g" else 3
        return qcols[:, base + slot:base + slot + 1]
    silu_sb = [p1.enter_context(nc.sbuf_tensor(f"silu{i}", [128, 512], F32))
               for i in range(3)]
    h_sb = p1.enter_context(nc.sbuf_tensor("h_sb", [128, INNER], F32))
    hqi = p1.enter_context(nc.sbuf_tensor("hqi", [128, 2048], I32))
    hqb = [p1.enter_context(nc.sbuf_tensor(f"hqb{i}", [128, INNER], BF16))
           for i in range(2)]

    sxin = dma_pair("xin")
    sshq = dma_pair("sthq", store=True)
    tok_x = [None] * n_t
    stok_hq = [None] * n_t

    A0, V0, T0 = ctr["sA"], ctr["sV"], ctr["sT"]
    ATm, ATq, ATh, ATe = {}, {}, {}, {}
    VHM, VVAR, VAQB, VH = {}, {}, {}, {}
    TT, GU = {}, {}
    _ca, _cv, _ct = [A0], [V0], [T0]

    def _tok(d, key, c):
        c[0] += 1
        d[key] = c[0]

    _tok(ATq, 0, _ca), _tok(ATq, 1, _ca), _tok(ATe, 0, _ca), _tok(ATe, 1, _ca)
    for _t in range(n_t):
        for _ib in range(NIB):
            _tok(ATm, (_t, _ib), _ca)
            if _ib == 2 and _t + 2 < n_t:
                _tok(ATq, _t + 2, _ca)
        _tok(ATh, (_t, 1), _ca)
        _tok(ATh, (_t, 2), _ca)
        if _t + 2 < n_t:
            _tok(ATe, _t + 2, _ca)
    for _t in (0, 1):
        _tok(VVAR, _t, _cv)
        _tok(VAQB, _t, _cv)
    for _t in range(n_t):
        for _ib in range(NIB):
            _tok(VHM, (_t, _ib), _cv)
            if _t + 2 < n_t:
                if _ib == 2:
                    _tok(VVAR, _t + 2, _cv)
                if _ib == 6:
                    _tok(VAQB, _t + 2, _cv)
        _tok(VH, (_t, 1), _cv)
        _tok(VH, (_t, 2), _cv)
    for _t in (0, 1):
        _tok(TT, _t, _ct)
    for _t in range(n_t):
        for _ib in range(NIB):
            _tok(GU, (_t, _ib), _ct)
        if _t + 2 < n_t:
            _tok(TT, _t + 2, _ct)
    GBANK = [(pbk[2], pbk[3]), (pbk[4], pbk[5]), (pbk[6], pbk[7])]

    with nc.Block() as blk:
        @blk.sync
        def _(sy):
            for t in range(n_t):
                if t >= 2:
                    wait(sy, "sync", "sV", VAQB[t - 2])
                tok_x[t] = dma(sy, sxin[t % 2], xa[t % 2][:],
                               x[t * 128:(t + 1) * 128, :])

        @blk.scalar
        def _(s):
            def sqrt_a(t):
                wait(s, "scalar", "sV", VVAR[t])
                s.activation(sc("srt", t % 2), sc("var", t % 2), AF.Sqrt,
                             bias=sc("eps", t % 2))
                tok = dinc(s, "sA")
                assert tok == ATq[t]

            def evac_a(t):
                wait(s, "scalar", "sT", TT[t])
                if t >= 3:
                    wait(s, "scalar", "sT", GU[(t - 3, 7)])
                s.copy(aqT[t % 3][:, 0:512], pbk[0][:, :])
                s.copy(aqT[t % 3][:, 512:1024], pbk[1][:, :])
                tok = dinc(s, "sA")
                assert tok == ATe[t]

            sqrt_a(0), sqrt_a(1), evac_a(0), evac_a(1)
            for t in range(n_t):
                par = t % 2
                wait(s, "scalar", "sV", VAQB[t])
                for ib in range(NIB):
                    gg = 8 * t + ib
                    wait(s, "scalar", "sT", GU[(t, ib)])
                    if gg >= 3:
                        tt, ii = divmod(gg - 3, 8)
                        wait(s, "scalar", "sV", VHM[(tt, ii)])
                    tok = inc(s.activation(silu_sb[gg % 3][:],
                                           GBANK[gg % 3][0][:, :], AF.Silu,
                                           scale=qc("g", t % 3)), "sA")
                    assert tok == ATm[(t, ib)]
                    if ib == 2 and t + 2 < n_t:
                        sqrt_a(t + 2)
                wait(s, "scalar", "sV", VH[(t, 1)])
                if t >= 2:
                    wait_t(s, "scalar", stok_hq[t - 2])
                tok = inc(s.activation(hqb[par][:, 0:2048], hqi[:],
                                       AF.Identity), "sA")
                assert tok == ATh[(t, 1)]
                wait(s, "scalar", "sV", VH[(t, 2)])
                s.activation(hqb[par][:, 2048:4096], hqi[:], AF.Identity)
                tok = dinc(s, "sA")
                assert tok == ATh[(t, 2)]
                stok_hq[t] = dma(s, sshq[par],
                                 hq_scr[t * 128:(t + 1) * 128, :], hqb[par][:])
                if t + 2 < n_t:
                    evac_a(t + 2)

        @blk.vector
        def _(v):
            def fchunk(t, j):
                par = t % 2
                sx = xa[par][:]
                if j == 0:
                    wait_t(v, "vector", tok_x[t])
                    v.reduce_sum(sc("sum", par), sx, AX.X)
                    v.drain()
                    v.tensor_scalar(sc("mean", par), sc("sum", par),
                                    1.0 / 1024.0, None, ALU.mult)
                    v.drain()
                elif j == 1:
                    v.tensor_tensor(xsq1[:], sx, sx, ALU.mult)
                    v.drain()
                elif j == 2:
                    v.reduce_sum(sc("ssq", par), xsq1[:], AX.X)
                    v.drain()
                    v.tensor_tensor(sc("m2", par), sc("mean", par),
                                    sc("mean", par), ALU.mult)
                    v.drain()
                    v.scalar_tensor_tensor(sc("var", par), sc("ssq", par),
                                           1.0 / 1024.0, sc("m2", par),
                                           ALU.mult, ALU.subtract)
                    tok = dinc(v, "sV")
                    assert tok == VVAR[t]
                elif j == 3:
                    wait(v, "vector", "sA", ATq[t])
                    v.reciprocal(sc("rstd", par), sc("srt", par))
                    v.drain()
                    v.tensor_scalar(sc("nmr", par), sc("mean", par),
                                    sc("rstd", par), -1.0, ALU.mult, ALU.mult)
                    v.drain()
                    v.tensor_scalar(lnf[par][:], sx, sc("rstd", par),
                                    sc("nmr", par), ALU.mult, ALU.add)
                    v.drain()
                elif j == 4:
                    v.reduce_max(sc("amax", par), lnf[par][:], AX.X,
                                 apply_absolute_value=True)
                    v.drain()
                    v.tensor_scalar(sc("amc", par), sc("amax", par),
                                    sc("eps", par), None, ALU.max)
                    v.drain()
                    v.reciprocal(sc("rc", par), sc("amc", par))
                    v.drain()
                elif j == 5:
                    v.tensor_scalar(sc("qs", par), sc("rc", par), 127.0, None,
                                    ALU.mult)
                    v.drain()
                    v.reciprocal(sc("aqrec", par), sc("qs", par))
                    v.drain()
                    v.tensor_scalar(qc("g", t % 3), sc("aqrec", par),
                                    wcols[:, 0:1], None, ALU.mult)
                    v.tensor_scalar(qc("u", t % 3), sc("aqrec", par),
                                    wcols[:, 1:2], None, ALU.mult)
                    v.drain()
                elif j == 6:
                    v.tensor_scalar(aqi[:], lnf[par][:], sc("qs", par), None,
                                    ALU.mult)
                    v.drain()
                    if t >= 3:
                        wait(v, "vector", "sT", TT[t - 1])
                    v.tensor_copy(aqb[t % 3][:], aqi[:])
                    tok = dinc(v, "sV")
                    assert tok == VAQB[t]

            for k in (0, 1):
                for j in range(7):
                    fchunk(k, j)
            for t in range(n_t):
                par = t % 2
                for ib in range(NIB):
                    gg = 8 * t + ib
                    wait(v, "vector", "sT", GU[(t, ib)])
                    wait(v, "vector", "sA", ATm[(t, ib)])
                    tok = inc(v.scalar_tensor_tensor(
                        h_sb[:, ib * 512:(ib + 1) * 512],
                        GBANK[gg % 3][1][:, :], qc("u", t % 3),
                        silu_sb[gg % 3][:], ALU.mult, ALU.mult), "sV")
                    assert tok == VHM[(t, ib)]
                    if t + 2 < n_t and ib < 7:
                        fchunk(t + 2, ib)
                v.drain()
                v.reduce_max(sc("hamax", par), h_sb[:], AX.X,
                             apply_absolute_value=True)
                v.drain()
                v.tensor_scalar(sc("hamc", par), sc("hamax", par),
                                sc("eps", par), None, ALU.max)
                v.drain()
                v.reciprocal(sc("hrc", par), sc("hamc", par))
                v.drain()
                v.tensor_scalar(sc("hqs", par), sc("hrc", par), 127.0, None,
                                ALU.mult)
                v.drain()
                v.reciprocal(hrecip_all[:, t:t + 1], sc("hqs", par))
                v.drain()
                if t >= 1:
                    wait(v, "vector", "sA", ATh[(t - 1, 2)])
                v.tensor_scalar(hqi[:], h_sb[:, 0:2048], sc("hqs", par), None,
                                ALU.mult)
                tok = dinc(v, "sV")
                assert tok == VH[(t, 1)]
                wait(v, "vector", "sA", ATh[(t, 1)])
                v.tensor_scalar(hqi[:], h_sb[:, 2048:4096], sc("hqs", par),
                                None, ALU.mult)
                tok = dinc(v, "sV")
                assert tok == VH[(t, 2)]

        @blk.tensor
        def _(te):
            def trans(t):
                wait(te, "tensor", "sV", VAQB[t])
                if t >= 1:
                    wait(te, "tensor", "sA", ATe[t - 1])
                tok = inc(emit_transposes(te, aqb[t % 3][:, :],
                                          [pbk[0], pbk[1]], 8), "sT")
                assert tok == TT[t]

            trans(0), trans(1)
            for t in range(n_t):
                wait(te, "tensor", "sA", ATe[t])
                for ib in range(NIB):
                    gg = 8 * t + ib
                    if gg >= 3:
                        tt, ii = divmod(gg - 3, 8)
                        wait(te, "tensor", "sV", VHM[(tt, ii)])
                    gb, ub = GBANK[gg % 3]
                    for c in range(DC):
                        te.matmul(gb[:, :],
                                  aqT[t % 3][:, c * 128:(c + 1) * 128],
                                  wg_sb[:, c * INNER + ib * 512:
                                        c * INNER + ib * 512 + 512],
                                  start=(c == 0), stop=(c == DC - 1))
                    last = None
                    for c in range(DC):
                        last = te.matmul(ub[:, :],
                                         aqT[t % 3][:, c * 128:(c + 1) * 128],
                                         wu_sb[:, c * INNER + ib * 512:
                                               c * INNER + ib * 512 + 512],
                                         start=(c == 0), stop=(c == DC - 1))
                    tok = inc(last, "sT")
                    assert tok == GU[(t, ib)]
                if t + 2 < n_t:
                    trans(t + 2)

    assert ctr["sA"] == _ca[0]
    assert ctr["sV"] == _cv[0]
    assert ctr["sT"] == _ct[0]
    p1.close()

    def run_down_phase(preamble, hq_src, res_src, out_dst, final):
        pd = contextlib.ExitStack()
        wd_sb = pd.enter_context(nc.sbuf_tensor(_nm("wd_sb"),
                                                [128, 32 * 1024], BF16))
        preamble(wd_sb)
        hqa = [pd.enter_context(nc.sbuf_tensor(_nm("hqa"), [128, INNER], BF16))
               for _ in range(2)]
        resa = [pd.enter_context(nc.sbuf_tensor(_nm("resa"), [128, D], F32))
                for _ in range(2)]
        hqT = [pd.enter_context(nc.sbuf_tensor(_nm("hqT"), [128, INNER], BF16))
               for _ in range(2)]
        y3sb = [pd.enter_context(nc.sbuf_tensor(_nm("y3sb"), [128, D], F32))
                for _ in range(2)]
        sdlh = dma_pair(_nm("dnlh"))
        sdlr = dma_pair(_nm("dnlr"))
        sds = dma_pair(_nm("dns"), store=True)
        tok_hq = [None] * n_t
        tok_res = [None] * n_t
        stok = [None] * n_t

        T0d, A0d, V0d = ctr["sT"], ctr["sA"], ctr["sV"]
        _pe_off = [1, 2, 3, 4, 6, 8, 10, 12]
        _dg_off = [5, 7, 9, 11, 13, 14, 15, 16]

        def TG(t, g):
            return T0d + 16 * t + _pe_off[g]

        def DG(t, dg):
            return T0d + 16 * t + _dg_off[dg]

        def EV(t, g):
            return A0d + 8 * t + 1 + g

        def Y3(t):
            return V0d + t + 1

        with nc.Block() as blk:
            @blk.sync
            def _(sy):
                wait_stores(sy, "sync")
                for t in range(n_t):
                    if t >= 2:
                        wait(sy, "sync", "sT", TG(t - 2, 7))
                        wait(sy, "sync", "sV", Y3(t - 2))
                    tok_hq[t] = dma(sy, sdlh[t % 2], hqa[t % 2][:],
                                    hq_src[t * 128:(t + 1) * 128, :])
                    tok_res[t] = dma(sy, sdlr[t % 2], resa[t % 2][:],
                                     res_src[t * 128:(t + 1) * 128, :])

            @blk.scalar
            def _(s):
                for t in range(n_t):
                    for g in range(8):
                        wait(s, "scalar", "sT", TG(t, g))
                        if t >= 2:
                            wait(s, "scalar", "sT", DG(t - 2, g))
                        s.copy(hqT[t % 2][:, g * 512:(g + 1) * 512],
                               pbk[g % 4][:, :])
                        tok = dinc(s, "sA")
                        assert tok == EV(t, g)
                    wait(s, "scalar", "sV", Y3(t))
                    stok[t] = dma(s, sds[t % 2],
                                  out_dst[t * 128:(t + 1) * 128, :],
                                  y3sb[t % 2][:])

            @blk.vector
            def _(v):
                for t in range(n_t):
                    dnb = (pbk[4], pbk[5]) if t % 2 == 0 else (pbk[6], pbk[7])
                    wait(v, "vector", "sT", DG(t, 7))
                    wait_t(v, "vector", tok_res[t])
                    if t >= 2:
                        wait_t(v, "vector", stok[t - 2])
                    v.tensor_scalar(sc("colt"), hrecip_all[:, t:t + 1],
                                    sdh_col[:, :], None, ALU.mult)
                    v.drain()
                    v.scalar_tensor_tensor(y3sb[t % 2][:, 0:512], dnb[0][:, :],
                                           sc("colt"), resa[t % 2][:, 0:512],
                                           ALU.mult, ALU.add)
                    v.scalar_tensor_tensor(y3sb[t % 2][:, 512:1024],
                                           dnb[1][:, :], sc("colt"),
                                           resa[t % 2][:, 512:1024],
                                           ALU.mult, ALU.add)
                    tok = inc(v.drain(), "sV")
                    assert tok == Y3(t)

            @blk.tensor
            def _(te):
                for t in range(n_t):
                    dnb = (pbk[4], pbk[5]) if t % 2 == 0 else (pbk[6], pbk[7])

                    def tgrp(g, t=t):
                        if t >= 1:
                            wait(te, "tensor", "sA", EV(t - 1, 4 + (g % 4)))
                        last = None
                        for k in range(4):
                            c = g * 4 + k
                            last = te.matmul(
                                pbk[g % 4][:, k * 128:(k + 1) * 128],
                                hqa[t % 2][:, c * 128:(c + 1) * 128],
                                ident_bf[:, :], start=(k == 0), stop=(k == 3))
                        tok = inc(last, "sT")
                        assert tok == TG(t, g)

                    def dgrp(dg, t=t, dnb=dnb):
                        wait(te, "tensor", "sA", EV(t, dg))
                        if dg == 0 and t >= 2:
                            wait(te, "tensor", "sV", Y3(t - 2))
                        last = None
                        for k in range(4):
                            ci = dg * 4 + k
                            te.matmul(dnb[0][:, :],
                                      hqT[t % 2][:, ci * 128:(ci + 1) * 128],
                                      wd_sb[:, ci * 1024:ci * 1024 + 512],
                                      start=(ci == 0), stop=(ci == 31))
                            last = te.matmul(
                                dnb[1][:, :],
                                hqT[t % 2][:, ci * 128:(ci + 1) * 128],
                                wd_sb[:, ci * 1024 + 512:ci * 1024 + 1024],
                                start=(ci == 0), stop=(ci == 31))
                        tok = inc(last, "sT")
                        assert tok == DG(t, dg)

                    wait_t(te, "tensor", tok_hq[t])
                    for g in range(4):
                        tgrp(g)
                    for g in range(4):
                        dgrp(g)
                        tgrp(4 + g)
                    for dg in range(4, 8):
                        dgrp(dg)

            assert ctr["sT"] == T0d + 16 * n_t
            assert ctr["sA"] == A0d + 8 * n_t
            assert ctr["sV"] == V0d + n_t

        pd.close()

    def p1b_pre(wd_sb):
        emit_wquant([wdt], [wd_sb], [dt_scr], [2], 32, 1024)

    if "p1b" not in phases:
        return finish()
    run_down_phase(p1b_pre, hq_scr, x, y1_scr, final=False)

    if "p2a" not in phases:
        return finish()
    p2 = contextlib.ExitStack()
    wq_sb = p2.enter_context(nc.sbuf_tensor("wq_sb", [128, DC * D], BF16))
    wk_sb = p2.enter_context(nc.sbuf_tensor("wk_sb", [128, DC * D], BF16))
    wv_sb = p2.enter_context(nc.sbuf_tensor("wv_sb", [128, DC * D], BF16))

    pre = contextlib.ExitStack()
    wtmp2 = [pre.enter_context(nc.sbuf_tensor(_nm("w2"), [128, D], F32))
             for _ in range(2)]
    jobs2 = [(w, dst, i) for (w, dst) in ((wq, wq_sb), (wk, wk_sb), (wv, wv_sb))
             for i in range(8)]
    sp2w = dma_pair("p2w")
    d2, v2base = [], ctr["sV"]
    with nc.Block() as blk:
        @blk.sync
        def _(sy):
            wait_stores(sy, "sync")
            for j, (w, dst, i) in enumerate(jobs2):
                if j >= 2:
                    wait(sy, "sync", "sV", v2base + j - 1)
                d2.append(dma(sy, sp2w[j % 2], wtmp2[j % 2][:],
                              w[i * 128:(i + 1) * 128, :]))

        @blk.vector
        def _(v):
            for j, (w, dst, i) in enumerate(jobs2):
                wait_t(v, "vector", d2[j])
                v.tensor_copy(dst[:, i * D:(i + 1) * D], wtmp2[j % 2][:])
                dinc(v, "sV")

    sp2o = dma_one("p2o")
    with nc.Block() as blk:
        @blk.sync
        def _(sy):
            dma(sy, sp2o, wtmp2[0][0:64, 0:M], omt)
            dma(sy, sp2o, wtmp2[0][64:128, 0:M], omt)

        @blk.vector
        def _(v):
            wait(v, "vector", sp2o, ctr[sp2o])
            v.tensor_copy(omdup_bf[:, :], wtmp2[0][:, 0:M])
            v.drain()
    pre.close()

    ya = [p2.enter_context(nc.sbuf_tensor(f"ya{i}", [128, D], F32))
          for i in range(2)]
    lnb_sb = [p2.enter_context(nc.sbuf_tensor(f"lnb{i}", [128, D], BF16))
              for i in range(2)]
    xsq2 = p2.enter_context(nc.sbuf_tensor("xsq2", [128, D], F32))
    lnT_sb = p2.enter_context(nc.sbuf_tensor("lnT_sb", [128, D], BF16))
    k_f = p2.enter_context(nc.sbuf_tensor("k_f", [128, D], F32))
    q_f = p2.enter_context(nc.sbuf_tensor("q_f", [128, D], F32))
    k_b = p2.enter_context(nc.sbuf_tensor("k_b", [128, D], BF16))
    k2_b = p2.enter_context(nc.sbuf_tensor("k2_b", [128, D], BF16))
    v_b = p2.enter_context(nc.sbuf_tensor("v_b", [128, D], BF16))
    pk_b = p2.enter_context(nc.sbuf_tensor("pk_b", [128, H * M], BF16))
    qb_sb = [p2.enter_context(nc.sbuf_tensor(f"qb{i}", [128, D], BF16))
             for i in range(2)]
    q2_sb = [p2.enter_context(nc.sbuf_tensor(f"q2{i}", [128, D], BF16))
             for i in range(2)]

    sy1in = dma_pair("y1in")
    sstq = dma_pair("stq", store=True)
    sstq2 = dma_pair("stq2", store=True)
    tok_y1 = [None] * (n_t + 1)
    stok_q = [None] * n_t
    stok_q2 = [None] * n_t

    A0p, V0p = ctr["sA"], ctr["sV"]

    def front2_stats(v, t):
        par = t % 2
        sx = ya[par][:]
        v.reduce_sum(sc("sum", par), sx, AX.X)
        v.drain()
        v.tensor_scalar(sc("mean", par), sc("sum", par), 1.0 / 1024.0, None,
                        ALU.mult)
        v.drain()
        v.tensor_tensor(xsq2[:], sx, sx, ALU.mult)
        v.drain()
        v.reduce_sum(sc("ssq", par), xsq2[:], AX.X)
        v.drain()
        v.tensor_tensor(sc("m2", par), sc("mean", par), sc("mean", par),
                        ALU.mult)
        v.drain()
        v.scalar_tensor_tensor(sc("var", par), sc("ssq", par), 1.0 / 1024.0,
                               sc("m2", par), ALU.mult, ALU.subtract)
        return dinc(v, "sV")

    def front2_ln(v, t, a_sq_tok):
        par = t % 2
        wait(v, "vector", "sA", a_sq_tok)
        v.reciprocal(sc("rstd", par), sc("srt", par))
        v.drain()
        v.tensor_scalar(sc("nmr", par), sc("mean", par), sc("rstd", par),
                        -1.0, ALU.mult, ALU.mult)
        v.drain()
        v.tensor_scalar(lnb_sb[par][:], ya[par][:], sc("rstd", par),
                        sc("nmr", par), ALU.mult, ALU.add)
        return dinc(v, "sV")

    def sqrt2_a(s, t, v_var_tok):
        wait(s, "scalar", "sV", v_var_tok)
        s.activation(sc("srt", t % 2), sc("var", t % 2), AF.Sqrt,
                     bias=sc("eps", t % 2))
        return dinc(s, "sA")

    with nc.Block() as blk:
        @blk.sync
        def _(sy):
            wait_stores(sy, "sync")
            tok_y1[0] = dma(sy, sy1in[0], ya[0][:], y1_scr[0:128, :])

        @blk.scalar
        def _(s):
            tok = sqrt2_a(s, 0, V0p + 1)
            assert tok == A0p + 1

        @blk.vector
        def _(v):
            wait_t(v, "vector", tok_y1[0])
            tok = front2_stats(v, 0)
            assert tok == V0p + 1
            tok = front2_ln(v, 0, A0p + 1)
            assert tok == V0p + 2

    for t in range(n_t):
        tb, ab, vb = ctr["sT"], ctr["sA"], ctr["sV"]
        with nc.Block() as blk:
            @blk.sync
            def _(sy, t=t):
                if t + 1 < n_t:
                    tok_y1[t + 1] = dma(sy, sy1in[(t + 1) % 2],
                                        ya[(t + 1) % 2][:],
                                        y1_scr[(t + 1) * 128:(t + 2) * 128, :])

            @blk.scalar
            def _(s, t=t, tb=tb, vb=vb):
                wait(s, "scalar", "sT", tb + 1)
                s.copy(lnT_sb[:, 0:512], pbk[5][:, :])
                s.copy(lnT_sb[:, 512:1024], pbk[6][:, :])
                dinc(s, "sA")
                wait(s, "scalar", "sT", tb + 2)
                s.copy(k_f[:, 0:512], pbk[5][:, :])
                s.copy(k_f[:, 512:1024], pbk[6][:, :])
                dinc(s, "sA")
                s.activation(k2_b[:, :], k_f[:, :], AF.Square)
                dinc(s, "sA")
                wait(s, "scalar", "sT", tb + 4)
                s.copy(q_f[:, 0:512], pbk[5][:, :])
                s.copy(q_f[:, 512:1024], pbk[6][:, :])
                dinc(s, "sA")
                if t >= 2:
                    wait_t(s, "scalar", stok_q2[t - 2])
                s.activation(q2_sb[t % 2][:, :], q_f[:, :], AF.Square)
                dinc(s, "sA")
                for hh in range(H):
                    wait(s, "scalar", "sT", tb + 5 + hh)
                    s.activation(
                        pk_b[:, hh * M:(hh + 1) * M],
                        pbk[5 + (hh % 2)][:, 0:M], AF.Exp)
                    dinc(s, "sA")
                if t + 1 < n_t:
                    tok = sqrt2_a(s, t + 1, vb + 4)
                    assert tok == ab + 22
                wait(s, "scalar", "sV", vb + 3)
                stok_q[t] = dma(s, sstq[t % 2],
                                qb_scr[t * 128:(t + 1) * 128, :], qb_sb[t % 2][:])
                stok_q2[t] = dma(s, sstq2[t % 2],
                                 q2_scr[t * 128:(t + 1) * 128, :],
                                 q2_sb[t % 2][:])

            @blk.vector
            def _(v, t=t, tb=tb, ab=ab, vb=vb):
                wait(v, "vector", "sA", ab + 2)
                v.tensor_copy(k_b[:, :], k_f[:, :])
                dinc(v, "sV")
                wait(v, "vector", "sT", tb + 3)
                v.tensor_copy(v_b[:, 0:512], pbk[5][:, :])
                v.tensor_copy(v_b[:, 512:1024], pbk[6][:, :])
                dinc(v, "sV")
                wait(v, "vector", "sA", ab + 4)
                if t >= 2:
                    wait_t(v, "vector", stok_q[t - 2])
                v.tensor_copy(qb_sb[t % 2][:, :], q_f[:, :])
                inc(v.drain(), "sV")
                if t + 1 < n_t:
                    wait_t(v, "vector", tok_y1[t + 1])
                    tok = front2_stats(v, t + 1)
                    assert tok == vb + 4
                    tok = front2_ln(v, t + 1, ab + 22)
                    assert tok == vb + 5

            @blk.tensor
            def _(te, t=t, ab=ab, vb=vb):
                wait(te, "tensor", "sV", V0p + 2 + 5 * t)
                inc(emit_transposes(te, lnb_sb[t % 2][:, :],
                                    [pbk[5], pbk[6]], 8), "sT")
                wait(te, "tensor", "sA", ab + 1)
                last = None
                for co in range(8):
                    bank = pbk[5] if co < 4 else pbk[6]
                    sl = bank[:, (co % 4) * 128:(co % 4) * 128 + 128]
                    for cd in range(DC):
                        last = te.matmul(
                            sl, wk_sb[:, cd * D + co * 128:cd * D + co * 128 + 128],
                            lnT_sb[:, cd * 128:(cd + 1) * 128],
                            start=(co % 4 == 0 and cd == 0),
                            stop=(co % 4 == 3 and cd == DC - 1))
                inc(last, "sT")
                wait(te, "tensor", "sA", ab + 2)
                last = None
                for half in range(2):
                    bank = pbk[5] if half == 0 else pbk[6]
                    for cd in range(DC):
                        last = te.matmul(
                            bank[:, :], lnT_sb[:, cd * 128:(cd + 1) * 128],
                            wv_sb[:, cd * D + half * 512:cd * D + half * 512 + 512],
                            start=(cd == 0), stop=(cd == DC - 1))
                inc(last, "sT")
                wait(te, "tensor", "sV", vb + 2)
                last = None
                for co in range(8):
                    bank = pbk[5] if co < 4 else pbk[6]
                    sl = bank[:, (co % 4) * 128:(co % 4) * 128 + 128]
                    for cd in range(DC):
                        last = te.matmul(
                            sl, wq_sb[:, cd * D + co * 128:cd * D + co * 128 + 128],
                            lnT_sb[:, cd * 128:(cd + 1) * 128],
                            start=(co % 4 == 0 and cd == 0),
                            stop=(co % 4 == 3 and cd == DC - 1))
                inc(last, "sT")
                wait(te, "tensor", "sV", vb + 1)
                wait(te, "tensor", "sA", ab + 4)
                for hh in range(H):
                    if hh >= 2:
                        wait(te, "tensor", "sA", ab + 6 + (hh - 2))
                    par = hh % 2
                    sl = pbk[5 + par][:, 0:M]
                    ksl = k_b[par * 64:par * 64 + 64,
                              (hh // 2) * 128:(hh // 2) * 128 + 128]
                    k2sl = k2_b[par * 64:par * 64 + 64,
                                (hh // 2) * 128:(hh // 2) * 128 + 128]
                    te.matmul(sl, ksl, omdup_bf[par * 64:par * 64 + 64, :],
                              start=True, stop=False)
                    inc(te.matmul(sl, k2sl, neghalf_bf[par * 64:par * 64 + 64, :],
                                  start=False, stop=True), "sT")
                last = None
                for hh in range(H):
                    wait(te, "tensor", "sA", ab + 6 + hh)
                    hl = hh - 8 if hh >= 8 else hh
                    for mc in range(2):
                        kvb = pbk[mc * 2 + (1 if hh >= 8 else 0)]
                        lhs = pk_b[:, hh * M + mc * 128:hh * M + mc * 128 + 128]
                        te.matmul(kvb[:, hl * 64:hl * 64 + 64], lhs,
                                  v_b[:, hh * 64:hh * 64 + 64],
                                  start=(t == 0 and hl == 0),
                                  stop=(t == n_t - 1 and hl == 7))
                        last = te.matmul(pbk[4][:, mc * 16 + hh:mc * 16 + hh + 1],
                                         lhs, onesbf_col[:, :],
                                         start=(t == 0 and hh == 0 and mc == 0),
                                         stop=(t == n_t - 1 and hh == H - 1
                                               and mc == 1))
                inc(last, "sT")

        assert ctr["sT"] == tb + 21, (ctr["sT"], tb)
        assert ctr["sA"] == ab + 5 + H + (1 if t + 1 < n_t else 0)
        assert ctr["sV"] == vb + 3 + (2 if t + 1 < n_t else 0)

    with nc.Block() as blk:
        @blk.vector
        def _(v):
            for mc in range(2):
                for hh in range(H):
                    kvb = pbk[mc * 2 + (1 if hh >= 8 else 0)]
                    hl = hh - 8 if hh >= 8 else hh
                    base = mc * (H * 65) + hh * 65
                    v.tensor_copy(kvaug_bf[:, base:base + 64],
                                  kvb[:, hl * 64:hl * 64 + 64])
                    v.tensor_copy(kvaug_bf[:, base + 64:base + 65],
                                  pbk[4][:, mc * 16 + hh:mc * 16 + hh + 1])
            v.drain()
    p2.close()

    if "p3a" not in phases:
        return finish()
    p3 = contextlib.ExitStack()
    wg3_sb = p3.enter_context(nc.sbuf_tensor("wg3_sb", [128, DC * INNER], BF16))
    wu3_sb = p3.enter_context(nc.sbuf_tensor("wu3_sb", [128, DC * INNER], BF16))
    stern = dma_one("tern")
    with nc.Block() as blk:
        @blk.sync
        def _(sy):
            wait_stores(sy, "sync")
            for i in range(8):
                dma(sy, stern, wg3_sb[:, i * INNER:(i + 1) * INNER],
                    gt_scr[i * 128:(i + 1) * 128, :])
            for i in range(8):
                dma(sy, stern, wu3_sb[:, i * INNER:(i + 1) * INNER],
                    ut_scr[i * 128:(i + 1) * 128, :])

    y1a = [p3.enter_context(nc.sbuf_tensor(f"y1a{i}", [128, D], F32))
           for i in range(2)]
    qba = [p3.enter_context(nc.sbuf_tensor(f"qba{i}", [128, D], BF16))
           for i in range(2)]
    q2a = [p3.enter_context(nc.sbuf_tensor(f"q2a{i}", [128, D], BF16))
           for i in range(2)]
    pq_b = p3.enter_context(nc.sbuf_tensor("pq_b", [128, 32 * 128], BF16))
    y2_sb = [p3.enter_context(nc.sbuf_tensor(f"y2sb{i}", [128, D], F32))
             for i in range(2)]
    lnf3 = p3.enter_context(nc.sbuf_tensor("lnf3", [128, D], F32))
    aqi3 = p3.enter_context(nc.sbuf_tensor("aqi3", [128, D], I32))
    aqb3 = p3.enter_context(nc.sbuf_tensor("aqb3", [128, D], BF16))
    aqT3 = p3.enter_context(nc.sbuf_tensor("aqT3", [128, D], BF16))
    silu3 = [p3.enter_context(nc.sbuf_tensor(f"silu3{i}", [128, 512], F32))
             for i in range(2)]
    h3_sb = p3.enter_context(nc.sbuf_tensor("h3_sb", [128, INNER], F32))
    hqi3 = p3.enter_context(nc.sbuf_tensor("hqi3", [128, 2048], I32))
    hqb3 = p3.enter_context(nc.sbuf_tensor("hqb3", [128, INNER], BF16))
    xsq3 = p3.enter_context(nc.sbuf_tensor("xsq3", [128, D], F32))
    ham83 = p3.enter_context(nc.sbuf_tensor("ham83", [128, 8], F32))

    sp3in = dma_pair("p3in")
    ssty2 = dma_pair("sty2", store=True)
    ssth2 = dma_pair("sth2", store=True)
    tok3 = [None] * (n_t + 1)
    stok_h2 = [None] * n_t
    stok_y2 = [None] * n_t

    def p3_loads(sy, tt):
        dma(sy, sp3in[tt % 2], y1a[tt % 2][:],
            y1_scr[tt * 128:(tt + 1) * 128, :])
        dma(sy, sp3in[tt % 2], qba[tt % 2][:],
            qb_scr[tt * 128:(tt + 1) * 128, :])
        tok3[tt] = dma(sy, sp3in[tt % 2], q2a[tt % 2][:],
                       q2_scr[tt * 128:(tt + 1) * 128, :])

    with nc.Block() as blk:
        @blk.sync
        def _(sy):
            p3_loads(sy, 0)

    ATT_BANK = [4] * 7 + [5] * 7 + [6] * 2
    ATT_SLOT = list(range(7)) + list(range(7)) + list(range(2))
    GBANK3 = [(pbk[2], pbk[3]), (pbk[7], pbk[6])]

    for t in range(n_t + 1):
        att = t < n_t
        ffn = t >= 1
        u = t - 1
        na, nf = (1 if att else 0), (1 if ffn else 0)
        tb, ab, vb = ctr["sT"], ctr["sA"], ctr["sV"]
        T_ATT = tb + 5
        T_TR = tb + 5 * na + 1

        def T_GU(ib, tb=tb, na=na):
            return tb + 5 * na + 2 + ib

        A_SRT = ab + 4 * na + 1
        A_EVAC = ab + 4 * na + 2

        def A_SIL(ib, ab=ab, na=na):
            return ab + 4 * na + 3 + ib

        A_C1 = ab + 4 * na + 11
        A_C2 = ab + 4 * na + 12
        V_VAR = vb + 1
        V_AQB = vb + 2
        V_Y2 = vb + 2 * nf + 1

        def V_HM(ib, vb=vb, nf=nf, na=na):
            return vb + 2 * nf + na + 1 + ib

        V_H1 = vb + 2 * nf + na + 9
        V_H2 = vb + 2 * nf + na + 10

        with nc.Block() as blk:
            @blk.sync
            def _(sy, t=t):
                if t + 1 < n_t:
                    p3_loads(sy, t + 1)

            @blk.scalar
            def _(s, t=t, u=u, att=att, ffn=ffn, tb=tb, ab=ab):
                if att:
                    for g in range(4):
                        wait(s, "scalar", "sT", tb + 1 + g)
                        bA = pbk[0] if g % 2 == 0 else pbk[2]
                        bB = pbk[1] if g % 2 == 0 else pbk[3]
                        s.activation(pq_b[:, g * 1024:g * 1024 + 512], bA[:, :],
                                     AF.Exp)
                        s.activation(pq_b[:, g * 1024 + 512:(g + 1) * 1024],
                                     bB[:, :], AF.Exp)
                        dinc(s, "sA")
                if ffn:
                    wait(s, "scalar", "sV", V_VAR)
                    s.activation(sc("srt"), sc("var"), AF.Sqrt, bias=sc("eps"))
                    tok = dinc(s, "sA")
                    assert tok == A_SRT
                    wait(s, "scalar", "sT", T_TR)
                    s.copy(aqT3[:, 0:512], pbk[0][:, :])
                    s.copy(aqT3[:, 512:1024], pbk[1][:, :])
                    tok = dinc(s, "sA")
                    assert tok == A_EVAC
                    wait(s, "scalar", "sV", V_AQB)
                    for ib in range(NIB):
                        wait(s, "scalar", "sT", T_GU(ib))
                        if ib >= 2:
                            wait(s, "scalar", "sV", V_HM(ib - 2))
                        tok = inc(s.activation(silu3[ib % 2][:],
                                               GBANK3[ib % 2][0][:, :], AF.Silu,
                                               scale=sc("colg")), "sA")
                        assert tok == A_SIL(ib)
                    wait(s, "scalar", "sV", V_H1)
                    if u >= 1:
                        wait_t(s, "scalar", stok_h2[u - 1])
                    tok = inc(s.activation(hqb3[:, 0:2048], hqi3[:],
                                           AF.Identity), "sA")
                    assert tok == A_C1
                    wait(s, "scalar", "sV", V_H2)
                    s.activation(hqb3[:, 2048:4096], hqi3[:], AF.Identity)
                    tok = dinc(s, "sA")
                    assert tok == A_C2
                    stok_h2[u] = dma(s, ssth2[u % 2],
                                     hq_scr[u * 128:(u + 1) * 128, :], hqb3[:])
                if att:
                    wait(s, "scalar", "sV", V_Y2)
                    stok_y2[t] = dma(s, ssty2[t % 2],
                                     y2_scr[t * 128:(t + 1) * 128, :],
                                     y2_sb[t % 2][:])

            @blk.vector
            def _(v, t=t, u=u, att=att, ffn=ffn):
                if ffn:
                    src = y2_sb[u % 2][:]
                    v.reduce_sum(sc("sum"), src, AX.X)
                    v.drain()
                    v.tensor_scalar(sc("mean"), sc("sum"), 1.0 / 1024.0, None,
                                    ALU.mult)
                    v.drain()
                    v.tensor_tensor(xsq3[:], src, src, ALU.mult)
                    v.drain()
                    v.reduce_sum(sc("ssq"), xsq3[:], AX.X)
                    v.drain()
                    v.tensor_tensor(sc("m2"), sc("mean"), sc("mean"), ALU.mult)
                    v.drain()
                    v.scalar_tensor_tensor(sc("var"), sc("ssq"), 1.0 / 1024.0,
                                           sc("m2"), ALU.mult, ALU.subtract)
                    tok = dinc(v, "sV")
                    assert tok == V_VAR
                    wait(v, "vector", "sA", A_SRT)
                    v.reciprocal(sc("rstd"), sc("srt"))
                    v.drain()
                    v.tensor_scalar(sc("nmr"), sc("mean"), sc("rstd"), -1.0,
                                    ALU.mult, ALU.mult)
                    v.drain()
                    v.tensor_scalar(lnf3[:], src, sc("rstd"), sc("nmr"),
                                    ALU.mult, ALU.add)
                    v.drain()
                    v.reduce_max(sc("amax"), lnf3[:], AX.X,
                                 apply_absolute_value=True)
                    v.drain()
                    v.tensor_scalar(sc("amc"), sc("amax"), sc("eps"), None,
                                    ALU.max)
                    v.drain()
                    v.reciprocal(sc("rc"), sc("amc"))
                    v.drain()
                    v.tensor_scalar(sc("qs"), sc("rc"), 127.0, None, ALU.mult)
                    v.drain()
                    v.reciprocal(sc("aqrec"), sc("qs"))
                    v.drain()
                    v.tensor_scalar(sc("colg"), sc("aqrec"), wcols[:, 0:1],
                                    None, ALU.mult)
                    v.tensor_scalar(sc("colu"), sc("aqrec"), wcols[:, 1:2],
                                    None, ALU.mult)
                    v.drain()
                    v.tensor_scalar(aqi3[:], lnf3[:], sc("qs"), None, ALU.mult)
                    v.drain()
                    v.tensor_copy(aqb3[:], aqi3[:])
                    tok = dinc(v, "sV")
                    assert tok == V_AQB
                if att:
                    wait(v, "vector", "sT", T_ATT)
                    wait_t(v, "vector", tok3[t])
                    if t >= 2:
                        wait_t(v, "vector", stok_y2[t - 2])
                    for hh in range(H):
                        v.tensor_copy(zcol[:, hh:hh + 1],
                                      pbk[ATT_BANK[hh]][:, ATT_SLOT[hh] * 65 + 64:
                                                        ATT_SLOT[hh] * 65 + 65])
                    v.drain()
                    v.tensor_scalar(zcol[:, :], zcol[:, :], 2.56e-4, None,
                                    ALU.add)
                    v.drain()
                    v.reciprocal(zcol[:, :], zcol[:, :])
                    v.drain()
                    for hh in range(H):
                        v.scalar_tensor_tensor(
                            y2_sb[t % 2][:, hh * 64:hh * 64 + 64],
                            pbk[ATT_BANK[hh]][:, ATT_SLOT[hh] * 65:
                                              ATT_SLOT[hh] * 65 + 64],
                            zcol[:, hh:hh + 1],
                            y1a[t % 2][:, hh * 64:hh * 64 + 64],
                            ALU.mult, ALU.add)
                    tok = inc(v.drain(), "sV")
                    assert tok == V_Y2
                if ffn:
                    for ib in range(NIB):
                        wait(v, "vector", "sT", T_GU(ib))
                        wait(v, "vector", "sA", A_SIL(ib))
                        tok = inc(v.scalar_tensor_tensor(
                            h3_sb[:, ib * 512:(ib + 1) * 512],
                            GBANK3[ib % 2][1][:, :], sc("colu"),
                            silu3[ib % 2][:], ALU.mult, ALU.mult), "sV")
                        assert tok == V_HM(ib)
                        v.drain()
                        v.reduce_max(ham83[:, ib:ib + 1],
                                     h3_sb[:, ib * 512:(ib + 1) * 512], AX.X,
                                     apply_absolute_value=True)
                    v.drain()
                    v.reduce_max(sc("hamax"), ham83[:, 0:8], AX.X,
                                 apply_absolute_value=True)
                    v.drain()
                    v.tensor_scalar(sc("amc"), sc("hamax"), sc("eps"), None,
                                    ALU.max)
                    v.drain()
                    v.reciprocal(sc("rc"), sc("amc"))
                    v.drain()
                    v.tensor_scalar(sc("hqs"), sc("rc"), 127.0, None, ALU.mult)
                    v.drain()
                    v.reciprocal(hrecip_all[:, u:u + 1], sc("hqs"))
                    v.drain()
                    v.tensor_scalar(hqi3[:], h3_sb[:, 0:2048], sc("hqs"), None,
                                    ALU.mult)
                    tok = dinc(v, "sV")
                    assert tok == V_H1
                    wait(v, "vector", "sA", A_C1)
                    v.tensor_scalar(hqi3[:], h3_sb[:, 2048:4096], sc("hqs"),
                                    None, ALU.mult)
                    tok = dinc(v, "sV")
                    assert tok == V_H2

            @blk.tensor
            def _(te, t=t, u=u, att=att, ffn=ffn, tb=tb, ab=ab):
                if att:
                    wait_t(te, "tensor", tok3[t])
                    if t == 0:
                        wait(te, "tensor", stern, ctr[stern])
                    for g in range(4):
                        mc, par = g // 2, g % 2
                        bA = pbk[0] if g % 2 == 0 else pbk[2]
                        bB = pbk[1] if g % 2 == 0 else pbk[3]
                        if g >= 2:
                            wait(te, "tensor", "sA", ab + 1 + (g - 2))
                        oms = omdup_bf[par * 64:par * 64 + 64,
                                       mc * 128:mc * 128 + 128]
                        ngs = neghalf_bf[par * 64:par * 64 + 64, 0:128]
                        last = None
                        for c in range(8):
                            bank = bA if c < 4 else bB
                            sl = bank[:, (c % 4) * 128:(c % 4) * 128 + 128]
                            qsl = qba[t % 2][par * 64:par * 64 + 64,
                                             c * 128:(c + 1) * 128]
                            q2sl = q2a[t % 2][par * 64:par * 64 + 64,
                                              c * 128:(c + 1) * 128]
                            te.matmul(sl, oms, qsl, start=(c % 4 == 0),
                                      stop=False)
                            last = te.matmul(sl, ngs, q2sl, start=False,
                                             stop=(c % 4 == 3))
                        inc(last, "sT")
                    wait(te, "tensor", "sA", ab + 4)
                    last = None
                    bank_started = set()
                    for hh in range(H):
                        bk, slt = ATT_BANK[hh], ATT_SLOT[hh]
                        for mc in range(2):
                            g = mc * 2 + (hh % 2)
                            j = g * 8 + hh // 2
                            first = bk not in bank_started
                            bank_started.add(bk)
                            last = te.matmul(
                                pbk[bk][:, slt * 65:slt * 65 + 65],
                                pq_b[:, j * 128:(j + 1) * 128],
                                kvaug_bf[:, mc * (H * 65) + hh * 65:
                                         mc * (H * 65) + (hh + 1) * 65],
                                start=first,
                                stop=(mc == 1 and hh in (6, 13, 15)))
                    tok = inc(last, "sT")
                    assert tok == T_ATT
                if ffn:
                    wait(te, "tensor", "sV", V_AQB)
                    tok = inc(emit_transposes(te, aqb3[:, :], [pbk[0], pbk[1]],
                                              8), "sT")
                    assert tok == T_TR
                    wait(te, "tensor", "sA", A_EVAC)
                    for ib in range(NIB):
                        if ib == 1 and att:
                            wait(te, "tensor", "sV", V_Y2)
                        if ib >= 2:
                            wait(te, "tensor", "sV", V_HM(ib - 2))
                        gb, ub = GBANK3[ib % 2]
                        for c in range(DC):
                            te.matmul(gb[:, :], aqT3[:, c * 128:(c + 1) * 128],
                                      wg3_sb[:, c * INNER + ib * 512:
                                             c * INNER + ib * 512 + 512],
                                      start=(c == 0), stop=(c == DC - 1))
                        last = None
                        for c in range(DC):
                            last = te.matmul(ub[:, :],
                                             aqT3[:, c * 128:(c + 1) * 128],
                                             wu3_sb[:, c * INNER + ib * 512:
                                                    c * INNER + ib * 512 + 512],
                                             start=(c == 0), stop=(c == DC - 1))
                        tok = inc(last, "sT")
                        assert tok == T_GU(ib)

        assert ctr["sT"] == tb + 5 * na + 9 * nf, (ctr["sT"], tb)
        assert ctr["sA"] == ab + 4 * na + 12 * nf
        assert ctr["sV"] == vb + na + 12 * nf
    p3.close()

    if "p3b" not in phases:
        return finish()
    swdld = dma_one("wdld")

    def p3b_pre(wd_sb):
        with nc.Block() as blk:
            @blk.sync
            def _(sy):
                wait_stores(sy, "sync")
                for ci in range(32):
                    dma(sy, swdld, wd_sb[:, ci * 1024:(ci + 1) * 1024],
                        dt_scr[ci * 128:(ci + 1) * 128, :])
                wait(sy, "sync", swdld, ctr[swdld])

    run_down_phase(p3b_pre, hq_scr, y2_scr, y, final=True)

    return finish()


_NC_CACHE = {}


def _get_program():
    if "nc" not in _NC_CACHE:
        _NC_CACHE["nc"] = build_program()
    return _NC_CACHE["nc"]


def kernel(L0_attn_x, L0_attn_Wq, L0_attn_Wk, L0_attn_Wv, L0_attn_omega,
           L0_ffn_w_gate, L0_ffn_w_up, L0_ffn_w_down):
    xs = np.ascontiguousarray(np.asarray(L0_attn_x, dtype=np.float32))
    wq_ = np.ascontiguousarray(np.asarray(L0_attn_Wq, dtype=np.float32))
    wk_ = np.ascontiguousarray(np.asarray(L0_attn_Wk, dtype=np.float32))
    wv_ = np.ascontiguousarray(np.asarray(L0_attn_Wv, dtype=np.float32))
    omt_ = np.ascontiguousarray(np.asarray(L0_attn_omega, dtype=np.float32).T)
    wgt_ = np.ascontiguousarray(np.asarray(L0_ffn_w_gate, dtype=np.float32).T)
    wut_ = np.ascontiguousarray(np.asarray(L0_ffn_w_up, dtype=np.float32).T)
    wdt_ = np.ascontiguousarray(np.asarray(L0_ffn_w_down, dtype=np.float32).T)

    nc = _get_program()
    in_maps = [{"x": np.ascontiguousarray(xs[c]), "wq": wq_, "wk": wk_,
                "wv": wv_, "omt": omt_, "wgt": wgt_, "wut": wut_, "wdt": wdt_}
               for c in range(8)]
    res = run_bass_kernel_spmd(nc, in_maps, core_ids=list(range(8)))
    return np.stack([np.asarray(res.results[c]["y"], dtype=np.float32)
                     for c in range(8)], axis=0)


if __name__ == "__main__":
    print("building...")
    nc = build_program()
    print("instructions:", nc.n_instructions())
